# version 01
# baseline (speedup 1.0000x reference)
import math
import numpy as np
import ml_dtypes
import concourse.bass as bass
import concourse.mybir as mybir
from concourse.bass_utils import run_bass_kernel_spmd
from contextlib import ExitStack

F32 = mybir.dt.float32
BF16 = mybir.dt.bfloat16
I32 = mybir.dt.int32
U32 = mybir.dt.uint32
ACT = mybir.ActivationFunctionType
ALU = mybir.AluOpType
BF = ml_dtypes.bfloat16

NCORES = 8


def full_cfg():
    return dict(
        B=2, S=4096, D=4096, NH=32, HD=128, NKV=8,
        INTER=11008, GROUP=128,
        NBLK=128, BLOCK=256, TOPK=16,
    )


def mini_cfg():
    return dict(
        B=2, S=512, D=1024, NH=8, HD=128, NKV=4,
        INTER=2048, GROUP=128,
        NBLK=16, BLOCK=128, TOPK=4,
    )


def derived(cfg):
    c = dict(cfg)
    c["T"] = cfg["B"] * cfg["S"] // NCORES
    c["NTT"] = c["T"] // 128
    c["PANEL"] = min(512, c["T"])
    c["NPAN"] = c["T"] // c["PANEL"]
    c["TPP"] = c["PANEL"] // 128
    c["KC"] = cfg["D"] // 128
    c["KCS"] = c["KC"] // NCORES
    c["NT_I"] = cfg["INTER"] // 128
    c["KVG"] = cfg["NH"] // cfg["NKV"]
    c["R_PER_KVH"] = NCORES // cfg["NKV"]
    c["CACHE_T"] = cfg["TOPK"] * cfg["BLOCK"]
    c["NEW_T"] = cfg["S"]
    c["KV_B"] = c["CACHE_T"] + c["NEW_T"]
    c["DP"] = cfg["D"] // 512
    c["QF"] = cfg["NH"] * cfg["HD"]
    c["KVF"] = cfg["NKV"] * cfg["HD"]
    c["WG_KC"] = c["KCS"]
    c["DQ_COLS"] = cfg["D"] // NCORES
    c["GRP_I"] = cfg["INTER"] // cfg["GROUP"]
    c["GRP_D"] = cfg["D"] // cfg["GROUP"]
    assert c["T"] % 128 == 0 and c["KCS"] >= 1 and cfg["D"] % 512 == 0
    assert c["CACHE_T"] % (c["R_PER_KVH"] * 128) == 0
    return c



def ftmaj(w):
    K, N = w.shape
    return np.ascontiguousarray(
        w.reshape(K // 128, 128, N // 128, 128).transpose(2, 1, 0, 3).reshape(N // 128 * 128, K))


def stage_inputs(cfg, I):
    c = derived(cfg)
    D, NH, HD, NKV = cfg["D"], cfg["NH"], cfg["HD"], cfg["NKV"]
    T = c["T"]
    x = np.asarray(I["x"]).reshape(-1, D)
    qkvw = np.asarray(I["qkvw"])
    qkvb = np.asarray(I["qkvb"]).reshape(3 * NH, HD).T
    ow = np.asarray(I["ow"])
    ob = np.asarray(I["ob"]).reshape(D // 128, 128).T
    wqT = np.ascontiguousarray(np.asarray(I["wq"]).T)
    wkT = np.ascontiguousarray(np.asarray(I["wk"]).T)
    wvT = np.ascontiguousarray(np.asarray(I["wv"]).T)
    woT = np.ascontiguousarray(np.asarray(I["wo"]).T)
    kcache = np.asarray(I["k_cache"])
    vcache = np.asarray(I["v_cache"])
    blkidx = np.asarray(I["block_idx"]).reshape(1, -1)
    fcos = np.asarray(I["fcos"]).reshape(1, HD // 2)
    fsin = np.asarray(I["fsin"]).reshape(1, HD // 2)

    gq, gsc, gqz = np.asarray(I["gq"]), np.asarray(I["gsc"]), np.asarray(I["gqz"])
    uq, usc, uqz = np.asarray(I["uq"]), np.asarray(I["usc"]), np.asarray(I["uqz"])
    dq, dsc, dqz = np.asarray(I["dq"]), np.asarray(I["dsc"]), np.asarray(I["dqz"])

    j = (np.arange(128) % 8)
    mask_nib = (np.int64(15) * (np.int64(1) << (4 * j))).astype(np.int64).astype(np.uint32).view(np.int32)[:, None]
    pow16 = (16.0 ** (-j.astype(np.float64))).astype(np.float32)[:, None]
    identf = np.eye(128, dtype=np.float32)
    identb = np.eye(128, dtype=np.float32).astype(BF)
    mup = np.zeros((128, 128), np.float32)
    mdn = np.zeros((128, 128), np.float32)
    ii = np.arange(64)
    mup[2 * ii, 2 * ii + 1] = 1.0
    mdn[2 * ii + 1, 2 * ii] = 1.0

    KR = D // NCORES
    QR = c["QF"] // NCORES
    in_maps = []
    for r in range(NCORES):
        kvh = r // c["R_PER_KVH"]
        tpart = r % c["R_PER_KVH"]
        ct = c["CACHE_T"] // c["R_PER_KVH"]
        m = dict(
            x=np.ascontiguousarray(x[r * T:(r + 1) * T]),
            qkvw=np.ascontiguousarray(ftmaj(qkvw[r * KR:(r + 1) * KR])),
            qkvb=np.ascontiguousarray(qkvb.astype(np.float32)),
            ow=np.ascontiguousarray(ow[r * KR:(r + 1) * KR]),
            ob=np.ascontiguousarray(ob.astype(np.float32)),
            ob_row=np.ascontiguousarray(np.asarray(I["ob"]).reshape(1, D).astype(np.float32)),
            wqT=np.ascontiguousarray(ftmaj(wqT[r * KR:(r + 1) * KR])),
            wkT=np.ascontiguousarray(ftmaj(wkT[r * KR:(r + 1) * KR])),
            wvT=np.ascontiguousarray(ftmaj(wvT[r * KR:(r + 1) * KR])),
            woT=np.ascontiguousarray(woT[r * QR:(r + 1) * QR]),
            kcache=np.ascontiguousarray(kcache[:, :, kvh, :].reshape(-1, HD)),
            vcache=np.ascontiguousarray(vcache[:, :, kvh, :].reshape(-1, HD)),
            blkidx=np.ascontiguousarray(
                blkidx[:, (tpart * ct) // cfg["BLOCK"]:((tpart + 1) * ct) // cfg["BLOCK"]].astype(np.int32)),
            csin=np.ascontiguousarray(np.stack([np.repeat(fcos.reshape(-1), 2),
                                                np.repeat(fsin.reshape(-1), 2)], axis=1).astype(np.float32)),
            gq=np.ascontiguousarray(gq[r * (KR // 8):(r + 1) * (KR // 8)]),
            gsc=np.ascontiguousarray(gsc[np.asarray(I["g_gidx"]).reshape(-1)[0::128][r * c["KCS"]:(r + 1) * c["KCS"]]]),
            gqz=np.ascontiguousarray(gqz[np.asarray(I["g_gidx"]).reshape(-1)[0::128][r * c["KCS"]:(r + 1) * c["KCS"]]]),
            uq=np.ascontiguousarray(uq[r * (KR // 8):(r + 1) * (KR // 8)]),
            usc=np.ascontiguousarray(usc[np.asarray(I["u_gidx"]).reshape(-1)[0::128][r * c["KCS"]:(r + 1) * c["KCS"]]]),
            uqz=np.ascontiguousarray(uqz[np.asarray(I["u_gidx"]).reshape(-1)[0::128][r * c["KCS"]:(r + 1) * c["KCS"]]]),
            dq=np.ascontiguousarray(dq[:, r * c["DQ_COLS"]:(r + 1) * c["DQ_COLS"]]),
            dsc=np.ascontiguousarray(dsc[np.asarray(I["d_gidx"]).reshape(-1)[0::128]][:, r * c["DQ_COLS"]:(r + 1) * c["DQ_COLS"]]),
            dqz=np.ascontiguousarray(dqz[np.asarray(I["d_gidx"]).reshape(-1)[0::128]][:, r * (c["DQ_COLS"] // 8):(r + 1) * (c["DQ_COLS"] // 8)]),
            mask_nib=mask_nib, pow16=pow16, identf=identf, identb=identb,
            mup=mup, mdn=mdn,
        )
        in_maps.append(m)
    return in_maps



ROLES = ("sync", "tensor", "vector", "scalar", "gpsimd")


class Ctx:
    def __init__(self, nc, cfg, sems, bufs, dbg):
        self.nc = nc
        self.cfg = cfg
        self.sems = sems
        self.bufs = bufs
        self.cnt = {k: 0 for k in sems}
        self.role = None
        self.eng = None
        self.dbg = dbg
        self.marks = {}
        self.outst = {k: 0 for k in sems}

    def bump(self, name, by):
        self.cnt[name] += by
        return self.cnt[name]

    def wait(self, role, name, val=None):
        v = self.cnt[name] if val is None else val
        if self.role == role:
            self.eng.wait_ge(self.sems[name], v)

    def dma(self, role, sem, out, in_, slow=False, fence=True):
        if fence and self.outst[sem] > 0:
            self.wait(role, sem, self.cnt[sem])
            self.outst[sem] = 0
        self.bump(sem, 16)
        self.outst[sem] += 1
        if self.role == role:
            self.eng.dma_start(out=out, in_=in_, allow_slow_non_contiguous=slow).then_inc(self.sems[sem], 16)

    def idma(self, role, sem, out, in_dram, idx_ap, fence=True):
        if fence and self.outst[sem] > 0:
            self.wait(role, sem, self.cnt[sem])
            self.outst[sem] = 0
        self.bump(sem, 16)
        self.outst[sem] += 1
        if self.role == role:
            self.eng.indirect_dma_start(
                out=out, out_offset=None, in_=in_dram,
                in_offset=bass.IndirectOffsetOnAxis(ap=idx_ap, axis=0),
            ).then_inc(self.sems[sem], 16)

    def op(self, role, fn, sem=None):
        if sem is not None:
            self.bump(sem, 1)
        if self.role == role:
            ins = fn(self.eng)
            if sem is not None:
                ins.then_inc(self.sems[sem], 1)

    def drain(self, role):
        if self.role == role:
            self.eng.drain()

    DMA_SEMS_SYNC = ("ld", "st", "ld0", "ld1", "ld2", "ld3", "ld4", "ld5")

    def barrier(self):
        for s in self.DMA_SEMS_SYNC:
            if self.cnt[s]:
                self.wait("sync", s, self.cnt[s])
                self.outst[s] = 0
        if self.cnt["gst"]:
            self.wait("gpsimd", "gst", self.cnt["gst"])
            self.outst["gst"] = 0
        eng_sem = dict(sync="qD", tensor="tD", vector="vD", scalar="sD", gpsimd="gD")
        for r, sname in eng_sem.items():
            self.bump(sname, 1)
            if self.role == r:
                if r != "sync":
                    self.eng.drain()
                self.eng.sem_inc(self.sems[sname], 1)
        for r in ROLES:
            for r2, sname in eng_sem.items():
                if r2 != r:
                    self.wait(r, sname, self.cnt[sname])

    def collective(self, sem, in_ap, out_ap, groups=None):
        self.bump(sem, 1)
        if self.role == "gpsimd":
            rg = [list(range(NCORES))] if groups is None else groups
            self.eng.collective_compute(
                "AllGather", ALU.bypass, replica_groups=rg,
                ins=[in_ap.opt()], outs=[out_ap.opt()],
            ).then_inc(self.sems[sem], 1)



def build_nc(cfg, dbg_stop=None):
    c = derived(cfg)
    D, NH, HD, NKV, INTER = cfg["D"], cfg["NH"], cfg["HD"], cfg["NKV"], cfg["INTER"]
    T, PANEL, NPAN, TPP, KC, KCS = c["T"], c["PANEL"], c["NPAN"], c["TPP"], c["KC"], c["KCS"]
    NTT, NT_I, QF, KVF = c["NTT"], c["NT_I"], c["QF"], c["KVF"]
    KVG, RPK = c["KVG"], c["R_PER_KVH"]
    CT = c["CACHE_T"] // RPK
    CACHE_T, NEW_T, KV_B = c["CACHE_T"], c["NEW_T"], c["KV_B"]
    NFT_QKV = 3 * D // 128
    NFT_Q, NFT_KV = QF // 128, KVF // 128
    DP = D // 512
    KR = D // NCORES
    QR = QF // NCORES
    DQC = c["DQ_COLS"]
    GRP_I = c["GRP_I"]
    SCALE = 1.0 / math.sqrt(HD)

    nc = bass.Bass("TRN2", target_bir_lowering=False, debug=False)

    def din(name, shape, dt=F32):
        return nc.dram_tensor(name, shape, dt, kind="ExternalInput").ap()

    x_in = din("x", [T, D])
    qkvw_in = din("qkvw", [NFT_QKV * 128, KCS * 128])
    qkvb_in = din("qkvb", [128, 3 * NH])
    ow_in = din("ow", [KR, D])
    ob_in = din("ob", [128, D // 128])
    obrow_in = din("ob_row", [1, D])
    wqT_in = din("wqT", [NFT_Q * 128, KCS * 128])
    wkT_in = din("wkT", [NFT_KV * 128, KCS * 128])
    wvT_in = din("wvT", [NFT_KV * 128, KCS * 128])
    woT_in = din("woT", [QR, D])
    kcache_in = din("kcache", [cfg["NBLK"] * cfg["BLOCK"], HD])
    vcache_in = din("vcache", [cfg["NBLK"] * cfg["BLOCK"], HD])
    blkidx_in = din("blkidx", [1, CT // cfg["BLOCK"]], I32)
    csin_in = din("csin", [128, 2])
    gq_in = din("gq", [KR // 8, INTER], I32)
    gsc_in = din("gsc", [KCS, INTER])
    gqz_in = din("gqz", [KCS, INTER // 8], I32)
    uq_in = din("uq", [KR // 8, INTER], I32)
    usc_in = din("usc", [KCS, INTER])
    uqz_in = din("uqz", [KCS, INTER // 8], I32)
    dq_in = din("dq", [INTER // 8, DQC], I32)
    dsc_in = din("dsc", [GRP_I, DQC])
    dqz_in = din("dqz", [GRP_I, DQC // 8], I32)
    mask_in = din("mask_nib", [128, 1], I32)
    pow16_in = din("pow16", [128, 1])
    identf_in = din("identf", [128, 128])
    identb_in = din("identb", [128, 128], BF16)
    mup_in = din("mup", [128, 128])
    mdn_in = din("mdn", [128, 128])

    out_ext = nc.dram_tensor("out", [T, D], F32, kind="ExternalOutput").ap()

    def dram(name, shape, dt):
        return nc.dram_tensor(name, shape, dt).ap()

    res1 = dram("res1", [T, D], F32)
    res2 = dram("res2", [T, D], F32)
    res3 = dram("res3", [T, D], F32)
    agw_qkvw_i = dram("agw_qkvw_i", [NFT_QKV * 128, KCS * 128], BF16)
    agw_qkvw_o = dram("agw_qkvw_o", [NCORES * NFT_QKV * 128, KCS * 128], BF16)
    agw_wqT_i = dram("agw_wqT_i", [NFT_Q * 128, KCS * 128], BF16)
    agw_wqT_o = dram("agw_wqT_o", [NCORES * NFT_Q * 128, KCS * 128], BF16)
    agw_wkT_i = dram("agw_wkT_i", [NFT_KV * 128, KCS * 128], BF16)
    agw_wkT_o = dram("agw_wkT_o", [NCORES * NFT_KV * 128, KCS * 128], BF16)
    agw_wvT_i = dram("agw_wvT_i", [NFT_KV * 128, KCS * 128], BF16)
    agw_wvT_o = dram("agw_wvT_o", [NCORES * NFT_KV * 128, KCS * 128], BF16)
    agw_ow_i = dram("agw_ow_i", [KR, D], BF16)
    agw_ow_o = dram("agw_ow_o", [D, D], BF16)
    agw_woT_i = dram("agw_woT_i", [QR, D], BF16)
    agw_woT_o = dram("agw_woT_o", [QF, D], BF16)
    agw_g_i = dram("agw_g_i", [NT_I * 128, KCS * 128], BF16)
    agw_g_o = dram("agw_g_o", [NCORES * NT_I * 128, KCS * 128], BF16)
    agw_u_i = dram("agw_u_i", [NT_I * 128, KCS * 128], BF16)
    agw_u_o = dram("agw_u_o", [NCORES * NT_I * 128, KCS * 128], BF16)
    agw_d_i = dram("agw_d_i", [INTER, DQC], BF16)
    agw_d_o = dram("agw_d_o", [NCORES * INTER, DQC], BF16)
    aga_k_i = dram("aga_k_i", [D, T], BF16)
    aga_k_o = dram("aga_k_o", [4 * D, T], BF16)
    aga_v_i = dram("aga_v_i", [T, D], BF16)
    aga_v_o = dram("aga_v_o", [4 * T, D], BF16)
    aga_nk_i = dram("aga_nk_i", [1, NH], F32)
    aga_nk_o = dram("aga_nk_o", [4 * 1, NH], F32)
    agb1_k_i = dram("agb1_k_i", [KVF, T], BF16)
    agb1_k_o = dram("agb1_k_o", [NCORES * KVF, T], BF16)
    agb1_v_i = dram("agb1_v_i", [T, KVF], BF16)
    agb1_v_o = dram("agb1_v_o", [NCORES * T, KVF], BF16)
    agb1_nk_i = dram("agb1_nk_i", [1, NKV], F32)
    agb1_nk_o = dram("agb1_nk_o", [NCORES * 1, NKV], F32)
    agb2_k_i = dram("agb2_k_i", [HD, CT], BF16)
    agb2_k_o = dram("agb2_k_o", [NCORES * HD, CT], BF16)
    agb2_v_i = dram("agb2_v_i", [CT, HD], BF16)
    agb2_v_o = dram("agb2_v_o", [NCORES * CT, HD], BF16)
    agb2_nk_i = dram("agb2_nk_i", [1, 1], F32)
    agb2_nk_o = dram("agb2_nk_o", [NCORES * 1, 1], F32)
    szg_d = dram("szg_d", [KCS, INTER], F32)
    szu_d = dram("szu_d", [KCS, INTER], F32)
    szd_d = dram("szd_d", [GRP_I, DQC], F32)

    es = ExitStack()
    sb = lambda name, shape, dt: es.enter_context(nc.sbuf_tensor(name, shape, dt))
    ps_ = lambda name, shape: es.enter_context(nc.psum_tensor(name, shape, F32))

    POOLB = 161 * 1024
    pool = sb("pool", [128, POOLB], mybir.dt.uint8)

    def carve(off, dt, shape):
        nbytes = shape[1] * {F32: 4, BF16: 2, I32: 4, U32: 4}[dt]
        assert off % 64 == 0 and off + nbytes <= POOLB, (off, nbytes)
        return pool[0:shape[0], off:off + nbytes].bitcast(dt)

    wb = [sb("wb0", [128, KC * 128], BF16), sb("wb1", [128, KC * 128], BF16)]
    rs = [sb("rs0", [128, 512], BF16), sb("rs1", [128, 512], BF16)]
    pt = [sb("pt0", [128, 512], BF16), sb("pt1", [128, 512], BF16)]
    pt2 = sb("pt2", [128, 512], BF16)
    pt3 = sb("pt3", [128, 512], BF16)
    obrow_b = sb("obrow_b", [1, cfg["D"]], BF16)
    onesrow_b = sb("onesrow_b", [1, 128], BF16)
    ev = [sb("ev0", [128, 512], F32), sb("ev1", [128, 512], F32),
          sb("ev2", [128, 512], F32), sb("ev3", [128, 512], F32)]
    identf = sb("identf_s", [128, 128], F32)
    identb = sb("identb_s", [128, 128], BF16)
    mupt = sb("mup_s", [128, 128], F32)
    mdnt = sb("mdn_s", [128, 128], F32)
    rmat = sb("rmat", [128, 128], BF16)
    rmatf = sb("rmatf", [128, 128], F32)
    qkvb_t = sb("qkvb_s", [128, 3 * NH], F32)
    ob_t = sb("ob_s", [128, D // 128], F32)
    maskt = sb("maskt", [128, 1], I32)
    pow16t = sb("pow16t", [128, 1], F32)
    cs_t = sb("cs_t", [128, 2], F32)
    onesb = sb("onesb", [128, 1], BF16)
    onesf = sb("onesf", [128, 1], F32)
    mqrow = sb("mqrow", [1, 64], F32)
    mkrow = sb("mkrow", [1, 64], F32)
    mtmp = sb("mtmp", [1, 256], F32)
    mbias = sb("mbias", [128, NH], F32)
    lnsc = sb("lnsc", [128, 8], F32)
    gidx = sb("gidx", [128, CT // 128], U32)
    gidxf = sb("gidxf", [128, CT // 128], F32)
    iotap = sb("iotap", [128, 1], I32)
    onesrow = sb("onesrow", [1, 128], F32)
    blki = sb("blki", [1, CT // cfg["BLOCK"]], F32)
    blkii = sb("blkii", [1, CT // cfg["BLOCK"]], I32)
    blkb = sb("blkb", [128, CT // cfg["BLOCK"]], F32)

    linv = sb("linv", [128, 512], F32)

    ps = [ps_(f"ps{i}", [128, 512]) for i in range(6)]
    pl = [ps_(f"pl{i}", [1, 512]) for i in range(2)]

    sems = {}
    for s in ("ld", "st", "gst", "tD", "vD", "sD", "gD", "qD",
              "ld0", "ld1", "ld2", "ld3", "ld4", "ld5",
              "ccq", "cco", "cca", "ccf", "ccw", "ccb2", "ccwo", "ccb1"):
        sems[s] = es.enter_context(nc.semaphore(s))

    bufs = dict(locals())
    return _emit(nc, c, es, bufs, sems, dbg_stop)


def _emit(nc, c, es, B, sems, dbg_stop):
    cfg = c
    D, NH, HD, NKV, INTER = c["D"], c["NH"], c["HD"], c["NKV"], c["INTER"]
    T, PANEL, NPAN, TPP, KC, KCS = c["T"], c["PANEL"], c["NPAN"], c["TPP"], c["KC"], c["KCS"]
    NTT, NT_I, QF, KVF = c["NTT"], c["NT_I"], c["QF"], c["KVF"]
    KVG, RPK = c["KVG"], c["R_PER_KVH"]
    CT = c["CACHE_T"] // RPK
    CACHE_T, NEW_T, KV_B = c["CACHE_T"], c["NEW_T"], c["KV_B"]
    BLOCK = c["BLOCK"]
    NFT_QKV, NFT_Q, NFT_KV = 3 * D // 128, QF // 128, KVF // 128
    DP = D // 512
    KR, QR, DQC, GRP_I = D // NCORES, QF // NCORES, c["DQ_COLS"], c["GRP_I"]
    SCALE = 1.0 / math.sqrt(HD)
    NPP = PANEL

    g = B
    x_in, out_ext = g["x_in"], g["out_ext"]
    res1, res2, res3 = g["res1"], g["res2"], g["res3"]
    pool, wb, rs, pt, ev = g["pool"], g["wb"], g["rs"], g["pt"], g["ev"]
    identf, identb, mupt, mdnt, rmat, rmatf = g["identf"], g["identb"], g["mupt"], g["mdnt"], g["rmat"], g["rmatf"]
    qkvb_t, ob_t, maskt, pow16t, cs_t = g["qkvb_t"], g["ob_t"], g["maskt"], g["pow16t"], g["cs_t"]
    onesb, onesf, mbias = g["onesb"], g["onesf"], g["mbias"]
    mqrow, mkrow, mtmp = g["mqrow"], g["mkrow"], g["mtmp"]
    lnsc, gidx, blki, blkb, linv = g["lnsc"], g["gidx"], g["blki"], g["blkb"], g["linv"]
    gidxf, iotap, onesrow, blkii = g["gidxf"], g["iotap"], g["onesrow"], g["blkii"]
    pt2, pt3, obrow_b, onesrow_b = g["pt2"], g["pt3"], g["obrow_b"], g["onesrow_b"]
    ps, pl = g["ps"], g["pl"]

    LNT_OFF, QT_OFF, KV_OFF, XT_OFF, XNB_OFF = 0, 32 * 1024, 96 * 1024, 121 * 1024, 153 * 1024
    def lnT():
        return pool[0:128, LNT_OFF:LNT_OFF + KC * PANEL * 2].bitcast(BF16)
    def qt():
        return pool[0:128, QT_OFF:QT_OFF + NH * T * 2].bitcast(BF16)
    def hT():
        return pool[0:128, QT_OFF:QT_OFF + NT_I * PANEL * 2].bitcast(BF16)
    def kvK(s):
        off = KV_OFF + s * KV_B * 2
        return pool[0:128, off:off + KV_B * 2].bitcast(BF16)
    def kvV(s):
        off = KV_OFF + (2 + s) * ((KV_B // 128) * 128 * 2)
        return pool[0:128, off:off + (KV_B // 128) * 128 * 2].bitcast(BF16)
    def xt(s):
        off = XT_OFF + s * D * 4
        return pool[0:128, off:off + D * 4].bitcast(F32)
    def xnb():
        return pool[0:128, XNB_OFF:XNB_OFF + D * 2].bitcast(BF16)
    DQN = min(2816, INTER)
    DQNB = DQN * 4
    def dq_q():
        return pool[0:128, 0:DQNB].bitcast(I32)
    def dq_s():
        return pool[0:128, DQNB:2 * DQNB].bitcast(F32)
    def dq_z():
        return pool[0:128, 2 * DQNB:3 * DQNB].bitcast(F32)
    def dq_o():
        return pool[0:128, 3 * DQNB:3 * DQNB + DQN * 2].bitcast(BF16)
    def dqf():
        return pool[0:128, 3 * DQNB + DQN * 2:4 * DQNB + DQN * 2].bitcast(F32)
    GOFF = 4 * DQNB + DQN * 2
    def gk(s):
        off = GOFF + s * HD * 4
        return pool[0:128, off:off + HD * 4].bitcast(F32)
    def gkv(s):
        off = GOFF + (2 + s) * HD * 4
        return pool[0:128, off:off + HD * 4].bitcast(F32)
    def gkb(s):
        off = GOFF + 4 * HD * 4 + s * HD * 2
        return pool[0:128, off:off + HD * 2].bitcast(BF16)
    def gvb(s):
        off = GOFF + 4 * HD * 4 + (2 + s) * HD * 2
        return pool[0:128, off:off + HD * 2].bitcast(BF16)
    def gsq(s):
        off = GOFF + 4 * HD * 4 + 4 * HD * 2 + s * HD * 2
        return pool[0:128, off:off + HD * 2].bitcast(BF16)

    agw_qkvw = g["agw_qkvw_o"].rearrange("(r f p) k -> r f p k", r=NCORES, p=128)
    agw_wqT = g["agw_wqT_o"].rearrange("(r f p) k -> r f p k", r=NCORES, p=128)
    agw_wkT = g["agw_wkT_o"].rearrange("(r f p) k -> r f p k", r=NCORES, p=128)
    agw_wvT = g["agw_wvT_o"].rearrange("(r f p) k -> r f p k", r=NCORES, p=128)
    agw_g = g["agw_g_o"].rearrange("(r f p) k -> r f p k", r=NCORES, p=128)
    agw_u = g["agw_u_o"].rearrange("(r f p) k -> r f p k", r=NCORES, p=128)
    agw_d = g["agw_d_o"].rearrange("(r i) d -> r i d", r=NCORES)
    aga_k = g["aga_k_o"].rearrange("(r f) t -> r f t", r=4)
    aga_v = g["aga_v_o"].rearrange("(r t) f -> r t f", r=4)
    aga_nk = g["aga_nk_o"]
    agb1_k = g["agb1_k_o"].rearrange("(r f) t -> r f t", r=NCORES)
    agb1_v = g["agb1_v_o"].rearrange("(r t) f -> r t f", r=NCORES)
    agb1_nk = g["agb1_nk_o"]
    agb2_k = g["agb2_k_o"].rearrange("(r f) t -> r f t", r=NCORES)
    agb2_v = g["agb2_v_o"].rearrange("(r t) f -> r t f", r=NCORES)
    agb2_nk = g["agb2_nk_o"]

    def lhsT_view(agw_view, ft):
        return agw_view[:, ft].rearrange("r p k -> p r k")

    RES = [x_in, res1, res2, res3, out_ext]

    def psT(i):
        return ps[i][:, 0:64].bitcast(BF16)

    def ph_prep(ctx):
        C = ctx
        C.dma("sync", "ld", identf[:], g["identf_in"])
        C.dma("sync", "ld", identb[:], g["identb_in"], fence=False)
        C.dma("sync", "ld", mupt[:], g["mup_in"], fence=False)
        C.dma("sync", "ld", mdnt[:], g["mdn_in"], fence=False)
        C.dma("sync", "ld", qkvb_t[:], g["qkvb_in"], fence=False)
        C.dma("sync", "ld", ob_t[:], g["ob_in"], fence=False)
        C.dma("sync", "ld", maskt[:], g["mask_in"], fence=False)
        C.dma("sync", "ld", pow16t[:], g["pow16_in"], fence=False)
        C.dma("sync", "ld", cs_t[:], g["csin_in"], fence=False)
        C.dma("sync", "ld", blkii[:], g["blkidx_in"], fence=False)
        C.dma("gpsimd", "gst", obrow_b[0:1, 0:D], g["obrow_in"])
        ld_small = C.cnt["ld"]

        C.wait("vector", "ld", ld_small)
        C.op("vector", lambda e: e.memset(onesf[:], 1.0))
        C.op("vector", lambda e: e.memset(onesb[:], 1.0))
        C.op("vector", lambda e: e.tensor_scalar(rmatf[:], identf[:], cs_t[:, 0:1], None, ALU.mult))
        C.op("vector", lambda e: e.tensor_scalar(mupt[:], mupt[:], cs_t[:, 1:2], None, ALU.mult))
        C.op("vector", lambda e: e.tensor_scalar(mdnt[:], mdnt[:], cs_t[:, 1:2], None, ALU.mult))
        C.drain("vector")
        C.op("vector", lambda e: e.tensor_tensor(rmatf[:], rmatf[:], mupt[:], ALU.add))
        C.drain("vector")
        C.op("vector", lambda e: e.tensor_tensor(rmatf[:], rmatf[:], mdnt[:], ALU.subtract), sem="vD")
        vD_rmat = C.cnt["vD"]
        C.wait("scalar", "vD", vD_rmat)
        C.op("scalar", lambda e: e.copy(rmat[:], rmatf[:]), sem="sD")

        C.op("vector", lambda e: e.memset(onesrow[:], 1.0))
        C.op("vector", lambda e: e.memset(onesrow_b[:], 1.0))
        C.op("vector", lambda e: e.tensor_copy(blki[:], blkii[:]), sem="vD")
        vD_ones = C.cnt["vD"]

        TPB = BLOCK // 128
        NTAU = CT // 128
        C.wait("tensor", "ld", ld_small)
        C.wait("tensor", "vD", vD_ones)
        C.op("tensor", lambda e: e.matmul(ps[5][:, 0:CT // BLOCK], onesrow[:],
                                          blki[:], start=True, stop=True), sem="tD")
        tD_blk = C.cnt["tD"]
        C.wait("vector", "tD", tD_blk)
        C.op("vector", lambda e: e.tensor_scalar(blkb[:], ps[5][:, 0:CT // BLOCK], float(BLOCK), None, ALU.mult))
        C.op("gpsimd", lambda e: e.iota(iotap[:], pattern=[[0, 1]], base=0,
                                        channel_multiplier=1), sem="gD")
        gD_iota = C.cnt["gD"]
        C.wait("vector", "gD", gD_iota)
        C.drain("vector")
        for tau in range(NTAU):
            sel = tau // TPB
            off = (tau % TPB) * 128
            C.op("vector", lambda e, tau=tau, sel=sel, off=off: e.scalar_tensor_tensor(
                gidxf[:, tau:tau + 1], iotap[:], float(off),
                blkb[:, sel:sel + 1], ALU.add, ALU.add))
        C.drain("vector")
        C.op("vector", lambda e: e.tensor_copy(gidx[:, 0:NTAU], gidxf[:, 0:NTAU]), sem="vD")
        vD_gidx = C.cnt["vD"]
        C.marks.update(ld_small=ld_small, vD_gidx=vD_gidx)

    def ph_gather(ctx):
        C = ctx
        NTAU = CT // 128
        C.op("vector", lambda e: e.memset(mtmp[0:1, 0:1], -3e38))
        C.drain("vector")
        C.wait("gpsimd", "vD", C.marks["vD_gidx"])
        tD0, sD0, vD0, st0 = C.cnt["tD"], C.cnt["sD"], C.cnt["vD"], C.cnt["st"]
        for tau in range(NTAU):
            s = tau % 2
            grp, gi = tau // 4, tau % 4
            if tau >= 2:
                C.wait("gpsimd", "tD", tD0 + 2 * (tau - 2) + 1)
                C.wait("gpsimd", "sD", sD0 + 2 * (tau - 2) + 2 + (tau - 2) // 4)
            C.idma("gpsimd", "gst", gk(s)[:, 0:HD], g["kcache_in"], gidx[:, tau:tau + 1])
            C.idma("gpsimd", "gst", gkv(s)[:, 0:HD], g["vcache_in"], gidx[:, tau:tau + 1], fence=False)
            gk_mark = C.cnt["gst"]
            C.wait("tensor", "gst", gk_mark)
            if tau >= 2:
                C.wait("tensor", "sD", sD0 + 2 * (tau - 2) + 1 + (tau - 2) // 4)
                C.wait("tensor", "st", st0 + 32 * (tau - 1))
            C.op("tensor", lambda e, s=s: e.transpose(ps[4 + s][:, 0:128], gk(s)[:, 0:HD], identf[:]),
                 sem="tD")
            tr_mark = C.cnt["tD"]
            C.wait("scalar", "tD", tr_mark)
            if tau >= 2:
                C.wait("scalar", "st", st0 + 32 * (tau - 1))
            C.op("scalar", lambda e, s=s: e.activation(gkb(s)[:, 0:HD], ps[4 + s][:, 0:128], ACT.Identity),
                 sem="sD")
            kevac = C.cnt["sD"]
            C.wait("scalar", "gst", gk_mark)
            C.op("scalar", lambda e, s=s: e.activation(gvb(s)[:, 0:HD], gkv(s)[:, 0:HD], ACT.Identity), sem="sD")
            vevac = C.cnt["sD"]
            C.wait("vector", "sD", kevac)
            if tau >= 2:
                C.wait("vector", "tD", tD0 + 2 * (tau - 2) + 2)
            C.op("vector", lambda e, s=s: e.tensor_tensor(gsq(s)[:, 0:128], gkb(s)[:, 0:HD], gkb(s)[:, 0:HD],
                                                          ALU.mult), sem="vD")
            C.wait("tensor", "vD", C.cnt["vD"])
            if gi == 0 and grp > 0:
                C.wait("tensor", "sD", sD0 + 2 * tau + grp)
            C.op("tensor", lambda e, s=s, gi=gi: e.matmul(pl[0][0:1, gi * 128:(gi + 1) * 128], onesb[:],
                                                          gsq(s)[:, 0:128], start=True, stop=True), sem="tD")
            C.wait("sync", "sD", kevac)
            C.dma("sync", "st", g["agb2_k_i"][0:HD, tau * 128:(tau + 1) * 128], gkb(s)[:, 0:HD])
            C.wait("sync", "sD", vevac)
            C.dma("sync", "st", g["agb2_v_i"][tau * 128:(tau + 1) * 128, 0:HD], gvb(s)[:, 0:HD])
            if gi == 3 or tau == NTAU - 1:
                C.wait("scalar", "tD", C.cnt["tD"])
                C.op("scalar", lambda e, gi=gi: e.activation(
                    linv[0:1, 0:(gi + 1) * 128], pl[0][0:1, 0:(gi + 1) * 128], ACT.Identity), sem="sD")
                C.wait("vector", "sD", C.cnt["sD"])
                C.op("vector", lambda e, gi=gi: e.tensor_reduce(
                    mtmp[0:1, 1:2], linv[0:1, 0:(gi + 1) * 128], mybir.AxisListType.X, ALU.max))
                C.drain("vector")
                C.op("vector", lambda e: e.tensor_tensor(mtmp[0:1, 0:1], mtmp[0:1, 0:1], mtmp[0:1, 1:2],
                                                         ALU.max), sem="vD")
                C.drain("vector")
        C.wait("sync", "vD", C.cnt["vD"])
        C.dma("sync", "st", g["agb2_nk_i"][0:1, 0:1], mtmp[0:1, 0:1])
        C.marks["st_agb2"] = C.cnt["st"]

    ZW = min(5504, max(INTER, DQC))
    Z0 = GOFF + 4 * 1024
    def zrow(R):
        return pool[0:R, Z0:Z0 + ZW * 4].bitcast(F32)
    def scrow(R):
        return pool[0:R, Z0 + ZW * 4:Z0 + 2 * ZW * 4].bitcast(F32)
    def qzt(R):
        return pool[0:R, Z0 + 2 * ZW * 4:Z0 + 2 * ZW * 4 + (ZW // 8) * 4].bitcast(I32)
    def zqi2(R):
        return pool[0:R, Z0 + 2 * ZW * 4 + 704 * 4:Z0 + 2 * ZW * 4 + 1408 * 4].bitcast(I32)
    def castb(s):
        off = Z0 + 2 * ZW * 4 + 1408 * 4 + s * 8 * 1024
        return pool[0:128, off:off + 8 * 1024].bitcast(BF16)

    def ph_wcast(ctx):
        C = ctx
        jobs = []
        for src, dst, npiece, width in (
            (g["qkvw_in"], g["agw_qkvw_i"], NFT_QKV, KCS * 128),
            (g["ow_in"], g["agw_ow_i"], KR // 128, D),
            (g["wqT_in"], g["agw_wqT_i"], NFT_Q, KCS * 128),
            (g["wkT_in"], g["agw_wkT_i"], NFT_KV, KCS * 128),
            (g["wvT_in"], g["agw_wvT_i"], NFT_KV, KCS * 128),
            (g["woT_in"], g["agw_woT_i"], QR // 128, D),
        ):
            for p in range(npiece):
                jobs.append((src[p * 128:(p + 1) * 128, 0:width], dst[p * 128:(p + 1) * 128, 0:width], width))
        sD0, st0 = C.cnt["sD"], C.cnt["st"]
        ldm = []
        for it, (srcap, dstap, width) in enumerate(jobs):
            s = it % 2
            if it >= 2:
                C.wait("sync", "sD", sD0 + it - 1)
            C.dma("sync", "ld" + str(s), xt(s)[:, 0:width], srcap)
            ldm.append(C.cnt["ld" + str(s)])
            C.wait("scalar", "ld" + str(s), ldm[it])
            if it >= 2:
                C.wait("scalar", "st", st0 + 16 * (it - 1))
            C.op("scalar", lambda e, s=s, w=width: e.activation(castb(s)[:, 0:w], xt(s)[:, 0:w], ACT.Identity),
                 sem="sD")
            C.wait("sync", "sD", C.cnt["sD"])
            C.dma("sync", "st", dstap, castb(s)[:, 0:width])
        C.marks["st_wcast"] = C.cnt["st"]
        n1 = NFT_QKV
        n2 = n1 + KR // 128
        C.marks["st_qkvw"] = st0 + 16 * n1
        C.marks["st_ow"] = st0 + 16 * n2

    def ph_qzsz(ctx):
        C = ctx
        for (qz_in, sc_in, sz_d, R, N) in (
            (g["gqz_in"], g["gsc_in"], g["szg_d"], KCS, INTER),
            (g["uqz_in"], g["usc_in"], g["szu_d"], KCS, INTER),
            (g["dqz_in"], g["dsc_in"], g["szd_d"], GRP_I, DQC),
        ):
            for c0 in range(0, N, ZW):
                w = min(ZW, N - c0)
                C.dma("sync", "ld", qzt(R)[0:R, 0:w // 8], qz_in[0:R, c0 // 8:(c0 + w) // 8])
                C.dma("sync", "ld", scrow(R)[0:R, 0:w], sc_in[0:R, c0:c0 + w])
                C.wait("vector", "ld", C.cnt["ld"])
                C.wait("vector", "st", C.cnt["st"])
                for jj in range(8):
                    C.op("vector", lambda e, jj=jj, R=R, w=w: e.tensor_scalar(
                        zqi2(R)[0:R, 0:w // 8], qzt(R)[0:R, 0:w // 8], 4 * jj, None, ALU.logical_shift_right))
                    C.drain("vector")
                    C.op("vector", lambda e, R=R, w=w: e.tensor_scalar(
                        zqi2(R)[0:R, 0:w // 8], zqi2(R)[0:R, 0:w // 8], 15, None, ALU.bitwise_and))
                    C.drain("vector")
                    C.op("vector", lambda e, jj=jj, R=R, w=w: e.tensor_copy(
                        zrow(R)[0:R, 0:w].rearrange("r (n e) -> r n e", e=8)[:, :, jj:jj + 1],
                        zqi2(R)[0:R, 0:w // 8].unsqueeze(2)))
                    C.drain("vector")
                C.op("vector", lambda e, R=R, w=w: e.tensor_scalar(zrow(R)[0:R, 0:w], zrow(R)[0:R, 0:w],
                                                                   1.0, None, ALU.add))
                C.drain("vector")
                C.op("vector", lambda e, R=R, w=w: e.tensor_tensor(zrow(R)[0:R, 0:w], zrow(R)[0:R, 0:w],
                                                                   scrow(R)[0:R, 0:w], ALU.mult), sem="vD")
                C.wait("sync", "vD", C.cnt["vD"])
                C.dma("sync", "st", sz_d[0:R, c0:c0 + w], zrow(R)[0:R, 0:w])
        C.marks["st_sz"] = C.cnt["st"]

    def ph_trigger1(ctx):
        C = ctx
        C.wait("gpsimd", "st", C.marks["st_qkvw"])
        C.collective("ccq", g["agw_qkvw_i"], g["agw_qkvw_o"])
        C.wait("gpsimd", "st", C.marks["st_ow"])
        C.collective("cco", g["agw_ow_i"], g["agw_ow_o"])

    def ph_dequant(ctx):
        C = ctx
        agg_i = g["agw_g_i"].rearrange("(f p) k -> f p k", p=128).rearrange("f p k -> p f k")
        agu_i = g["agw_u_i"].rearrange("(f p) k -> f p k", p=128).rearrange("f p k -> p f k")
        jobs = []
        for (q_in, sz_d, sc_in, dsti, R, N, isdown) in (
            (g["gq_in"], g["szg_d"], g["gsc_in"], agg_i, KCS, INTER, False),
            (g["uq_in"], g["szu_d"], g["usc_in"], agu_i, KCS, INTER, False),
            (g["dq_in"], g["szd_d"], g["dsc_in"], None, GRP_I, DQC, True),
        ):
            for j in range(R):
                for c0 in range(0, N, DQN):
                    w = min(DQN, N - c0)
                    jobs.append((q_in, sz_d, sc_in, dsti, j, c0, w, isdown))
        vD0 = C.cnt["vD"]
        st_dq0 = C.cnt["st"]
        for it, (q_in, sz_d, sc_in, dsti, j, c0, w, isdown) in enumerate(jobs):
            if it >= 1:
                C.wait("sync", "vD", vD0 + 2 * it)
            C.dma("sync", "ld0", dq_q()[:, 0:w],
                  q_in[16 * j:16 * (j + 1), c0:c0 + w].unsqueeze(1).broadcast_to([16, 8, w]))
            C.dma("sync", "ld0", dq_s()[:, 0:w],
                  sc_in[j:j + 1, c0:c0 + w].broadcast_to([128, w]), fence=False)
            C.dma("sync", "ld0", dq_z()[:, 0:w],
                  sz_d[j:j + 1, c0:c0 + w].broadcast_to([128, w]), fence=False)
            ldmk = C.cnt["ld0"]
            C.wait("vector", "ld0", ldmk)
            if it >= 1:
                C.wait("vector", "st", st_dq0 + 16 * it)
            C.op("vector", lambda e, w=w: e.tensor_tensor(
                dq_q()[:, 0:w], dq_q()[:, 0:w], maskt[:, 0:1].broadcast_to([128, w]), ALU.bitwise_and))
            C.drain("vector")
            C.op("vector", lambda e, w=w: e.scalar_tensor_tensor(
                dqf()[:, 0:w], dq_q()[:, 0:w], pow16t[:, 0:1], dq_s()[:, 0:w],
                ALU.mult, ALU.mult), sem="vD")
            C.drain("vector")
            C.op("vector", lambda e, w=w: e.tensor_tensor(
                dq_o()[:, 0:w], dqf()[:, 0:w], dq_z()[:, 0:w], ALU.subtract), sem="vD")
            C.wait("sync", "vD", C.cnt["vD"])
            if isdown:
                dst = g["agw_d_i"][j * 128:(j + 1) * 128, 0:w]
            else:
                ft0, ft1 = c0 // 128, (c0 + w) // 128
                dst = dsti[:, ft0:ft1, j * 128:(j + 1) * 128]
            C.dma("sync", "st", dst, dq_o()[:, 0:w] if isdown else
                  dq_o()[:, 0:w].rearrange("p (f z) -> p f z", z=128))
        C.marks["st_dq"] = C.cnt["st"]

    def ph_trigger2(ctx):
        C = ctx
        C.wait("gpsimd", "st", C.marks["st_dq"])
        C.collective("ccf", g["agw_g_i"], g["agw_g_o"])
        C.collective("ccf", g["agw_u_i"], g["agw_u_o"])
        C.collective("ccf", g["agw_d_i"], g["agw_d_o"])
        C.collective("ccw", g["agw_wqT_i"], g["agw_wqT_o"])
        C.collective("ccw", g["agw_wkT_i"], g["agw_wkT_o"])
        C.collective("ccw", g["agw_wvT_i"], g["agw_wvT_o"])
        C.wait("gpsimd", "st", C.marks["st_agb2"])
        C.collective("ccb2", g["agb2_k_i"], g["agb2_k_o"])
        C.collective("ccb2", g["agb2_v_i"], g["agb2_v_o"])
        C.collective("ccb2", g["agb2_nk_i"], g["agb2_nk_o"])
        C.collective("ccwo", g["agw_woT_i"], g["agw_woT_o"])

    def ln_panel(ctx, src_rows, pan):
        C = ctx
        norm_marks, ev_marks = {}, {}
        for tt in range(TPP):
            i = pan * TPP + tt
            s = tt % 2
            if tt >= 2:
                C.wait("sync", "sD", norm_marks[tt - 2])
            C.dma("sync", "ld" + str(s), xt(s)[:, 0:D], src_rows[i * 128:(i + 1) * 128, 0:D])
            ldmk = C.cnt["ld" + str(s)]
            C.wait("scalar", "ld" + str(s), ldmk)
            C.op("scalar", lambda e, s=s: e.activation(xnb()[:, 0:D], xt(s)[:, 0:D], ACT.Identity,
                                                       accum_out=lnsc[:, 0:1]))
            C.drain("scalar")
            C.op("scalar", lambda e, s=s: e.activation(xnb()[:, 0:D], xt(s)[:, 0:D], ACT.Square,
                                                       accum_out=lnsc[:, 1:2]), sem="sD")
            C.wait("vector", "sD", C.cnt["sD"])
            C.op("vector", lambda e: e.tensor_scalar(lnsc[:, 2:3], lnsc[:, 0:1], 1.0 / D, None, ALU.mult))
            C.op("vector", lambda e: e.tensor_scalar(lnsc[:, 3:4], lnsc[:, 1:2], 1.0 / D, None, ALU.mult))
            C.drain("vector")
            C.op("vector", lambda e: e.tensor_tensor(lnsc[:, 4:5], lnsc[:, 2:3], lnsc[:, 2:3], ALU.mult))
            C.drain("vector")
            C.op("vector", lambda e: e.tensor_tensor(lnsc[:, 4:5], lnsc[:, 3:4], lnsc[:, 4:5], ALU.subtract))
            C.drain("vector")
            C.op("vector", lambda e: e.tensor_scalar(lnsc[:, 4:5], lnsc[:, 4:5], 1e-5, None, ALU.add), sem="vD")
            C.wait("scalar", "vD", C.cnt["vD"])
            C.op("scalar", lambda e: e.activation(lnsc[:, 5:6], lnsc[:, 4:5], ACT.Sqrt), sem="sD")
            C.wait("vector", "sD", C.cnt["sD"])
            C.op("vector", lambda e: e.reciprocal(lnsc[:, 6:7], lnsc[:, 5:6]))
            C.drain("vector")
            C.op("vector", lambda e: e.tensor_tensor(lnsc[:, 7:8], lnsc[:, 6:7], lnsc[:, 6:7], ALU.mult))
            C.drain("vector")
            C.op("vector", lambda e: e.tensor_tensor(lnsc[:, 7:8], lnsc[:, 4:5], lnsc[:, 7:8], ALU.mult))
            C.drain("vector")
            C.op("vector", lambda e: e.tensor_scalar(lnsc[:, 7:8], lnsc[:, 7:8], -0.5, 1.5, ALU.mult, ALU.add))
            C.drain("vector")
            C.op("vector", lambda e: e.tensor_tensor(lnsc[:, 6:7], lnsc[:, 6:7], lnsc[:, 7:8], ALU.mult))
            C.drain("vector")
            C.op("vector", lambda e: e.tensor_tensor(lnsc[:, 7:8], lnsc[:, 2:3], lnsc[:, 6:7], ALU.mult))
            C.drain("vector")
            C.op("vector", lambda e: e.tensor_scalar(lnsc[:, 7:8], lnsc[:, 7:8], -1.0, None, ALU.mult), sem="vD")
            C.wait("scalar", "vD", C.cnt["vD"])
            C.op("scalar", lambda e, s=s: e.activation(xnb()[:, 0:D], xt(s)[:, 0:D], ACT.Identity,
                                                       bias=lnsc[:, 7:8], scale=lnsc[:, 6:7]), sem="sD")
            norm_marks[tt] = C.cnt["sD"]
            nrm = C.cnt["sD"]
            for kc in range(KC):
                b = kc % 2
                C.wait("tensor", "sD", nrm if kc < 2 else ev_marks[(tt, kc - 2)])
                C.op("tensor", lambda e, b=b, kc=kc: e.transpose(psT(4 + b),
                                                                 xnb()[:, kc * 128:(kc + 1) * 128], identb[:]),
                     sem="tD")
                C.wait("scalar", "tD", C.cnt["tD"])
                C.op("scalar", lambda e, b=b, kc=kc, tt=tt: e.activation(
                    lnT()[:, kc * PANEL + tt * 128: kc * PANEL + (tt + 1) * 128],
                    psT(4 + b), ACT.Identity), sem="sD")
                ev_marks[(tt, kc)] = C.cnt["sD"]
        C.marks["lnT_sD"] = C.cnt["sD"]

    def gemm_panel(ctx, pan, jobs, rhs_off, epi, epi_bank_free, nkc=None, extra_k1=None):
        C = ctx
        NKC = KC if nkc is None else nkc
        mm_done = {}
        for it, (view, ft) in enumerate(jobs):
            s = it % 2
            if it >= 2:
                C.wait("sync", "tD", mm_done[it - 2])
            C.dma("sync", "ld" + str(s), wb[s][:, 0:NKC * 128], lhsT_view(view, ft))
            ldmk = C.cnt["ld" + str(s)]
            C.wait("tensor", "ld" + str(s), ldmk)
            if it >= 2:
                C.wait("tensor", *epi_bank_free(it - 2))
            bank = ps[it % 2]
            for kc in range(NKC):
                C.op("tensor", lambda e, kc=kc, s=s, bank=bank: e.matmul(
                    bank[:, 0:PANEL], wb[s][:, kc * 128:(kc + 1) * 128],
                    lnT()[:, kc * PANEL + rhs_off:kc * PANEL + rhs_off + PANEL],
                    start=(kc == 0), stop=(kc == NKC - 1 and extra_k1 is None)), sem="tD")
            if extra_k1 is not None:
                extra_k1(C, it, bank)
            mm_done[it] = C.cnt["tD"]
            epi(C, bank, ft, pan, it)

    def ph_lnqkv(ctx):
        C = ctx
        C.op("vector", lambda e: e.memset(mqrow[:], -3e38))
        C.op("vector", lambda e: e.memset(mkrow[:], -3e38))
        C.drain("vector")
        C.wait("tensor", "ccq", C.cnt["ccq"])
        C.wait("sync", "ccq", C.cnt["ccq"])
        for pan in range(NPAN):
            ln_panel(C, x_in, pan)
            C.barrier()
            fts = ([("k", NH + h) for h in range(NH)] + [("v", 2 * NH + h) for h in range(NH)]
                   + [("q", h) for h in range(NH)])
            bank_free = {-2: ("sD", 0), -1: ("sD", 0)}
            ptfree = {-2: ("sD", 0), -1: ("sD", 0)}
            sqfree = [("sD", 0)]
            vtfree = [("sD", 0)]
            trfree = {0: ("sD", 0), 1: ("sD", 0)}
            nrmfree = [("sD", 0)]
            linvfree = [("vD", 0)]

            def epi(C2, bank, ftk, pan2, it):
                kind, ft = fts[it]
                h = ft % NH
                C2.wait("scalar", "tD", C2.cnt["tD"])
                if kind == "q":
                    dst = qt()[:, h * T + pan2 * PANEL: h * T + pan2 * PANEL + PANEL]
                    C2.op("scalar", lambda e, dst=dst, ft=ft: e.activation(
                        dst, bank[:, 0:PANEL], ACT.Identity, bias=qkvb_t[:, ft:ft + 1]), sem="sD")
                    evk = C2.cnt["sD"]
                    bank_free[it] = ("sD", evk)
                    ptfree[it] = ptfree[it - 2]
                    C2.wait("vector", "sD", evk)
                    C2.wait("vector", *sqfree[0])
                    C2.op("vector", lambda e, dst=dst: e.tensor_tensor(pt2[:, 0:PANEL], dst, dst, ALU.mult),
                          sem="vD")
                    C2.wait("tensor", "vD", C2.cnt["vD"])
                    C2.wait("tensor", *nrmfree[0])
                    C2.op("tensor", lambda e: e.matmul(pl[0][0:1, 0:PANEL], onesb[:], pt2[:, 0:PANEL],
                                                       start=True, stop=True), sem="tD")
                    sqfree[0] = ("tD", C2.cnt["tD"])
                    C2.wait("scalar", "tD", C2.cnt["tD"])
                    C2.wait("scalar", *linvfree[0])
                    C2.op("scalar", lambda e: e.activation(
                        linv[0:1, 0:PANEL], pl[0][0:1, 0:PANEL], ACT.Identity), sem="sD")
                    nrmfree[0] = ("sD", C2.cnt["sD"])
                    C2.wait("vector", "sD", C2.cnt["sD"])
                    C2.op("vector", lambda e: e.tensor_reduce(
                        mtmp[0:1, 2:3], linv[0:1, 0:PANEL], mybir.AxisListType.X, ALU.max))
                    C2.drain("vector")
                    C2.op("vector", lambda e, h=h: e.tensor_tensor(
                        mqrow[0:1, h:h + 1], mqrow[0:1, h:h + 1], mtmp[0:1, 2:3], ALU.max), sem="vD")
                    C2.drain("vector")
                    linvfree[0] = ("vD", C2.cnt["vD"])
                elif kind == "k":
                    C2.wait("scalar", *ptfree[it - 2])
                    sl = pt[it % 2]
                    C2.op("scalar", lambda e, sl=sl, ft=ft: e.activation(
                        sl[:, 0:PANEL], bank[:, 0:PANEL], ACT.Identity, bias=qkvb_t[:, ft:ft + 1]), sem="sD")
                    evk = C2.cnt["sD"]
                    bank_free[it] = ("sD", evk)
                    C2.wait("sync", "sD", evk)
                    C2.dma("sync", "st", g["aga_k_i"][h * 128:(h + 1) * 128, pan2 * PANEL:pan2 * PANEL + PANEL],
                           sl[:, 0:PANEL])
                    stk = C2.cnt["st"]
                    C2.wait("vector", "sD", evk)
                    C2.wait("vector", *sqfree[0])
                    C2.op("vector", lambda e, sl=sl: e.tensor_tensor(pt2[:, 0:PANEL], sl[:, 0:PANEL],
                                                                     sl[:, 0:PANEL], ALU.mult), sem="vD")
                    vsq = C2.cnt["vD"]
                    C2.wait("tensor", "vD", vsq)
                    C2.wait("tensor", *nrmfree[0])
                    C2.op("tensor", lambda e: e.matmul(pl[0][0:1, 0:PANEL], onesb[:], pt2[:, 0:PANEL],
                                                       start=True, stop=True), sem="tD")
                    sqfree[0] = ("tD", C2.cnt["tD"])
                    C2.wait("scalar", "tD", C2.cnt["tD"])
                    C2.wait("scalar", *linvfree[0])
                    C2.op("scalar", lambda e: e.activation(
                        linv[0:1, 0:PANEL], pl[0][0:1, 0:PANEL], ACT.Identity), sem="sD")
                    nrmfree[0] = ("sD", C2.cnt["sD"])
                    C2.wait("vector", "sD", C2.cnt["sD"])
                    C2.op("vector", lambda e: e.tensor_reduce(
                        mtmp[0:1, 2:3], linv[0:1, 0:PANEL], mybir.AxisListType.X, ALU.max))
                    C2.drain("vector")
                    C2.op("vector", lambda e, h=h: e.tensor_tensor(
                        mkrow[0:1, h:h + 1], mkrow[0:1, h:h + 1], mtmp[0:1, 2:3], ALU.max), sem="vD")
                    C2.drain("vector")
                    linvfree[0] = ("vD", C2.cnt["vD"])
                    ptfree[it] = ("st", stk)
                else:
                    C2.wait("scalar", *ptfree[it - 2])
                    sl = pt[it % 2]
                    C2.op("scalar", lambda e, sl=sl, ft=ft: e.activation(
                        sl[:, 0:PANEL], bank[:, 0:PANEL], ACT.Identity, bias=qkvb_t[:, ft:ft + 1]), sem="sD")
                    evk = C2.cnt["sD"]
                    bank_free[it] = ("sD", evk)
                    lasttr = 0
                    for j in range(TPP):
                        C2.wait("tensor", "sD", evk)
                        C2.wait("tensor", *trfree[j % 2])
                        C2.op("tensor", lambda e, j=j, sl=sl: e.transpose(
                            psT(4 + j % 2), sl[:, j * 128:(j + 1) * 128], identb[:]), sem="tD")
                        lasttr = C2.cnt["tD"]
                        C2.wait("scalar", "tD", lasttr)
                        if j == 0:
                            C2.wait("scalar", *vtfree[0])
                        C2.op("scalar", lambda e, j=j: e.activation(
                            pt3[:, j * 128:(j + 1) * 128], psT(4 + j % 2), ACT.Identity), sem="sD")
                        trfree[j % 2] = ("sD", C2.cnt["sD"])
                    C2.wait("sync", "sD", C2.cnt["sD"])
                    C2.dma("sync", "st",
                           g["aga_v_i"][pan2 * PANEL:(pan2 + 1) * PANEL, h * 128:(h + 1) * 128]
                           .rearrange("(j p) c -> p j c", p=128),
                           pt3[:, 0:PANEL].rearrange("p (j cc) -> p j cc", cc=128))
                    vtfree[0] = ("st", C2.cnt["st"])
                    ptfree[it] = ("tD", lasttr)

            gemm_panel(C, pan, [(agw_qkvw, ft) for _, ft in fts], 0, epi,
                       lambda it: bank_free[it])
            C.barrier()
        C.wait("sync", "vD", C.cnt["vD"])
        C.dma("sync", "st", g["aga_nk_i"][0:1, 0:NH], mkrow[0:1, 0:NH])
        C.marks["st_aga"] = C.cnt["st"]

    def ph_trigger_aga(ctx):
        C = ctx
        C.wait("gpsimd", "st", C.marks["st_aga"])
        grp = [[0, 1, 2, 3], [4, 5, 6, 7]]
        C.collective("cca", g["aga_k_i"], g["aga_k_o"], groups=grp)
        C.collective("cca", g["aga_v_i"], g["aga_v_o"], groups=grp)
        C.collective("cca", g["aga_nk_i"], g["aga_nk_o"], groups=grp)

    def mprep(ctx, which):
        C = ctx
        if which == "a":
            C.wait("sync", "cca", C.cnt["cca"])
            C.dma("sync", "ld", mtmp[0:1, 64:64 + NH * 4],
                  g["aga_nk_o"].rearrange("r h -> h r"), slow=True)
            C.wait("vector", "ld", C.cnt["ld"])
            C.op("vector", lambda e: e.tensor_reduce(
                mkrow[0:1, 0:NH], mtmp[0:1, 64:64 + NH * 4].rearrange("o (h r) -> o h r", r=4),
                mybir.AxisListType.X, ALU.max), sem="vD")
            C.drain("vector")
        else:
            C.wait("sync", "ccb1", C.cnt["ccb1"])
            C.wait("sync", "ccb2", C.cnt["ccb2"])
            C.dma("sync", "ld", mtmp[0:1, 64:64 + NKV * 4],
                  g["agb1_nk_o"][0:4].rearrange("r h -> h r"), slow=True)
            C.dma("sync", "ld", mtmp[0:1, 160:160 + NCORES], g["agb2_nk_o"].rearrange("r o -> o r"),
                  fence=False, slow=True)
            C.wait("vector", "ld", C.cnt["ld"])
            C.op("vector", lambda e: e.tensor_reduce(
                mkrow[0:1, 0:NKV], mtmp[0:1, 64:64 + NKV * 4].rearrange("o (h r) -> o h r", r=4),
                mybir.AxisListType.X, ALU.max))
            C.drain("vector")
            C.op("vector", lambda e: e.tensor_reduce(
                mtmp[0:1, 0:NKV], mtmp[0:1, 160:160 + NCORES].rearrange("o (h r) -> o h r", r=RPK),
                mybir.AxisListType.X, ALU.max))
            C.drain("vector")
            C.op("vector", lambda e: e.tensor_tensor(
                mkrow[0:1, 0:NKV], mkrow[0:1, 0:NKV], mtmp[0:1, 0:NKV], ALU.max))
            C.drain("vector")
            C.op("vector", lambda e: e.tensor_copy(
                mkrow[0:1, 32:32 + NH].rearrange("o (k gg) -> o k gg", gg=KVG),
                mkrow[0:1, 0:NKV].unsqueeze(2).broadcast_to([1, NKV, KVG])))
            C.drain("vector")
            C.op("vector", lambda e: e.tensor_copy(mkrow[0:1, 0:NH], mkrow[0:1, 32:32 + NH]), sem="vD")
            C.drain("vector")
        C.op("vector", lambda e: e.tensor_tensor(mtmp[0:1, 0:NH], mqrow[0:1, 0:NH], mkrow[0:1, 0:NH],
                                                 ALU.mult), sem="vD")
        C.wait("scalar", "vD", C.cnt["vD"])
        C.op("scalar", lambda e: e.activation(mtmp[0:1, 32:32 + NH], mtmp[0:1, 0:NH], ACT.Sqrt), sem="sD")
        C.wait("vector", "sD", C.cnt["sD"])
        C.op("vector", lambda e: e.tensor_scalar(mtmp[0:1, 0:NH], mtmp[0:1, 32:32 + NH], -SCALE, None,
                                                 ALU.mult), sem="vD")
        C.wait("tensor", "vD", C.cnt["vD"])
        C.op("tensor", lambda e: e.matmul(ps[4][:, 0:NH], onesrow[0:1, 0:128], mtmp[0:1, 0:NH],
                                          start=True, stop=True), sem="tD")
        C.wait("scalar", "tD", C.cnt["tD"])
        C.op("scalar", lambda e: e.activation(mbias[:, 0:NH], ps[4][:, 0:NH], ACT.Identity), sem="sD")

    def ph_attn(ctx, which):
        C = ctx
        if which == "a":
            groups = [[h] for h in range(NH)]
            KVLEN = cfg["S"]
        else:
            groups = [list(range(kvh * KVG, (kvh + 1) * KVG)) for kvh in range(NKV)]
            KVLEN = KV_B
        NTK = KVLEN // 128
        gmark = {}
        gdone = {-2: None, -1: None}
        plfree = {0: ("vD", 0), 1: ("vD", 0)}
        ctxfree = {0: ("vD", 0), 1: ("vD", 0)}
        bcast_free = ("vD", 0)
        expfree = {}
        ctxmm = {}

        for gi, grp in enumerate(groups):
            s = gi % 2
            kse, vse = "ld" + str(2 + s), "ld" + str(4 + s)
            if gi >= 2:
                C.wait("sync", "tD", gdone[gi - 2])
            if which == "a":
                h = grp[0]
                C.dma("sync", kse, kvK(s)[:, 0:KVLEN],
                      aga_k[:, h * 128:(h + 1) * 128, :].rearrange("r p t -> p r t"))
                C.dma("sync", vse, kvV(s)[:, 0:NTK * 128],
                      aga_v.rearrange("r (j p) f -> r j p f", p=128)[:, :, :, h * 128:(h + 1) * 128]
                      .rearrange("r j p f -> p r j f"))
            else:
                kvh = gi
                C.dma("sync", kse, kvK(s)[:, 0:CACHE_T],
                      agb2_k[kvh * RPK:(kvh + 1) * RPK].rearrange("r p t -> p r t") if RPK > 1
                      else agb2_k[kvh * RPK])
                C.dma("sync", kse, kvK(s)[:, CACHE_T:CACHE_T + NEW_T],
                      agb1_k[0:4, kvh * 128:(kvh + 1) * 128, :].rearrange("r p t -> p r t"))
                C.dma("sync", vse, kvV(s)[:, 0:(CACHE_T // 128) * 128],
                      agb2_v[kvh * RPK:(kvh + 1) * RPK].rearrange("r (j p) f -> p (r j) f", p=128))
                C.dma("sync", vse, kvV(s)[:, CACHE_T:CACHE_T + NEW_T],
                      agb1_v[0:4].rearrange("r (j p) f -> r j p f", p=128)[:, :, :, kvh * 128:(kvh + 1) * 128]
                      .rearrange("r j p f -> p r j f"))
            gmark[gi] = (C.cnt[kse], C.cnt[vse])
            for h in grp:
                for pan in range(NPAN):
                    qoff = h * T + pan * PANEL
                    C.wait("tensor", kse, gmark[gi][0])
                    C.wait("tensor", vse, gmark[gi][1])
                    for tau in range(NTK):
                        b = tau % 2
                        if (h == grp[0] and pan == 0) and tau < 2:
                            pass
                        if expfree.get((gi, b)) is not None:
                            C.wait("tensor", "sD", expfree[(gi, b)])
                        C.op("tensor", lambda e, b=b, tau=tau, qoff=qoff, s=s: e.matmul(
                            ps[b][:, 0:PANEL], kvK(s)[:, tau * 128:(tau + 1) * 128],
                            qt()[:, qoff:qoff + PANEL], start=True, stop=True), sem="tD")
                        smark = C.cnt["tD"]
                        C.wait("scalar", "tD", smark)
                        if ctxmm.get((gi, b)) is not None:
                            C.wait("scalar", "tD", ctxmm[(gi, b)])
                        C.op("scalar", lambda e, b=b, h=h: e.activation(
                            pt[b][:, 0:PANEL], ps[b][:, 0:PANEL], ACT.Exp,
                            bias=mbias[:, h:h + 1], scale=SCALE), sem="sD")
                        expfree[(gi, b)] = C.cnt["sD"]
                        C.wait("tensor", "sD", C.cnt["sD"])
                        if tau == 0:
                            C.wait("tensor", *plfree[pan])
                            C.wait("tensor", *ctxfree[pan])
                        C.op("tensor", lambda e, b=b, pan=pan, tau=tau: e.matmul(
                            pl[pan][0:1, 0:PANEL], onesb[:], pt[b][:, 0:PANEL],
                            start=(tau == 0), stop=(tau == NTK - 1)), sem="tD")
                        C.op("tensor", lambda e, b=b, pan=pan, tau=tau, s=s: e.matmul(
                            ps[2 + pan][:, 0:PANEL], kvV(s)[:, tau * 128:(tau + 1) * 128],
                            pt[b][:, 0:PANEL], start=(tau == 0), stop=(tau == NTK - 1)), sem="tD")
                        ctxmm[(gi, b)] = C.cnt["tD"]
                    lstop = C.cnt["tD"] - 1
                    C.wait("vector", "tD", C.cnt["tD"])
                    C.wait("vector", *bcast_free)
                    C.op("vector", lambda e, pan=pan: e.reciprocal(linv[0:1, 0:PANEL],
                                                                   pl[pan][0:1, 0:PANEL]), sem="vD")
                    plfree[pan] = ("vD", C.cnt["vD"])
                    C.wait("tensor", "vD", C.cnt["vD"])
                    C.op("tensor", lambda e: e.matmul(ps[4][:, 0:PANEL], onesrow[0:1, 0:128],
                                                      linv[0:1, 0:PANEL], start=True, stop=True), sem="tD")
                    C.wait("scalar", "tD", C.cnt["tD"])
                    C.op("scalar", lambda e: e.activation(ev[0][:, 0:PANEL], ps[4][:, 0:PANEL],
                                                          ACT.Identity), sem="sD")
                    C.wait("vector", "sD", C.cnt["sD"])
                    C.op("vector", lambda e, pan=pan, qoff=qoff: e.tensor_tensor(
                        qt()[:, qoff:qoff + PANEL], ps[2 + pan][:, 0:PANEL], ev[0][:, 0:PANEL],
                        ALU.mult), sem="vD")
                    ctxfree[pan] = ("vD", C.cnt["vD"])
                    bcast_free = ("vD", C.cnt["vD"])
            gdone[gi] = C.cnt["tD"]

    def ph_outproj(ctx, which, res_src, res_dst):
        C = ctx
        if which == "a":
            C.wait("sync", "cco", C.cnt["cco"])
            wsrc = g["agw_ow_o"]
            bias = True
        else:
            C.wait("sync", "ccwo", C.cnt["ccwo"])
            wsrc = g["agw_woT_o"]
            bias = False
        NFC = QF // 128
        NRND = NTT // TPP
        rsfree = {-2: ("tD", 0), -1: ("tD", 0)}
        xcfree = {-2: ("vD", 0), -1: ("vD", 0)}
        evfree = {-2: ("st", 0), -1: ("st", 0)}
        bankfree = {ii: ("vD", 0) for ii in range(TPP)}
        bankdone = {}
        it_r = 0
        it_x = 0
        for dp in range(DP):
            for rnd in range(NRND):
                for fc in range(NFC):
                    sr = it_r % 2
                    if it_r >= 2:
                        C.wait("sync", *rsfree[it_r - 2])
                    C.dma("sync", "ld" + str(2 + sr), rs[sr][:, 0:512],
                          wsrc[fc * 128:(fc + 1) * 128, dp * 512:(dp + 1) * 512])
                    ldmk = C.cnt["ld" + str(2 + sr)]
                    C.wait("tensor", "ld" + str(2 + sr), ldmk)
                    for ii in range(TPP):
                        i = rnd * TPP + ii
                        if fc == 0:
                            C.wait("tensor", *bankfree[ii])
                        C.op("tensor", lambda e, sr=sr, fc=fc, ii=ii, i=i: e.matmul(
                            ps[ii][:, 0:512], qt()[:, fc * T + i * 128:fc * T + (i + 1) * 128],
                            rs[sr][:, 0:512], start=(fc == 0), stop=(fc == NFC - 1 and not bias)),
                            sem="tD")
                    rsfree[it_r] = ("tD", C.cnt["tD"])
                    it_r += 1
                if bias:
                    for ii in range(TPP):
                        C.op("tensor", lambda e, ii=ii, dp=dp: e.matmul(
                            ps[ii][:, 0:512], onesrow_b[0:1, 0:128], obrow_b[0:1, dp * 512:(dp + 1) * 512],
                            start=False, stop=True), sem="tD")
                bankdone[(dp, rnd)] = C.cnt["tD"]
                for ii in range(TPP):
                    i = rnd * TPP + ii
                    sx = it_x % 2
                    if it_x >= 2:
                        C.wait("sync", *xcfree[it_x - 2])
                    C.dma("sync", "ld" + str(4 + sx), ev[2 + sx][:, 0:512],
                          res_src[i * 128:(i + 1) * 128, dp * 512:(dp + 1) * 512])
                    ldmk = C.cnt["ld" + str(4 + sx)]
                    C.wait("vector", "ld" + str(4 + sx), ldmk)
                    C.wait("vector", "tD", bankdone[(dp, rnd)])
                    C.wait("vector", *evfree[it_x - 2])
                    C.op("vector", lambda e, ii=ii, sx=sx: e.scalar_tensor_tensor(
                        ev[sx][:, 0:512], ps[ii][:, 0:512], 1.0, ev[2 + sx][:, 0:512],
                        ALU.mult, ALU.add), sem="vD")
                    xcfree[it_x] = ("vD", C.cnt["vD"])
                    bankfree[ii] = ("vD", C.cnt["vD"])
                    C.wait("sync", "vD", C.cnt["vD"])
                    C.dma("sync", "st", res_dst[i * 128:(i + 1) * 128, dp * 512:(dp + 1) * 512],
                          ev[sx][:, 0:512])
                    evfree[it_x] = ("st", C.cnt["st"])
                    it_x += 1

    def ph_ffn(ctx, res_src, res_dst):
        C = ctx
        C.wait("sync", "ccf", C.cnt["ccf"])
        C.wait("tensor", "ccf", C.cnt["ccf"])
        RPD = max(1, 512 // DQC)
        for pan in range(NPAN):
            ln_panel(C, res_src, pan)
            C.barrier()
            gfree = ("vD", 0)
            g_sD = 0
            u_vD = 0
            pt0free = ("tD", 0)
            for nt in range(NT_I):
                C.dma("sync", "ld0", wb[0][:, 0:KC * 128], lhsT_view(agw_g, nt))
                l0 = C.cnt["ld0"]
                C.dma("sync", "ld1", wb[1][:, 0:KC * 128], lhsT_view(agw_u, nt))
                l1 = C.cnt["ld1"]
                C.wait("tensor", "ld0", l0)
                if nt >= 1:
                    C.wait("tensor", "sD", g_sD)
                for kc in range(KC):
                    C.op("tensor", lambda e, kc=kc: e.matmul(
                        ps[0][:, 0:PANEL], wb[0][:, kc * 128:(kc + 1) * 128],
                        lnT()[:, kc * PANEL:(kc + 1) * PANEL],
                        start=(kc == 0), stop=(kc == KC - 1)), sem="tD")
                gmm = C.cnt["tD"]
                C.wait("tensor", "ld1", l1)
                if nt >= 1:
                    C.wait("tensor", "vD", u_vD)
                for kc in range(KC):
                    C.op("tensor", lambda e, kc=kc: e.matmul(
                        ps[1][:, 0:PANEL], wb[1][:, kc * 128:(kc + 1) * 128],
                        lnT()[:, kc * PANEL:(kc + 1) * PANEL],
                        start=(kc == 0), stop=(kc == KC - 1)), sem="tD")
                umm = C.cnt["tD"]
                C.wait("scalar", "tD", gmm)
                if nt >= 1:
                    C.wait("scalar", "vD", u_vD)
                C.op("scalar", lambda e: e.activation(pt[0][:, 0:PANEL], ps[0][:, 0:PANEL], ACT.Silu),
                     sem="sD")
                g_sD = C.cnt["sD"]
                C.wait("vector", "sD", g_sD)
                C.wait("vector", "tD", umm)
                C.op("vector", lambda e, nt=nt: e.tensor_tensor(
                    hT()[:, nt * PANEL:(nt + 1) * PANEL], pt[0][:, 0:PANEL], ps[1][:, 0:PANEL],
                    ALU.mult), sem="vD")
                u_vD = C.cnt["vD"]
                if nt >= 0:
                    C.wait("sync", "tD", umm)
            C.barrier()
            it_r = 0
            it_x = 0
            rsfree = {-2: ("tD", 0), -1: ("tD", 0)}
            xcfree = {-2: ("vD", 0), -1: ("vD", 0)}
            evfree = {-2: ("st", 0), -1: ("st", 0)}
            bankfree = {ii: ("vD", 0) for ii in range(TPP)}
            for dp in range(DP):
                for kc in range(NT_I):
                    sr = it_r % 2
                    if it_r >= 2:
                        C.wait("sync", *rsfree[it_r - 2])
                    r0 = dp * RPD if DQC <= 512 else dp * 512 // DQC
                    C.dma("sync", "ld" + str(2 + sr), rs[sr][:, 0:512],
                          agw_d[r0:r0 + RPD, kc * 128:(kc + 1) * 128, :].rearrange("r p cc -> p r cc"))
                    ldmk = C.cnt["ld" + str(2 + sr)]
                    C.wait("tensor", "ld" + str(2 + sr), ldmk)
                    for ii in range(TPP):
                        if kc == 0:
                            C.wait("tensor", *bankfree[ii])
                        C.op("tensor", lambda e, sr=sr, kc=kc, ii=ii: e.matmul(
                            ps[2 + ii][:, 0:512], hT()[:, kc * PANEL + ii * 128:kc * PANEL + (ii + 1) * 128],
                            rs[sr][:, 0:512], start=(kc == 0), stop=(kc == NT_I - 1)), sem="tD")
                    rsfree[it_r] = ("tD", C.cnt["tD"])
                    it_r += 1
                bankdone = C.cnt["tD"]
                for ii in range(TPP):
                    i = pan * TPP + ii
                    sx = it_x % 2
                    if it_x >= 2:
                        C.wait("sync", *xcfree[it_x - 2])
                    C.dma("sync", "ld" + str(4 + sx), ev[2 + sx][:, 0:512],
                          res_src[i * 128:(i + 1) * 128, dp * 512:(dp + 1) * 512])
                    ldmk = C.cnt["ld" + str(4 + sx)]
                    C.wait("vector", "ld" + str(4 + sx), ldmk)
                    C.wait("vector", "tD", bankdone)
                    C.wait("vector", *evfree[it_x - 2])
                    C.op("vector", lambda e, ii=ii, sx=sx: e.scalar_tensor_tensor(
                        ev[sx][:, 0:512], ps[2 + ii][:, 0:512], 1.0, ev[2 + sx][:, 0:512],
                        ALU.mult, ALU.add), sem="vD")
                    xcfree[it_x] = ("vD", C.cnt["vD"])
                    bankfree[ii] = ("vD", C.cnt["vD"])
                    C.wait("sync", "vD", C.cnt["vD"])
                    C.dma("sync", "st", res_dst[i * 128:(i + 1) * 128, dp * 512:(dp + 1) * 512],
                          ev[sx][:, 0:512])
                    evfree[it_x] = ("st", C.cnt["st"])
                    it_x += 1
            C.barrier()

    def ph_bproj(ctx, res_src):
        C = ctx
        C.op("vector", lambda e: e.memset(mqrow[:], -3e38))
        C.op("vector", lambda e: e.memset(mkrow[:], -3e38))
        C.drain("vector")
        C.wait("tensor", "ccw", C.cnt["ccw"])
        C.wait("sync", "ccw", C.cnt["ccw"])
        for pan in range(NPAN):
            ln_panel(C, res_src, pan)
            C.barrier()
            jobs = ([("k", agw_wkT, h) for h in range(NFT_KV)] + [("v", agw_wvT, h) for h in range(NFT_KV)]
                    + [("q", agw_wqT, h) for h in range(NFT_Q)])
            bank_free = {-2: ("sD", 0), -1: ("sD", 0)}
            ptfree = {-2: ("sD", 0), -1: ("sD", 0)}
            sqfree = [("sD", 0)]
            vtfree = [("sD", 0)]
            trfree = {0: ("sD", 0), 1: ("sD", 0)}
            nrmfree = [("sD", 0)]
            linvfree = [("vD", 0)]
            ropefree = [("sD", 0)]

            def epi(C2, bank, ftk, pan2, it):
                kind, _, h = jobs[it]
                C2.wait("scalar", "tD", C2.cnt["tD"])
                C2.wait("scalar", *ptfree[it - 2])
                sl = pt[it % 2]
                C2.op("scalar", lambda e, sl=sl: e.activation(sl[:, 0:PANEL], bank[:, 0:PANEL],
                                                              ACT.Identity), sem="sD")
                evk = C2.cnt["sD"]
                bank_free[it] = ("sD", evk)
                if kind in ("q", "k"):
                    C2.wait("tensor", "sD", evk)
                    C2.wait("tensor", *ropefree[0])
                    C2.op("tensor", lambda e, sl=sl: e.matmul(ps[4][:, 0:PANEL], rmat[:],
                                                              sl[:, 0:PANEL], start=True, stop=True), sem="tD")
                    rmm = C2.cnt["tD"]
                    C2.wait("scalar", "tD", rmm)
                    if kind == "q":
                        dst = qt()[:, h * T + pan2 * PANEL: h * T + pan2 * PANEL + PANEL]
                    else:
                        dst = pt3[:, 0:PANEL]
                        C2.wait("scalar", *vtfree[0])
                    C2.op("scalar", lambda e, dst=dst: e.activation(dst, ps[4][:, 0:PANEL], ACT.Identity),
                          sem="sD")
                    ropefree[0] = ("sD", C2.cnt["sD"])
                    revk = C2.cnt["sD"]
                    if kind == "k":
                        C2.wait("sync", "sD", revk)
                        C2.dma("sync", "st", g["agb1_k_i"][h * 128:(h + 1) * 128,
                                                           pan2 * PANEL:pan2 * PANEL + PANEL], pt3[:, 0:PANEL])
                        vtfree[0] = ("st", C2.cnt["st"])
                    C2.wait("vector", "sD", revk)
                    C2.wait("vector", *sqfree[0])
                    srcv = dst
                    C2.op("vector", lambda e, srcv=srcv: e.tensor_tensor(pt2[:, 0:PANEL], srcv, srcv,
                                                                         ALU.mult), sem="vD")
                    C2.wait("tensor", "vD", C2.cnt["vD"])
                    C2.wait("tensor", *nrmfree[0])
                    C2.op("tensor", lambda e: e.matmul(pl[0][0:1, 0:PANEL], onesb[:], pt2[:, 0:PANEL],
                                                       start=True, stop=True), sem="tD")
                    sqfree[0] = ("tD", C2.cnt["tD"])
                    C2.wait("scalar", "tD", C2.cnt["tD"])
                    C2.wait("scalar", *linvfree[0])
                    C2.op("scalar", lambda e: e.activation(
                        linv[0:1, 0:PANEL], pl[0][0:1, 0:PANEL], ACT.Identity), sem="sD")
                    nrmfree[0] = ("sD", C2.cnt["sD"])
                    C2.wait("vector", "sD", C2.cnt["sD"])
                    C2.op("vector", lambda e: e.tensor_reduce(
                        mtmp[0:1, 2:3], linv[0:1, 0:PANEL], mybir.AxisListType.X, ALU.max))
                    C2.drain("vector")
                    mrow_dst = mqrow if kind == "q" else mkrow
                    C2.op("vector", lambda e, h=h, mrow_dst=mrow_dst: e.tensor_tensor(
                        mrow_dst[0:1, h:h + 1], mrow_dst[0:1, h:h + 1], mtmp[0:1, 2:3], ALU.max),
                        sem="vD")
                    C2.drain("vector")
                    linvfree[0] = ("vD", C2.cnt["vD"])
                    ptfree[it] = ("tD", rmm)
                else:
                    lasttr = 0
                    for j in range(TPP):
                        C2.wait("tensor", "sD", evk)
                        C2.wait("tensor", *trfree[j % 2])
                        C2.op("tensor", lambda e, j=j, sl=sl: e.transpose(
                            psT(4 + j % 2), sl[:, j * 128:(j + 1) * 128], identb[:]), sem="tD")
                        lasttr = C2.cnt["tD"]
                        C2.wait("scalar", "tD", lasttr)
                        if j == 0:
                            C2.wait("scalar", *vtfree[0])
                        C2.op("scalar", lambda e, j=j: e.activation(
                            pt3[:, j * 128:(j + 1) * 128], psT(4 + j % 2), ACT.Identity), sem="sD")
                        trfree[j % 2] = ("sD", C2.cnt["sD"])
                    C2.wait("sync", "sD", C2.cnt["sD"])
                    C2.dma("sync", "st",
                           g["agb1_v_i"][pan2 * PANEL:(pan2 + 1) * PANEL, h * 128:(h + 1) * 128]
                           .rearrange("(j p) c -> p j c", p=128),
                           pt3[:, 0:PANEL].rearrange("p (j cc) -> p j cc", cc=128))
                    vtfree[0] = ("st", C2.cnt["st"])
                    ptfree[it] = ("tD", lasttr)

            gemm_panel(C, pan, [(v, h) for _, v, h in jobs], 0, epi, lambda it: bank_free[it])
            C.barrier()
        C.wait("sync", "vD", C.cnt["vD"])
        C.dma("sync", "st", g["agb1_nk_i"][0:1, 0:NKV], mkrow[0:1, 0:NKV])
        C.marks["st_agb1"] = C.cnt["st"]

    def ph_trigger_agb1(ctx):
        C = ctx
        C.wait("gpsimd", "st", C.marks["st_agb1"])
        C.collective("ccb1", g["agb1_k_i"], g["agb1_k_o"])
        C.collective("ccb1", g["agb1_v_i"], g["agb1_v_o"])
        C.collective("ccb1", g["agb1_nk_i"], g["agb1_nk_o"])

    def run_all(ctx):
        ph_prep(ctx)
        ctx.barrier()
        ph_qzsz(ctx)
        ctx.barrier()
        ph_wcast(ctx)
        ph_trigger1(ctx)
        ph_dequant(ctx)
        ctx.barrier()
        ph_gather(ctx)
        ctx.barrier()
        ph_trigger2(ctx)
        ph_lnqkv(ctx)
        ph_trigger_aga(ctx)
        ctx.barrier()
        mprep(ctx, "a")
        ctx.barrier()
        ph_attn(ctx, "a")
        ctx.barrier()
        ph_outproj(ctx, "a", x_in, out_ext if dbg_stop == 1 else res1)
        ctx.barrier()
        if dbg_stop == 1:
            return
        ph_ffn(ctx, res1, out_ext if dbg_stop == 2 else res2)
        if dbg_stop == 2:
            return
        ph_bproj(ctx, res2)
        ph_trigger_agb1(ctx)
        ctx.barrier()
        mprep(ctx, "b")
        ctx.barrier()
        ph_attn(ctx, "b")
        ctx.barrier()
        ph_outproj(ctx, "b", res2, out_ext if dbg_stop == 3 else res3)
        ctx.barrier()
        if dbg_stop == 3:
            return
        ph_ffn(ctx, res3, out_ext)

    with es:
        with nc.Block() as block:
            starters = dict(sync=block.sync, tensor=block.tensor, vector=block.vector,
                            scalar=block.scalar, gpsimd=block.gpsimd)
            for role in ROLES:
                def make(role):
                    def fn(eng):
                        ctx = Ctx(nc, cfg, sems, B, dbg_stop)
                        ctx.role = role
                        ctx.eng = eng
                        run_all(ctx)
                        ctx.barrier()
                    return fn
                starters[role](make(role))
    return nc


def kernel_generic(cfg, I, dbg_stop=None):
    in_maps = stage_inputs(cfg, I)
    nc = build_nc(cfg, dbg_stop=dbg_stop)
    res = run_bass_kernel_spmd(nc, in_maps, core_ids=list(range(NCORES)))
    c = derived(cfg)
    out = np.concatenate([np.asarray(r["out"]) for r in res.results], axis=0)
    return out.reshape(cfg["B"], cfg["S"], cfg["D"])



def kernel(L0_attn_a_x, L0_attn_a_attn_qkvw, L0_attn_a_attn_qkvb, L0_attn_a_attn_ow,
           L0_attn_a_attn_ob, L0_attn_b_freqs_cos, L0_attn_b_freqs_sin, L0_attn_b_wq,
           L0_attn_b_wk, L0_attn_b_wv, L0_attn_b_wo, L0_attn_b_k_cache, L0_attn_b_v_cache,
           L0_attn_b_block_idx, L0_ffn_gate_qweight, L0_ffn_gate_scales, L0_ffn_gate_qzeros,
           L0_ffn_gate_g_idx, L0_ffn_up_qweight, L0_ffn_up_scales, L0_ffn_up_qzeros,
           L0_ffn_up_g_idx, L0_ffn_down_qweight, L0_ffn_down_scales, L0_ffn_down_qzeros,
           L0_ffn_down_g_idx):
    I = dict(
        x=L0_attn_a_x, qkvw=L0_attn_a_attn_qkvw, qkvb=L0_attn_a_attn_qkvb,
        ow=L0_attn_a_attn_ow, ob=L0_attn_a_attn_ob,
        fcos=L0_attn_b_freqs_cos, fsin=L0_attn_b_freqs_sin,
        wq=L0_attn_b_wq, wk=L0_attn_b_wk, wv=L0_attn_b_wv, wo=L0_attn_b_wo,
        k_cache=L0_attn_b_k_cache, v_cache=L0_attn_b_v_cache, block_idx=L0_attn_b_block_idx,
        gq=L0_ffn_gate_qweight, gsc=L0_ffn_gate_scales, gqz=L0_ffn_gate_qzeros,
        g_gidx=L0_ffn_gate_g_idx,
        uq=L0_ffn_up_qweight, usc=L0_ffn_up_scales, uqz=L0_ffn_up_qzeros,
        u_gidx=L0_ffn_up_g_idx,
        dq=L0_ffn_down_qweight, dsc=L0_ffn_down_scales, dqz=L0_ffn_down_qzeros,
        d_gidx=L0_ffn_down_g_idx,
    )
    out = kernel_generic(full_cfg(), I)
    return out.astype(np.float32)


# version 02
# speedup vs baseline: 1.0542x; 1.0542x over previous
import math
import numpy as np
import ml_dtypes
import concourse.bass as bass
import concourse.mybir as mybir
from concourse.bass_utils import run_bass_kernel_spmd
from contextlib import ExitStack

F32 = mybir.dt.float32
BF16 = mybir.dt.bfloat16
I32 = mybir.dt.int32
U32 = mybir.dt.uint32
ACT = mybir.ActivationFunctionType
ALU = mybir.AluOpType
BF = ml_dtypes.bfloat16

NCORES = 8


def full_cfg():
    return dict(
        B=2, S=4096, D=4096, NH=32, HD=128, NKV=8,
        INTER=11008, GROUP=128,
        NBLK=128, BLOCK=256, TOPK=16,
    )


def mini_cfg():
    return dict(
        B=2, S=512, D=1024, NH=8, HD=128, NKV=4,
        INTER=2048, GROUP=128,
        NBLK=16, BLOCK=128, TOPK=4,
    )


def derived(cfg):
    c = dict(cfg)
    c["T"] = cfg["B"] * cfg["S"] // NCORES
    c["NTT"] = c["T"] // 128
    c["PANEL"] = min(512, c["T"])
    c["NPAN"] = c["T"] // c["PANEL"]
    c["TPP"] = c["PANEL"] // 128
    c["KC"] = cfg["D"] // 128
    c["KCS"] = c["KC"] // NCORES
    c["NT_I"] = cfg["INTER"] // 128
    c["KVG"] = cfg["NH"] // cfg["NKV"]
    c["R_PER_KVH"] = NCORES // cfg["NKV"]
    c["CACHE_T"] = cfg["TOPK"] * cfg["BLOCK"]
    c["NEW_T"] = cfg["S"]
    c["KV_B"] = c["CACHE_T"] + c["NEW_T"]
    c["DP"] = cfg["D"] // 512
    c["QF"] = cfg["NH"] * cfg["HD"]
    c["KVF"] = cfg["NKV"] * cfg["HD"]
    c["WG_KC"] = c["KCS"]
    c["DQ_COLS"] = cfg["D"] // NCORES
    c["GRP_I"] = cfg["INTER"] // cfg["GROUP"]
    c["GRP_D"] = cfg["D"] // cfg["GROUP"]
    assert c["T"] % 128 == 0 and c["KCS"] >= 1 and cfg["D"] % 512 == 0
    assert c["CACHE_T"] % (c["R_PER_KVH"] * 128) == 0
    return c



def ftmaj(w):
    K, N = w.shape
    return np.ascontiguousarray(
        w.reshape(K // 128, 128, N // 128, 128).transpose(2, 1, 0, 3).reshape(N // 128 * 128, K))


def stage_inputs(cfg, I):
    c = derived(cfg)
    D, NH, HD, NKV = cfg["D"], cfg["NH"], cfg["HD"], cfg["NKV"]
    T = c["T"]
    x = np.asarray(I["x"]).reshape(-1, D)
    qkvw = np.asarray(I["qkvw"])
    qkvb = np.asarray(I["qkvb"]).reshape(3 * NH, HD).T
    ow = np.asarray(I["ow"])
    ob = np.asarray(I["ob"]).reshape(D // 128, 128).T
    wqT = np.ascontiguousarray(np.asarray(I["wq"]).T)
    wkT = np.ascontiguousarray(np.asarray(I["wk"]).T)
    wvT = np.ascontiguousarray(np.asarray(I["wv"]).T)
    woT = np.ascontiguousarray(np.asarray(I["wo"]).T)
    kcache = np.asarray(I["k_cache"])
    vcache = np.asarray(I["v_cache"])
    blkidx = np.asarray(I["block_idx"]).reshape(1, -1)
    fcos = np.asarray(I["fcos"]).reshape(1, HD // 2)
    fsin = np.asarray(I["fsin"]).reshape(1, HD // 2)

    gq, gsc, gqz = np.asarray(I["gq"]), np.asarray(I["gsc"]), np.asarray(I["gqz"])
    uq, usc, uqz = np.asarray(I["uq"]), np.asarray(I["usc"]), np.asarray(I["uqz"])
    dq, dsc, dqz = np.asarray(I["dq"]), np.asarray(I["dsc"]), np.asarray(I["dqz"])

    j = (np.arange(128) % 8)
    mask_nib = (np.int64(15) * (np.int64(1) << (4 * j))).astype(np.int64).astype(np.uint32).view(np.int32)[:, None]
    pow16 = (16.0 ** (-j.astype(np.float64))).astype(np.float32)[:, None]
    identf = np.eye(128, dtype=np.float32)
    identb = np.eye(128, dtype=np.float32).astype(BF)
    mup = np.zeros((128, 128), np.float32)
    mdn = np.zeros((128, 128), np.float32)
    ii = np.arange(64)
    mup[2 * ii, 2 * ii + 1] = 1.0
    mdn[2 * ii + 1, 2 * ii] = 1.0

    KR = D // NCORES
    QR = c["QF"] // NCORES
    in_maps = []
    for r in range(NCORES):
        kvh = r // c["R_PER_KVH"]
        tpart = r % c["R_PER_KVH"]
        ct = c["CACHE_T"] // c["R_PER_KVH"]
        m = dict(
            x=np.ascontiguousarray(x[r * T:(r + 1) * T]),
            qkvw=np.ascontiguousarray(ftmaj(qkvw[r * KR:(r + 1) * KR])),
            qkvb=np.ascontiguousarray(qkvb.astype(np.float32)),
            ow=np.ascontiguousarray(ow[r * KR:(r + 1) * KR]),
            ob=np.ascontiguousarray(ob.astype(np.float32)),
            ob_row=np.ascontiguousarray(np.asarray(I["ob"]).reshape(1, D).astype(np.float32)),
            wqT=np.ascontiguousarray(ftmaj(wqT[r * KR:(r + 1) * KR])),
            wkT=np.ascontiguousarray(ftmaj(wkT[r * KR:(r + 1) * KR])),
            wvT=np.ascontiguousarray(ftmaj(wvT[r * KR:(r + 1) * KR])),
            woT=np.ascontiguousarray(woT[r * QR:(r + 1) * QR]),
            kcache=np.ascontiguousarray(kcache[:, :, kvh, :].reshape(-1, HD)),
            vcache=np.ascontiguousarray(vcache[:, :, kvh, :].reshape(-1, HD)),
            blkidx=np.ascontiguousarray(
                blkidx[:, (tpart * ct) // cfg["BLOCK"]:((tpart + 1) * ct) // cfg["BLOCK"]].astype(np.int32)),
            csin=np.ascontiguousarray(np.stack([np.repeat(fcos.reshape(-1), 2),
                                                np.repeat(fsin.reshape(-1), 2)], axis=1).astype(np.float32)),
            gq=np.ascontiguousarray(gq[r * (KR // 8):(r + 1) * (KR // 8)]),
            gsc=np.ascontiguousarray(gsc[np.asarray(I["g_gidx"]).reshape(-1)[0::128][r * c["KCS"]:(r + 1) * c["KCS"]]]),
            gqz=np.ascontiguousarray(gqz[np.asarray(I["g_gidx"]).reshape(-1)[0::128][r * c["KCS"]:(r + 1) * c["KCS"]]]),
            uq=np.ascontiguousarray(uq[r * (KR // 8):(r + 1) * (KR // 8)]),
            usc=np.ascontiguousarray(usc[np.asarray(I["u_gidx"]).reshape(-1)[0::128][r * c["KCS"]:(r + 1) * c["KCS"]]]),
            uqz=np.ascontiguousarray(uqz[np.asarray(I["u_gidx"]).reshape(-1)[0::128][r * c["KCS"]:(r + 1) * c["KCS"]]]),
            dq=np.ascontiguousarray(dq[:, r * c["DQ_COLS"]:(r + 1) * c["DQ_COLS"]]),
            dsc=np.ascontiguousarray(dsc[np.asarray(I["d_gidx"]).reshape(-1)[0::128]][:, r * c["DQ_COLS"]:(r + 1) * c["DQ_COLS"]]),
            dqz=np.ascontiguousarray(dqz[np.asarray(I["d_gidx"]).reshape(-1)[0::128]][:, r * (c["DQ_COLS"] // 8):(r + 1) * (c["DQ_COLS"] // 8)]),
            mask_nib=mask_nib, pow16=pow16, identf=identf, identb=identb,
            mup=mup, mdn=mdn,
        )
        in_maps.append(m)
    return in_maps



ROLES = ("sync", "tensor", "vector", "scalar", "gpsimd")


class Ctx:
    def __init__(self, nc, cfg, sems, bufs, dbg):
        self.nc = nc
        self.cfg = cfg
        self.sems = sems
        self.bufs = bufs
        self.cnt = {k: 0 for k in sems}
        self.role = None
        self.eng = None
        self.dbg = dbg
        self.marks = {}
        self.outst = {k: 0 for k in sems}

    def bump(self, name, by):
        self.cnt[name] += by
        return self.cnt[name]

    def wait(self, role, name, val=None):
        v = self.cnt[name] if val is None else val
        if self.role == role:
            self.eng.wait_ge(self.sems[name], v)

    def dma(self, role, sem, out, in_, slow=False, fence=True):
        if fence and self.outst[sem] > 0:
            self.wait(role, sem, self.cnt[sem])
            self.outst[sem] = 0
        self.bump(sem, 16)
        self.outst[sem] += 1
        if self.role == role:
            self.eng.dma_start(out=out, in_=in_, allow_slow_non_contiguous=slow).then_inc(self.sems[sem], 16)

    def idma(self, role, sem, out, in_dram, idx_ap, fence=True):
        if fence and self.outst[sem] > 0:
            self.wait(role, sem, self.cnt[sem])
            self.outst[sem] = 0
        self.bump(sem, 16)
        self.outst[sem] += 1
        if self.role == role:
            self.eng.indirect_dma_start(
                out=out, out_offset=None, in_=in_dram,
                in_offset=bass.IndirectOffsetOnAxis(ap=idx_ap, axis=0),
            ).then_inc(self.sems[sem], 16)

    def op(self, role, fn, sem=None):
        if sem is not None:
            self.bump(sem, 1)
        if self.role == role:
            ins = fn(self.eng)
            if sem is not None:
                ins.then_inc(self.sems[sem], 1)

    def drain(self, role):
        if self.role == role:
            self.eng.drain()

    DMA_SEMS_SYNC = ("ld", "st", "ld0", "ld1", "ld2", "ld3", "ld4", "ld5")

    def barrier(self):
        for s in self.DMA_SEMS_SYNC:
            if self.cnt[s]:
                self.wait("sync", s, self.cnt[s])
                self.outst[s] = 0
        if self.cnt["gst"]:
            self.wait("gpsimd", "gst", self.cnt["gst"])
            self.outst["gst"] = 0
        eng_sem = dict(sync="qD", tensor="tD", vector="vD", scalar="sD", gpsimd="gD")
        for r, sname in eng_sem.items():
            self.bump(sname, 1)
            if self.role == r:
                if r != "sync":
                    self.eng.drain()
                self.eng.sem_inc(self.sems[sname], 1)
        for r in ROLES:
            for r2, sname in eng_sem.items():
                if r2 != r:
                    self.wait(r, sname, self.cnt[sname])

    def collective(self, sem, in_ap, out_ap, groups=None):
        self.bump(sem, 1)
        if self.role == "gpsimd":
            rg = [list(range(NCORES))] if groups is None else groups
            self.eng.collective_compute(
                "AllGather", ALU.bypass, replica_groups=rg,
                ins=[in_ap.opt()], outs=[out_ap.opt()],
            ).then_inc(self.sems[sem], 1)



def build_nc(cfg, dbg_stop=None):
    c = derived(cfg)
    D, NH, HD, NKV, INTER = cfg["D"], cfg["NH"], cfg["HD"], cfg["NKV"], cfg["INTER"]
    T, PANEL, NPAN, TPP, KC, KCS = c["T"], c["PANEL"], c["NPAN"], c["TPP"], c["KC"], c["KCS"]
    NTT, NT_I, QF, KVF = c["NTT"], c["NT_I"], c["QF"], c["KVF"]
    KVG, RPK = c["KVG"], c["R_PER_KVH"]
    CT = c["CACHE_T"] // RPK
    CACHE_T, NEW_T, KV_B = c["CACHE_T"], c["NEW_T"], c["KV_B"]
    NFT_QKV = 3 * D // 128
    NFT_Q, NFT_KV = QF // 128, KVF // 128
    DP = D // 512
    KR = D // NCORES
    QR = QF // NCORES
    DQC = c["DQ_COLS"]
    GRP_I = c["GRP_I"]
    SCALE = 1.0 / math.sqrt(HD)

    nc = bass.Bass("TRN2", target_bir_lowering=False, debug=False)

    def din(name, shape, dt=F32):
        return nc.dram_tensor(name, shape, dt, kind="ExternalInput").ap()

    x_in = din("x", [T, D])
    qkvw_in = din("qkvw", [NFT_QKV * 128, KCS * 128])
    qkvb_in = din("qkvb", [128, 3 * NH])
    ow_in = din("ow", [KR, D])
    ob_in = din("ob", [128, D // 128])
    obrow_in = din("ob_row", [1, D])
    wqT_in = din("wqT", [NFT_Q * 128, KCS * 128])
    wkT_in = din("wkT", [NFT_KV * 128, KCS * 128])
    wvT_in = din("wvT", [NFT_KV * 128, KCS * 128])
    woT_in = din("woT", [QR, D])
    kcache_in = din("kcache", [cfg["NBLK"] * cfg["BLOCK"], HD])
    vcache_in = din("vcache", [cfg["NBLK"] * cfg["BLOCK"], HD])
    blkidx_in = din("blkidx", [1, CT // cfg["BLOCK"]], I32)
    csin_in = din("csin", [128, 2])
    gq_in = din("gq", [KR // 8, INTER], I32)
    gsc_in = din("gsc", [KCS, INTER])
    gqz_in = din("gqz", [KCS, INTER // 8], I32)
    uq_in = din("uq", [KR // 8, INTER], I32)
    usc_in = din("usc", [KCS, INTER])
    uqz_in = din("uqz", [KCS, INTER // 8], I32)
    dq_in = din("dq", [INTER // 8, DQC], I32)
    dsc_in = din("dsc", [GRP_I, DQC])
    dqz_in = din("dqz", [GRP_I, DQC // 8], I32)
    mask_in = din("mask_nib", [128, 1], I32)
    pow16_in = din("pow16", [128, 1])
    identf_in = din("identf", [128, 128])
    identb_in = din("identb", [128, 128], BF16)
    mup_in = din("mup", [128, 128])
    mdn_in = din("mdn", [128, 128])

    out_ext = nc.dram_tensor("out", [T, D], F32, kind="ExternalOutput").ap()

    def dram(name, shape, dt):
        return nc.dram_tensor(name, shape, dt).ap()

    res1 = dram("res1", [T, D], F32)
    res2 = dram("res2", [T, D], F32)
    res3 = dram("res3", [T, D], F32)
    agw_qkvw_i = dram("agw_qkvw_i", [NFT_QKV * 128, KCS * 128], BF16)
    agw_qkvw_o = dram("agw_qkvw_o", [NCORES * NFT_QKV * 128, KCS * 128], BF16)
    agw_wqT_i = dram("agw_wqT_i", [NFT_Q * 128, KCS * 128], BF16)
    agw_wqT_o = dram("agw_wqT_o", [NCORES * NFT_Q * 128, KCS * 128], BF16)
    agw_wkT_i = dram("agw_wkT_i", [NFT_KV * 128, KCS * 128], BF16)
    agw_wkT_o = dram("agw_wkT_o", [NCORES * NFT_KV * 128, KCS * 128], BF16)
    agw_wvT_i = dram("agw_wvT_i", [NFT_KV * 128, KCS * 128], BF16)
    agw_wvT_o = dram("agw_wvT_o", [NCORES * NFT_KV * 128, KCS * 128], BF16)
    agw_ow_i = dram("agw_ow_i", [KR, D], BF16)
    agw_ow_o = dram("agw_ow_o", [D, D], BF16)
    agw_woT_i = dram("agw_woT_i", [QR, D], BF16)
    agw_woT_o = dram("agw_woT_o", [QF, D], BF16)
    agw_g_i = dram("agw_g_i", [NT_I * 128, KCS * 128], BF16)
    agw_g_o = dram("agw_g_o", [NCORES * NT_I * 128, KCS * 128], BF16)
    agw_u_i = dram("agw_u_i", [NT_I * 128, KCS * 128], BF16)
    agw_u_o = dram("agw_u_o", [NCORES * NT_I * 128, KCS * 128], BF16)
    agw_d_i = dram("agw_d_i", [INTER, DQC], BF16)
    agw_d_o = dram("agw_d_o", [NCORES * INTER, DQC], BF16)
    aga_k_i = dram("aga_k_i", [D, T], BF16)
    aga_k_o = dram("aga_k_o", [4 * D, T], BF16)
    aga_v_i = dram("aga_v_i", [T, D], BF16)
    aga_v_o = dram("aga_v_o", [4 * T, D], BF16)
    aga_nk_i = dram("aga_nk_i", [1, NH], F32)
    aga_nk_o = dram("aga_nk_o", [4 * 1, NH], F32)
    agb1_k_i = dram("agb1_k_i", [KVF, T], BF16)
    agb1_k_o = dram("agb1_k_o", [NCORES * KVF, T], BF16)
    agb1_v_i = dram("agb1_v_i", [T, KVF], BF16)
    agb1_v_o = dram("agb1_v_o", [NCORES * T, KVF], BF16)
    agb1_nk_i = dram("agb1_nk_i", [1, NKV], F32)
    agb1_nk_o = dram("agb1_nk_o", [NCORES * 1, NKV], F32)
    agb2_k_i = dram("agb2_k_i", [HD, CT], BF16)
    agb2_k_o = dram("agb2_k_o", [NCORES * HD, CT], BF16)
    agb2_v_i = dram("agb2_v_i", [CT, HD], BF16)
    agb2_v_o = dram("agb2_v_o", [NCORES * CT, HD], BF16)
    agb2_nk_i = dram("agb2_nk_i", [1, 1], F32)
    agb2_nk_o = dram("agb2_nk_o", [NCORES * 1, 1], F32)
    szg_d = dram("szg_d", [KCS, INTER], F32)
    szu_d = dram("szu_d", [KCS, INTER], F32)
    szd_d = dram("szd_d", [GRP_I, DQC], F32)

    es = ExitStack()
    sb = lambda name, shape, dt: es.enter_context(nc.sbuf_tensor(name, shape, dt))
    ps_ = lambda name, shape: es.enter_context(nc.psum_tensor(name, shape, F32))

    POOLB = 161 * 1024
    pool = sb("pool", [128, POOLB], mybir.dt.uint8)

    def carve(off, dt, shape):
        nbytes = shape[1] * {F32: 4, BF16: 2, I32: 4, U32: 4}[dt]
        assert off % 64 == 0 and off + nbytes <= POOLB, (off, nbytes)
        return pool[0:shape[0], off:off + nbytes].bitcast(dt)

    wb = [sb("wb0", [128, KC * 128], BF16), sb("wb1", [128, KC * 128], BF16)]
    rs = [sb("rs0", [128, 512], BF16), sb("rs1", [128, 512], BF16)]
    pt = [sb("pt0", [128, 512], BF16), sb("pt1", [128, 512], BF16)]
    pt2 = sb("pt2", [128, 512], BF16)
    pt3 = sb("pt3", [128, 512], BF16)
    obrow_b = sb("obrow_b", [1, cfg["D"]], BF16)
    onesrow_b = sb("onesrow_b", [1, 128], BF16)
    ev = [sb("ev0", [128, 512], F32), sb("ev1", [128, 512], F32),
          sb("ev2", [128, 512], F32), sb("ev3", [128, 512], F32)]
    identf = sb("identf_s", [128, 128], F32)
    identb = sb("identb_s", [128, 128], BF16)
    mupt = sb("mup_s", [128, 128], F32)
    mdnt = sb("mdn_s", [128, 128], F32)
    rmat = sb("rmat", [128, 128], BF16)
    rmatf = sb("rmatf", [128, 128], F32)
    qkvb_t = sb("qkvb_s", [128, 3 * NH], F32)
    ob_t = sb("ob_s", [128, D // 128], F32)
    maskt = sb("maskt", [128, 1], I32)
    pow16t = sb("pow16t", [128, 1], F32)
    cs_t = sb("cs_t", [128, 2], F32)
    onesb = sb("onesb", [128, 1], BF16)
    onesf = sb("onesf", [128, 1], F32)
    mqrow = sb("mqrow", [1, 64], F32)
    mkrow = sb("mkrow", [1, 64], F32)
    mtmp = sb("mtmp", [1, 256], F32)
    mbias = sb("mbias", [128, NH], F32)
    lnsc = sb("lnsc", [128, 8], F32)
    gidx = sb("gidx", [128, CT // 128], U32)
    gidxf = sb("gidxf", [128, CT // 128], F32)
    iotap = sb("iotap", [128, 1], I32)
    onesrow = sb("onesrow", [1, 128], F32)
    blki = sb("blki", [1, CT // cfg["BLOCK"]], F32)
    blkii = sb("blkii", [1, CT // cfg["BLOCK"]], I32)
    blkb = sb("blkb", [128, CT // cfg["BLOCK"]], F32)

    linv = sb("linv", [128, 512], F32)

    ps = [ps_(f"ps{i}", [128, 512]) for i in range(6)]
    pl = [ps_(f"pl{i}", [1, 512]) for i in range(2)]

    sems = {}
    for s in ("ld", "st", "gst", "tD", "vD", "sD", "gD", "qD",
              "ld0", "ld1", "ld2", "ld3", "ld4", "ld5",
              "ccq", "cco", "cca", "ccf", "ccw", "ccb2", "ccwo", "ccb1"):
        sems[s] = es.enter_context(nc.semaphore(s))

    bufs = dict(locals())
    return _emit(nc, c, es, bufs, sems, dbg_stop)


def _emit(nc, c, es, B, sems, dbg_stop):
    cfg = c
    D, NH, HD, NKV, INTER = c["D"], c["NH"], c["HD"], c["NKV"], c["INTER"]
    T, PANEL, NPAN, TPP, KC, KCS = c["T"], c["PANEL"], c["NPAN"], c["TPP"], c["KC"], c["KCS"]
    NTT, NT_I, QF, KVF = c["NTT"], c["NT_I"], c["QF"], c["KVF"]
    KVG, RPK = c["KVG"], c["R_PER_KVH"]
    CT = c["CACHE_T"] // RPK
    CACHE_T, NEW_T, KV_B = c["CACHE_T"], c["NEW_T"], c["KV_B"]
    BLOCK = c["BLOCK"]
    NFT_QKV, NFT_Q, NFT_KV = 3 * D // 128, QF // 128, KVF // 128
    DP = D // 512
    KR, QR, DQC, GRP_I = D // NCORES, QF // NCORES, c["DQ_COLS"], c["GRP_I"]
    SCALE = 1.0 / math.sqrt(HD)
    NPP = PANEL

    g = B
    x_in, out_ext = g["x_in"], g["out_ext"]
    res1, res2, res3 = g["res1"], g["res2"], g["res3"]
    pool, wb, rs, pt, ev = g["pool"], g["wb"], g["rs"], g["pt"], g["ev"]
    identf, identb, mupt, mdnt, rmat, rmatf = g["identf"], g["identb"], g["mupt"], g["mdnt"], g["rmat"], g["rmatf"]
    qkvb_t, ob_t, maskt, pow16t, cs_t = g["qkvb_t"], g["ob_t"], g["maskt"], g["pow16t"], g["cs_t"]
    onesb, onesf, mbias = g["onesb"], g["onesf"], g["mbias"]
    mqrow, mkrow, mtmp = g["mqrow"], g["mkrow"], g["mtmp"]
    lnsc, gidx, blki, blkb, linv = g["lnsc"], g["gidx"], g["blki"], g["blkb"], g["linv"]
    gidxf, iotap, onesrow, blkii = g["gidxf"], g["iotap"], g["onesrow"], g["blkii"]
    pt2, pt3, obrow_b, onesrow_b = g["pt2"], g["pt3"], g["obrow_b"], g["onesrow_b"]
    ps, pl = g["ps"], g["pl"]

    LNT_OFF, QT_OFF, KV_OFF, XT_OFF, XNB_OFF = 0, 32 * 1024, 96 * 1024, 121 * 1024, 153 * 1024
    def lnT():
        return pool[0:128, LNT_OFF:LNT_OFF + KC * PANEL * 2].bitcast(BF16)
    def qt():
        return pool[0:128, QT_OFF:QT_OFF + NH * T * 2].bitcast(BF16)
    def hT():
        return pool[0:128, QT_OFF:QT_OFF + NT_I * PANEL * 2].bitcast(BF16)
    def kvK(s):
        off = KV_OFF + s * KV_B * 2
        return pool[0:128, off:off + KV_B * 2].bitcast(BF16)
    def kvV(s):
        off = KV_OFF + (2 + s) * ((KV_B // 128) * 128 * 2)
        return pool[0:128, off:off + (KV_B // 128) * 128 * 2].bitcast(BF16)
    def xt(s):
        off = XT_OFF + s * D * 4
        return pool[0:128, off:off + D * 4].bitcast(F32)
    def xnb():
        return pool[0:128, XNB_OFF:XNB_OFF + D * 2].bitcast(BF16)
    DQN = min(2816, INTER)
    DQNB = DQN * 4
    def dq_q():
        return pool[0:128, 0:DQNB].bitcast(I32)
    def dq_s():
        return pool[0:128, DQNB:2 * DQNB].bitcast(F32)
    def dq_z():
        return pool[0:128, 2 * DQNB:3 * DQNB].bitcast(F32)
    def dq_o():
        return pool[0:128, 3 * DQNB:3 * DQNB + DQN * 2].bitcast(BF16)
    def dqf():
        return pool[0:128, 3 * DQNB + DQN * 2:4 * DQNB + DQN * 2].bitcast(F32)
    GOFF = 4 * DQNB + DQN * 2
    def gk(s):
        off = GOFF + s * HD * 4
        return pool[0:128, off:off + HD * 4].bitcast(F32)
    def gkv(s):
        off = GOFF + (2 + s) * HD * 4
        return pool[0:128, off:off + HD * 4].bitcast(F32)
    def gkb(s):
        off = GOFF + 4 * HD * 4 + s * HD * 2
        return pool[0:128, off:off + HD * 2].bitcast(BF16)
    def gvb(s):
        off = GOFF + 4 * HD * 4 + (2 + s) * HD * 2
        return pool[0:128, off:off + HD * 2].bitcast(BF16)
    def gsq(s):
        off = GOFF + 4 * HD * 4 + 4 * HD * 2 + s * HD * 2
        return pool[0:128, off:off + HD * 2].bitcast(BF16)

    agw_qkvw = g["agw_qkvw_o"].rearrange("(r f p) k -> r f p k", r=NCORES, p=128)
    agw_wqT = g["agw_wqT_o"].rearrange("(r f p) k -> r f p k", r=NCORES, p=128)
    agw_wkT = g["agw_wkT_o"].rearrange("(r f p) k -> r f p k", r=NCORES, p=128)
    agw_wvT = g["agw_wvT_o"].rearrange("(r f p) k -> r f p k", r=NCORES, p=128)
    agw_g = g["agw_g_o"].rearrange("(r f p) k -> r f p k", r=NCORES, p=128)
    agw_u = g["agw_u_o"].rearrange("(r f p) k -> r f p k", r=NCORES, p=128)
    agw_d = g["agw_d_o"].rearrange("(r i) d -> r i d", r=NCORES)
    aga_k = g["aga_k_o"].rearrange("(r f) t -> r f t", r=4)
    aga_v = g["aga_v_o"].rearrange("(r t) f -> r t f", r=4)
    aga_nk = g["aga_nk_o"]
    agb1_k = g["agb1_k_o"].rearrange("(r f) t -> r f t", r=NCORES)
    agb1_v = g["agb1_v_o"].rearrange("(r t) f -> r t f", r=NCORES)
    agb1_nk = g["agb1_nk_o"]
    agb2_k = g["agb2_k_o"].rearrange("(r f) t -> r f t", r=NCORES)
    agb2_v = g["agb2_v_o"].rearrange("(r t) f -> r t f", r=NCORES)
    agb2_nk = g["agb2_nk_o"]

    def lhsT_view(agw_view, ft):
        return agw_view[:, ft].rearrange("r p k -> p r k")

    RES = [x_in, res1, res2, res3, out_ext]

    def psT(i):
        return ps[i][:, 0:64].bitcast(BF16)

    def ph_prep(ctx):
        C = ctx
        C.dma("sync", "ld", identf[:], g["identf_in"])
        C.dma("sync", "ld", identb[:], g["identb_in"], fence=False)
        C.dma("sync", "ld", mupt[:], g["mup_in"], fence=False)
        C.dma("sync", "ld", mdnt[:], g["mdn_in"], fence=False)
        C.dma("sync", "ld", qkvb_t[:], g["qkvb_in"], fence=False)
        C.dma("sync", "ld", ob_t[:], g["ob_in"], fence=False)
        C.dma("sync", "ld", maskt[:], g["mask_in"], fence=False)
        C.dma("sync", "ld", pow16t[:], g["pow16_in"], fence=False)
        C.dma("sync", "ld", cs_t[:], g["csin_in"], fence=False)
        C.dma("sync", "ld", blkii[:], g["blkidx_in"], fence=False)
        C.dma("gpsimd", "gst", obrow_b[0:1, 0:D], g["obrow_in"])
        ld_small = C.cnt["ld"]

        C.wait("vector", "ld", ld_small)
        C.op("vector", lambda e: e.memset(onesf[:], 1.0))
        C.op("vector", lambda e: e.memset(onesb[:], 1.0))
        C.op("vector", lambda e: e.tensor_scalar(rmatf[:], identf[:], cs_t[:, 0:1], None, ALU.mult))
        C.op("vector", lambda e: e.tensor_scalar(mupt[:], mupt[:], cs_t[:, 1:2], None, ALU.mult))
        C.op("vector", lambda e: e.tensor_scalar(mdnt[:], mdnt[:], cs_t[:, 1:2], None, ALU.mult))
        C.drain("vector")
        C.op("vector", lambda e: e.tensor_tensor(rmatf[:], rmatf[:], mupt[:], ALU.add))
        C.drain("vector")
        C.op("vector", lambda e: e.tensor_tensor(rmatf[:], rmatf[:], mdnt[:], ALU.subtract), sem="vD")
        vD_rmat = C.cnt["vD"]
        C.wait("scalar", "vD", vD_rmat)
        C.op("scalar", lambda e: e.copy(rmat[:], rmatf[:]), sem="sD")

        C.op("vector", lambda e: e.memset(onesrow[:], 1.0))
        C.op("vector", lambda e: e.memset(onesrow_b[:], 1.0))
        C.op("vector", lambda e: e.tensor_copy(blki[:], blkii[:]), sem="vD")
        vD_ones = C.cnt["vD"]

        TPB = BLOCK // 128
        NTAU = CT // 128
        C.wait("tensor", "ld", ld_small)
        C.wait("tensor", "vD", vD_ones)
        C.op("tensor", lambda e: e.matmul(ps[5][:, 0:CT // BLOCK], onesrow[:],
                                          blki[:], start=True, stop=True), sem="tD")
        tD_blk = C.cnt["tD"]
        C.wait("vector", "tD", tD_blk)
        C.op("vector", lambda e: e.tensor_scalar(blkb[:], ps[5][:, 0:CT // BLOCK], float(BLOCK), None, ALU.mult))
        C.op("gpsimd", lambda e: e.iota(iotap[:], pattern=[[0, 1]], base=0,
                                        channel_multiplier=1), sem="gD")
        gD_iota = C.cnt["gD"]
        C.wait("vector", "gD", gD_iota)
        C.drain("vector")
        for tau in range(NTAU):
            sel = tau // TPB
            off = (tau % TPB) * 128
            C.op("vector", lambda e, tau=tau, sel=sel, off=off: e.scalar_tensor_tensor(
                gidxf[:, tau:tau + 1], iotap[:], float(off),
                blkb[:, sel:sel + 1], ALU.add, ALU.add))
        C.drain("vector")
        C.op("vector", lambda e: e.tensor_copy(gidx[:, 0:NTAU], gidxf[:, 0:NTAU]), sem="vD")
        vD_gidx = C.cnt["vD"]
        C.marks.update(ld_small=ld_small, vD_gidx=vD_gidx)

    def ph_gather(ctx):
        C = ctx
        NTAU = CT // 128
        C.op("vector", lambda e: e.memset(mtmp[0:1, 0:1], -3e38))
        C.drain("vector")
        C.wait("gpsimd", "vD", C.marks["vD_gidx"])
        tD0, sD0, vD0, st0 = C.cnt["tD"], C.cnt["sD"], C.cnt["vD"], C.cnt["st"]
        for tau in range(NTAU):
            s = tau % 2
            grp, gi = tau // 4, tau % 4
            if tau >= 2:
                C.wait("gpsimd", "tD", tD0 + 2 * (tau - 2) + 1)
                C.wait("gpsimd", "sD", sD0 + 2 * (tau - 2) + 2 + (tau - 2) // 4)
            C.idma("gpsimd", "gst", gk(s)[:, 0:HD], g["kcache_in"], gidx[:, tau:tau + 1])
            C.idma("gpsimd", "gst", gkv(s)[:, 0:HD], g["vcache_in"], gidx[:, tau:tau + 1], fence=False)
            gk_mark = C.cnt["gst"]
            C.wait("tensor", "gst", gk_mark)
            if tau >= 2:
                C.wait("tensor", "sD", sD0 + 2 * (tau - 2) + 1 + (tau - 2) // 4)
                C.wait("tensor", "st", st0 + 32 * (tau - 1))
            C.op("tensor", lambda e, s=s: e.transpose(ps[4 + s][:, 0:128], gk(s)[:, 0:HD], identf[:]),
                 sem="tD")
            tr_mark = C.cnt["tD"]
            C.wait("scalar", "tD", tr_mark)
            if tau >= 2:
                C.wait("scalar", "st", st0 + 32 * (tau - 1))
            C.op("scalar", lambda e, s=s: e.activation(gkb(s)[:, 0:HD], ps[4 + s][:, 0:128], ACT.Identity),
                 sem="sD")
            kevac = C.cnt["sD"]
            C.wait("scalar", "gst", gk_mark)
            C.op("scalar", lambda e, s=s: e.activation(gvb(s)[:, 0:HD], gkv(s)[:, 0:HD], ACT.Identity), sem="sD")
            vevac = C.cnt["sD"]
            C.wait("vector", "sD", kevac)
            if tau >= 2:
                C.wait("vector", "tD", tD0 + 2 * (tau - 2) + 2)
            C.op("vector", lambda e, s=s: e.tensor_tensor(gsq(s)[:, 0:128], gkb(s)[:, 0:HD], gkb(s)[:, 0:HD],
                                                          ALU.mult), sem="vD")
            C.wait("tensor", "vD", C.cnt["vD"])
            if gi == 0 and grp > 0:
                C.wait("tensor", "sD", sD0 + 2 * tau + grp)
            C.op("tensor", lambda e, s=s, gi=gi: e.matmul(pl[0][0:1, gi * 128:(gi + 1) * 128], onesb[:],
                                                          gsq(s)[:, 0:128], start=True, stop=True), sem="tD")
            C.wait("sync", "sD", kevac)
            C.dma("sync", "st", g["agb2_k_i"][0:HD, tau * 128:(tau + 1) * 128], gkb(s)[:, 0:HD])
            C.wait("sync", "sD", vevac)
            C.dma("sync", "st", g["agb2_v_i"][tau * 128:(tau + 1) * 128, 0:HD], gvb(s)[:, 0:HD])
            if gi == 3 or tau == NTAU - 1:
                C.wait("scalar", "tD", C.cnt["tD"])
                C.op("scalar", lambda e, gi=gi: e.activation(
                    linv[0:1, 0:(gi + 1) * 128], pl[0][0:1, 0:(gi + 1) * 128], ACT.Identity), sem="sD")
                C.wait("vector", "sD", C.cnt["sD"])
                C.op("vector", lambda e, gi=gi: e.tensor_reduce(
                    mtmp[0:1, 1:2], linv[0:1, 0:(gi + 1) * 128], mybir.AxisListType.X, ALU.max))
                C.drain("vector")
                C.op("vector", lambda e: e.tensor_tensor(mtmp[0:1, 0:1], mtmp[0:1, 0:1], mtmp[0:1, 1:2],
                                                         ALU.max), sem="vD")
                C.drain("vector")
        C.wait("sync", "vD", C.cnt["vD"])
        C.dma("sync", "st", g["agb2_nk_i"][0:1, 0:1], mtmp[0:1, 0:1])
        C.marks["st_agb2"] = C.cnt["st"]

    ZW = min(5504, max(INTER, DQC))
    Z0 = GOFF + 4 * 1024
    def zrow(R):
        return pool[0:R, Z0:Z0 + ZW * 4].bitcast(F32)
    def scrow(R):
        return pool[0:R, Z0 + ZW * 4:Z0 + 2 * ZW * 4].bitcast(F32)
    def qzt(R):
        return pool[0:R, Z0 + 2 * ZW * 4:Z0 + 2 * ZW * 4 + (ZW // 8) * 4].bitcast(I32)
    def zqi2(R):
        return pool[0:R, Z0 + 2 * ZW * 4 + 704 * 4:Z0 + 2 * ZW * 4 + 1408 * 4].bitcast(I32)
    def castb(s):
        off = Z0 + 2 * ZW * 4 + 1408 * 4 + s * 8 * 1024
        return pool[0:128, off:off + 8 * 1024].bitcast(BF16)

    def ph_wcast(ctx):
        C = ctx
        jobs = []
        for src, dst, npiece, width in (
            (g["qkvw_in"], g["agw_qkvw_i"], NFT_QKV, KCS * 128),
            (g["ow_in"], g["agw_ow_i"], KR // 128, D),
            (g["wqT_in"], g["agw_wqT_i"], NFT_Q, KCS * 128),
            (g["wkT_in"], g["agw_wkT_i"], NFT_KV, KCS * 128),
            (g["wvT_in"], g["agw_wvT_i"], NFT_KV, KCS * 128),
            (g["woT_in"], g["agw_woT_i"], QR // 128, D),
        ):
            for p in range(npiece):
                jobs.append((src[p * 128:(p + 1) * 128, 0:width], dst[p * 128:(p + 1) * 128, 0:width], width))
        sD0, st0 = C.cnt["sD"], C.cnt["st"]
        ldm = []
        for it, (srcap, dstap, width) in enumerate(jobs):
            s = it % 2
            if it >= 2:
                C.wait("sync", "sD", sD0 + it - 1)
            C.dma("sync", "ld" + str(s), xt(s)[:, 0:width], srcap)
            ldm.append(C.cnt["ld" + str(s)])
            C.wait("scalar", "ld" + str(s), ldm[it])
            if it >= 2:
                C.wait("scalar", "st", st0 + 16 * (it - 1))
            C.op("scalar", lambda e, s=s, w=width: e.activation(castb(s)[:, 0:w], xt(s)[:, 0:w], ACT.Identity),
                 sem="sD")
            C.wait("sync", "sD", C.cnt["sD"])
            C.dma("sync", "st", dstap, castb(s)[:, 0:width])
        C.marks["st_wcast"] = C.cnt["st"]
        n1 = NFT_QKV
        n2 = n1 + KR // 128
        C.marks["st_qkvw"] = st0 + 16 * n1
        C.marks["st_ow"] = st0 + 16 * n2

    def ph_qzsz(ctx):
        C = ctx
        for (qz_in, sc_in, sz_d, R, N) in (
            (g["gqz_in"], g["gsc_in"], g["szg_d"], KCS, INTER),
            (g["uqz_in"], g["usc_in"], g["szu_d"], KCS, INTER),
            (g["dqz_in"], g["dsc_in"], g["szd_d"], GRP_I, DQC),
        ):
            for c0 in range(0, N, ZW):
                w = min(ZW, N - c0)
                C.dma("sync", "ld", qzt(R)[0:R, 0:w // 8], qz_in[0:R, c0 // 8:(c0 + w) // 8])
                C.dma("sync", "ld", scrow(R)[0:R, 0:w], sc_in[0:R, c0:c0 + w])
                C.wait("vector", "ld", C.cnt["ld"])
                C.wait("vector", "st", C.cnt["st"])
                for jj in range(8):
                    C.op("vector", lambda e, jj=jj, R=R, w=w: e.tensor_scalar(
                        zqi2(R)[0:R, 0:w // 8], qzt(R)[0:R, 0:w // 8], 4 * jj, None, ALU.logical_shift_right))
                    C.drain("vector")
                    C.op("vector", lambda e, R=R, w=w: e.tensor_scalar(
                        zqi2(R)[0:R, 0:w // 8], zqi2(R)[0:R, 0:w // 8], 15, None, ALU.bitwise_and))
                    C.drain("vector")
                    C.op("vector", lambda e, jj=jj, R=R, w=w: e.tensor_copy(
                        zrow(R)[0:R, 0:w].rearrange("r (n e) -> r n e", e=8)[:, :, jj:jj + 1],
                        zqi2(R)[0:R, 0:w // 8].unsqueeze(2)))
                    C.drain("vector")
                C.op("vector", lambda e, R=R, w=w: e.tensor_scalar(zrow(R)[0:R, 0:w], zrow(R)[0:R, 0:w],
                                                                   1.0, None, ALU.add))
                C.drain("vector")
                C.op("vector", lambda e, R=R, w=w: e.tensor_tensor(zrow(R)[0:R, 0:w], zrow(R)[0:R, 0:w],
                                                                   scrow(R)[0:R, 0:w], ALU.mult), sem="vD")
                C.wait("sync", "vD", C.cnt["vD"])
                C.dma("sync", "st", sz_d[0:R, c0:c0 + w], zrow(R)[0:R, 0:w])
        C.marks["st_sz"] = C.cnt["st"]

    def ph_trigger1(ctx):
        C = ctx
        C.wait("gpsimd", "st", C.marks["st_qkvw"])
        C.collective("ccq", g["agw_qkvw_i"], g["agw_qkvw_o"])
        C.wait("gpsimd", "st", C.marks["st_ow"])
        C.collective("cco", g["agw_ow_i"], g["agw_ow_o"])

    def ph_dequant(ctx):
        C = ctx
        agg_i = g["agw_g_i"].rearrange("(f p) k -> f p k", p=128).rearrange("f p k -> p f k")
        agu_i = g["agw_u_i"].rearrange("(f p) k -> f p k", p=128).rearrange("f p k -> p f k")
        jobs = []
        for (q_in, sz_d, sc_in, dsti, R, N, isdown) in (
            (g["gq_in"], g["szg_d"], g["gsc_in"], agg_i, KCS, INTER, False),
            (g["uq_in"], g["szu_d"], g["usc_in"], agu_i, KCS, INTER, False),
            (g["dq_in"], g["szd_d"], g["dsc_in"], None, GRP_I, DQC, True),
        ):
            for j in range(R):
                for c0 in range(0, N, DQN):
                    w = min(DQN, N - c0)
                    jobs.append((q_in, sz_d, sc_in, dsti, j, c0, w, isdown))
        vD0 = C.cnt["vD"]
        st_dq0 = C.cnt["st"]
        for it, (q_in, sz_d, sc_in, dsti, j, c0, w, isdown) in enumerate(jobs):
            if it >= 1:
                C.wait("sync", "vD", vD0 + 2 * it)
            C.dma("sync", "ld0", dq_q()[:, 0:w],
                  q_in[16 * j:16 * (j + 1), c0:c0 + w].unsqueeze(1).broadcast_to([16, 8, w]))
            C.dma("sync", "ld0", dq_s()[:, 0:w],
                  sc_in[j:j + 1, c0:c0 + w].broadcast_to([128, w]), fence=False)
            C.dma("sync", "ld0", dq_z()[:, 0:w],
                  sz_d[j:j + 1, c0:c0 + w].broadcast_to([128, w]), fence=False)
            ldmk = C.cnt["ld0"]
            C.wait("vector", "ld0", ldmk)
            if it >= 1:
                C.wait("vector", "st", st_dq0 + 16 * it)
            C.op("vector", lambda e, w=w: e.tensor_tensor(
                dq_q()[:, 0:w], dq_q()[:, 0:w], maskt[:, 0:1].broadcast_to([128, w]), ALU.bitwise_and))
            C.drain("vector")
            C.op("vector", lambda e, w=w: e.scalar_tensor_tensor(
                dqf()[:, 0:w], dq_q()[:, 0:w], pow16t[:, 0:1], dq_s()[:, 0:w],
                ALU.mult, ALU.mult), sem="vD")
            C.drain("vector")
            C.op("vector", lambda e, w=w: e.tensor_tensor(
                dq_o()[:, 0:w], dqf()[:, 0:w], dq_z()[:, 0:w], ALU.subtract), sem="vD")
            C.wait("sync", "vD", C.cnt["vD"])
            if isdown:
                dst = g["agw_d_i"][j * 128:(j + 1) * 128, 0:w]
            else:
                ft0, ft1 = c0 // 128, (c0 + w) // 128
                dst = dsti[:, ft0:ft1, j * 128:(j + 1) * 128]
            C.dma("sync", "st", dst, dq_o()[:, 0:w] if isdown else
                  dq_o()[:, 0:w].rearrange("p (f z) -> p f z", z=128))
        C.marks["st_dq"] = C.cnt["st"]

    def ph_trigger2(ctx):
        C = ctx
        C.wait("gpsimd", "st", C.marks["st_dq"])
        C.collective("ccf", g["agw_g_i"], g["agw_g_o"])
        C.collective("ccf", g["agw_u_i"], g["agw_u_o"])
        C.collective("ccf", g["agw_d_i"], g["agw_d_o"])
        C.collective("ccw", g["agw_wqT_i"], g["agw_wqT_o"])
        C.collective("ccw", g["agw_wkT_i"], g["agw_wkT_o"])
        C.collective("ccw", g["agw_wvT_i"], g["agw_wvT_o"])
        C.wait("gpsimd", "st", C.marks["st_agb2"])
        C.collective("ccb2", g["agb2_k_i"], g["agb2_k_o"])
        C.collective("ccb2", g["agb2_v_i"], g["agb2_v_o"])
        C.collective("ccb2", g["agb2_nk_i"], g["agb2_nk_o"])
        C.collective("ccwo", g["agw_woT_i"], g["agw_woT_o"])

    def ln_panel(ctx, src_rows, pan):
        C = ctx
        norm_marks, ev_marks = {}, {}
        for tt in range(TPP):
            i = pan * TPP + tt
            s = tt % 2
            if tt >= 2:
                C.wait("sync", "sD", norm_marks[tt - 2])
            C.dma("sync", "ld" + str(s), xt(s)[:, 0:D], src_rows[i * 128:(i + 1) * 128, 0:D])
            ldmk = C.cnt["ld" + str(s)]
            C.wait("scalar", "ld" + str(s), ldmk)
            C.op("scalar", lambda e, s=s: e.activation(xnb()[:, 0:D], xt(s)[:, 0:D], ACT.Identity,
                                                       accum_out=lnsc[:, 0:1]))
            C.drain("scalar")
            C.op("scalar", lambda e, s=s: e.activation(xnb()[:, 0:D], xt(s)[:, 0:D], ACT.Square,
                                                       accum_out=lnsc[:, 1:2]), sem="sD")
            C.wait("vector", "sD", C.cnt["sD"])
            C.op("vector", lambda e: e.tensor_scalar(lnsc[:, 2:3], lnsc[:, 0:1], 1.0 / D, None, ALU.mult))
            C.op("vector", lambda e: e.tensor_scalar(lnsc[:, 3:4], lnsc[:, 1:2], 1.0 / D, None, ALU.mult))
            C.drain("vector")
            C.op("vector", lambda e: e.tensor_tensor(lnsc[:, 4:5], lnsc[:, 2:3], lnsc[:, 2:3], ALU.mult))
            C.drain("vector")
            C.op("vector", lambda e: e.tensor_tensor(lnsc[:, 4:5], lnsc[:, 3:4], lnsc[:, 4:5], ALU.subtract))
            C.drain("vector")
            C.op("vector", lambda e: e.tensor_scalar(lnsc[:, 4:5], lnsc[:, 4:5], 1e-5, None, ALU.add), sem="vD")
            C.wait("scalar", "vD", C.cnt["vD"])
            C.op("scalar", lambda e: e.activation(lnsc[:, 5:6], lnsc[:, 4:5], ACT.Sqrt), sem="sD")
            C.wait("vector", "sD", C.cnt["sD"])
            C.op("vector", lambda e: e.reciprocal(lnsc[:, 6:7], lnsc[:, 5:6]))
            C.drain("vector")
            C.op("vector", lambda e: e.tensor_tensor(lnsc[:, 7:8], lnsc[:, 6:7], lnsc[:, 6:7], ALU.mult))
            C.drain("vector")
            C.op("vector", lambda e: e.tensor_tensor(lnsc[:, 7:8], lnsc[:, 4:5], lnsc[:, 7:8], ALU.mult))
            C.drain("vector")
            C.op("vector", lambda e: e.tensor_scalar(lnsc[:, 7:8], lnsc[:, 7:8], -0.5, 1.5, ALU.mult, ALU.add))
            C.drain("vector")
            C.op("vector", lambda e: e.tensor_tensor(lnsc[:, 6:7], lnsc[:, 6:7], lnsc[:, 7:8], ALU.mult))
            C.drain("vector")
            C.op("vector", lambda e: e.tensor_tensor(lnsc[:, 7:8], lnsc[:, 2:3], lnsc[:, 6:7], ALU.mult))
            C.drain("vector")
            C.op("vector", lambda e: e.tensor_scalar(lnsc[:, 7:8], lnsc[:, 7:8], -1.0, None, ALU.mult), sem="vD")
            C.wait("scalar", "vD", C.cnt["vD"])
            C.op("scalar", lambda e, s=s: e.activation(xnb()[:, 0:D], xt(s)[:, 0:D], ACT.Identity,
                                                       bias=lnsc[:, 7:8], scale=lnsc[:, 6:7]), sem="sD")
            norm_marks[tt] = C.cnt["sD"]
            nrm = C.cnt["sD"]
            for kc in range(KC):
                b = kc % 2
                C.wait("tensor", "sD", nrm if kc < 2 else ev_marks[(tt, kc - 2)])
                C.op("tensor", lambda e, b=b, kc=kc: e.transpose(psT(4 + b),
                                                                 xnb()[:, kc * 128:(kc + 1) * 128], identb[:]),
                     sem="tD")
                C.wait("scalar", "tD", C.cnt["tD"])
                C.op("scalar", lambda e, b=b, kc=kc, tt=tt: e.activation(
                    lnT()[:, kc * PANEL + tt * 128: kc * PANEL + (tt + 1) * 128],
                    psT(4 + b), ACT.Identity), sem="sD")
                ev_marks[(tt, kc)] = C.cnt["sD"]
        C.marks["lnT_sD"] = C.cnt["sD"]

    def gemm_panel(ctx, pan, jobs, rhs_off, epi, epi_bank_free, nkc=None, extra_k1=None):
        C = ctx
        NKC = KC if nkc is None else nkc
        mm_done = {}
        for it, (view, ft) in enumerate(jobs):
            s = it % 2
            if it >= 2:
                C.wait("sync", "tD", mm_done[it - 2])
            C.dma("sync", "ld" + str(s), wb[s][:, 0:NKC * 128], lhsT_view(view, ft))
            ldmk = C.cnt["ld" + str(s)]
            C.wait("tensor", "ld" + str(s), ldmk)
            if it >= 2:
                C.wait("tensor", *epi_bank_free(it - 2))
            bank = ps[it % 2]
            for kc in range(NKC):
                C.op("tensor", lambda e, kc=kc, s=s, bank=bank: e.matmul(
                    bank[:, 0:PANEL], wb[s][:, kc * 128:(kc + 1) * 128],
                    lnT()[:, kc * PANEL + rhs_off:kc * PANEL + rhs_off + PANEL],
                    start=(kc == 0), stop=(kc == NKC - 1 and extra_k1 is None)), sem="tD")
            if extra_k1 is not None:
                extra_k1(C, it, bank)
            mm_done[it] = C.cnt["tD"]
            epi(C, bank, ft, pan, it)

    def ph_lnqkv(ctx):
        C = ctx
        C.op("vector", lambda e: e.memset(mqrow[:], -3e38))
        C.op("vector", lambda e: e.memset(mkrow[:], -3e38))
        C.drain("vector")
        C.wait("tensor", "ccq", C.cnt["ccq"])
        C.wait("sync", "ccq", C.cnt["ccq"])
        for pan in range(NPAN):
            ln_panel(C, x_in, pan)
            C.barrier()
            fts = ([("k", NH + h) for h in range(NH)] + [("v", 2 * NH + h) for h in range(NH)]
                   + [("q", h) for h in range(NH)])
            bank_free = {-2: ("sD", 0), -1: ("sD", 0)}
            ptfree = {-2: ("sD", 0), -1: ("sD", 0)}
            sqfree = [("sD", 0)]
            vtfree = [("sD", 0)]
            trfree = {0: ("sD", 0), 1: ("sD", 0)}
            nrmfree = [("sD", 0)]
            linvfree = [("vD", 0)]

            def epi(C2, bank, ftk, pan2, it):
                kind, ft = fts[it]
                h = ft % NH
                C2.wait("scalar", "tD", C2.cnt["tD"])
                if kind == "q":
                    dst = qt()[:, h * T + pan2 * PANEL: h * T + pan2 * PANEL + PANEL]
                    C2.op("scalar", lambda e, dst=dst, ft=ft: e.activation(
                        dst, bank[:, 0:PANEL], ACT.Identity, bias=qkvb_t[:, ft:ft + 1]), sem="sD")
                    evk = C2.cnt["sD"]
                    bank_free[it] = ("sD", evk)
                    ptfree[it] = ptfree[it - 2]
                    C2.wait("vector", "sD", evk)
                    C2.wait("vector", *sqfree[0])
                    C2.op("vector", lambda e, dst=dst: e.tensor_tensor(pt2[:, 0:PANEL], dst, dst, ALU.mult),
                          sem="vD")
                    C2.wait("tensor", "vD", C2.cnt["vD"])
                    C2.wait("tensor", *nrmfree[0])
                    C2.op("tensor", lambda e: e.matmul(pl[0][0:1, 0:PANEL], onesb[:], pt2[:, 0:PANEL],
                                                       start=True, stop=True), sem="tD")
                    sqfree[0] = ("tD", C2.cnt["tD"])
                    C2.wait("scalar", "tD", C2.cnt["tD"])
                    C2.wait("scalar", *linvfree[0])
                    C2.op("scalar", lambda e: e.activation(
                        linv[0:1, 0:PANEL], pl[0][0:1, 0:PANEL], ACT.Identity), sem="sD")
                    nrmfree[0] = ("sD", C2.cnt["sD"])
                    C2.wait("vector", "sD", C2.cnt["sD"])
                    C2.op("vector", lambda e: e.tensor_reduce(
                        mtmp[0:1, 2:3], linv[0:1, 0:PANEL], mybir.AxisListType.X, ALU.max))
                    C2.drain("vector")
                    C2.op("vector", lambda e, h=h: e.tensor_tensor(
                        mqrow[0:1, h:h + 1], mqrow[0:1, h:h + 1], mtmp[0:1, 2:3], ALU.max), sem="vD")
                    C2.drain("vector")
                    linvfree[0] = ("vD", C2.cnt["vD"])
                elif kind == "k":
                    C2.wait("scalar", *ptfree[it - 2])
                    sl = pt[it % 2]
                    C2.op("scalar", lambda e, sl=sl, ft=ft: e.activation(
                        sl[:, 0:PANEL], bank[:, 0:PANEL], ACT.Identity, bias=qkvb_t[:, ft:ft + 1]), sem="sD")
                    evk = C2.cnt["sD"]
                    bank_free[it] = ("sD", evk)
                    C2.wait("sync", "sD", evk)
                    C2.dma("sync", "st", g["aga_k_i"][h * 128:(h + 1) * 128, pan2 * PANEL:pan2 * PANEL + PANEL],
                           sl[:, 0:PANEL])
                    stk = C2.cnt["st"]
                    C2.wait("vector", "sD", evk)
                    C2.wait("vector", *sqfree[0])
                    C2.op("vector", lambda e, sl=sl: e.tensor_tensor(pt2[:, 0:PANEL], sl[:, 0:PANEL],
                                                                     sl[:, 0:PANEL], ALU.mult), sem="vD")
                    vsq = C2.cnt["vD"]
                    C2.wait("tensor", "vD", vsq)
                    C2.wait("tensor", *nrmfree[0])
                    C2.op("tensor", lambda e: e.matmul(pl[0][0:1, 0:PANEL], onesb[:], pt2[:, 0:PANEL],
                                                       start=True, stop=True), sem="tD")
                    sqfree[0] = ("tD", C2.cnt["tD"])
                    C2.wait("scalar", "tD", C2.cnt["tD"])
                    C2.wait("scalar", *linvfree[0])
                    C2.op("scalar", lambda e: e.activation(
                        linv[0:1, 0:PANEL], pl[0][0:1, 0:PANEL], ACT.Identity), sem="sD")
                    nrmfree[0] = ("sD", C2.cnt["sD"])
                    C2.wait("vector", "sD", C2.cnt["sD"])
                    C2.op("vector", lambda e: e.tensor_reduce(
                        mtmp[0:1, 2:3], linv[0:1, 0:PANEL], mybir.AxisListType.X, ALU.max))
                    C2.drain("vector")
                    C2.op("vector", lambda e, h=h: e.tensor_tensor(
                        mkrow[0:1, h:h + 1], mkrow[0:1, h:h + 1], mtmp[0:1, 2:3], ALU.max), sem="vD")
                    C2.drain("vector")
                    linvfree[0] = ("vD", C2.cnt["vD"])
                    ptfree[it] = ("st", stk)
                else:
                    C2.wait("scalar", *ptfree[it - 2])
                    sl = pt[it % 2]
                    C2.op("scalar", lambda e, sl=sl, ft=ft: e.activation(
                        sl[:, 0:PANEL], bank[:, 0:PANEL], ACT.Identity, bias=qkvb_t[:, ft:ft + 1]), sem="sD")
                    evk = C2.cnt["sD"]
                    bank_free[it] = ("sD", evk)
                    lasttr = 0
                    for j in range(TPP):
                        C2.wait("tensor", "sD", evk)
                        C2.wait("tensor", *trfree[j % 2])
                        C2.op("tensor", lambda e, j=j, sl=sl: e.transpose(
                            psT(4 + j % 2), sl[:, j * 128:(j + 1) * 128], identb[:]), sem="tD")
                        lasttr = C2.cnt["tD"]
                        C2.wait("scalar", "tD", lasttr)
                        if j == 0:
                            C2.wait("scalar", *vtfree[0])
                        C2.op("scalar", lambda e, j=j: e.activation(
                            pt3[:, j * 128:(j + 1) * 128], psT(4 + j % 2), ACT.Identity), sem="sD")
                        trfree[j % 2] = ("sD", C2.cnt["sD"])
                    C2.wait("sync", "sD", C2.cnt["sD"])
                    C2.dma("sync", "st",
                           g["aga_v_i"][pan2 * PANEL:(pan2 + 1) * PANEL, h * 128:(h + 1) * 128]
                           .rearrange("(j p) c -> p j c", p=128),
                           pt3[:, 0:PANEL].rearrange("p (j cc) -> p j cc", cc=128))
                    vtfree[0] = ("st", C2.cnt["st"])
                    ptfree[it] = ("tD", lasttr)

            gemm_panel(C, pan, [(agw_qkvw, ft) for _, ft in fts], 0, epi,
                       lambda it: bank_free[it])
            C.barrier()
        C.wait("sync", "vD", C.cnt["vD"])
        C.dma("sync", "st", g["aga_nk_i"][0:1, 0:NH], mkrow[0:1, 0:NH])
        C.marks["st_aga"] = C.cnt["st"]

    def ph_trigger_aga(ctx):
        C = ctx
        C.wait("gpsimd", "st", C.marks["st_aga"])
        grp = [[0, 1, 2, 3], [4, 5, 6, 7]]
        C.collective("cca", g["aga_k_i"], g["aga_k_o"], groups=grp)
        C.collective("cca", g["aga_v_i"], g["aga_v_o"], groups=grp)
        C.collective("cca", g["aga_nk_i"], g["aga_nk_o"], groups=grp)

    def mprep(ctx, which):
        C = ctx
        if which == "a":
            C.wait("sync", "cca", C.cnt["cca"])
            C.dma("sync", "ld", mtmp[0:1, 64:64 + NH * 4],
                  g["aga_nk_o"].rearrange("r h -> h r"), slow=True)
            C.wait("vector", "ld", C.cnt["ld"])
            C.op("vector", lambda e: e.tensor_reduce(
                mkrow[0:1, 0:NH], mtmp[0:1, 64:64 + NH * 4].rearrange("o (h r) -> o h r", r=4),
                mybir.AxisListType.X, ALU.max), sem="vD")
            C.drain("vector")
        else:
            C.wait("sync", "ccb1", C.cnt["ccb1"])
            C.wait("sync", "ccb2", C.cnt["ccb2"])
            C.dma("sync", "ld", mtmp[0:1, 64:64 + NKV * 4],
                  g["agb1_nk_o"][0:4].rearrange("r h -> h r"), slow=True)
            C.dma("sync", "ld", mtmp[0:1, 160:160 + NCORES], g["agb2_nk_o"].rearrange("r o -> o r"),
                  fence=False, slow=True)
            C.wait("vector", "ld", C.cnt["ld"])
            C.op("vector", lambda e: e.tensor_reduce(
                mkrow[0:1, 0:NKV], mtmp[0:1, 64:64 + NKV * 4].rearrange("o (h r) -> o h r", r=4),
                mybir.AxisListType.X, ALU.max))
            C.drain("vector")
            C.op("vector", lambda e: e.tensor_reduce(
                mtmp[0:1, 0:NKV], mtmp[0:1, 160:160 + NCORES].rearrange("o (h r) -> o h r", r=RPK),
                mybir.AxisListType.X, ALU.max))
            C.drain("vector")
            C.op("vector", lambda e: e.tensor_tensor(
                mkrow[0:1, 0:NKV], mkrow[0:1, 0:NKV], mtmp[0:1, 0:NKV], ALU.max))
            C.drain("vector")
            C.op("vector", lambda e: e.tensor_copy(
                mkrow[0:1, 32:32 + NH].rearrange("o (k gg) -> o k gg", gg=KVG),
                mkrow[0:1, 0:NKV].unsqueeze(2).broadcast_to([1, NKV, KVG])))
            C.drain("vector")
            C.op("vector", lambda e: e.tensor_copy(mkrow[0:1, 0:NH], mkrow[0:1, 32:32 + NH]), sem="vD")
            C.drain("vector")
        C.op("vector", lambda e: e.tensor_tensor(mtmp[0:1, 0:NH], mqrow[0:1, 0:NH], mkrow[0:1, 0:NH],
                                                 ALU.mult), sem="vD")
        C.wait("scalar", "vD", C.cnt["vD"])
        C.op("scalar", lambda e: e.activation(mtmp[0:1, 32:32 + NH], mtmp[0:1, 0:NH], ACT.Sqrt), sem="sD")
        C.wait("vector", "sD", C.cnt["sD"])
        C.op("vector", lambda e: e.tensor_scalar(mtmp[0:1, 0:NH], mtmp[0:1, 32:32 + NH], -SCALE, None,
                                                 ALU.mult), sem="vD")
        C.wait("tensor", "vD", C.cnt["vD"])
        C.op("tensor", lambda e: e.matmul(ps[4][:, 0:NH], onesrow[0:1, 0:128], mtmp[0:1, 0:NH],
                                          start=True, stop=True), sem="tD")
        C.wait("scalar", "tD", C.cnt["tD"])
        C.op("scalar", lambda e: e.activation(mbias[:, 0:NH], ps[4][:, 0:NH], ACT.Identity), sem="sD")

    def ph_attn(ctx, which):
        C = ctx
        if which == "a":
            groups = [[h] for h in range(NH)]
            KVLEN = cfg["S"]
        else:
            groups = [list(range(kvh * KVG, (kvh + 1) * KVG)) for kvh in range(NKV)]
            KVLEN = KV_B
        NTK = KVLEN // 128
        gmark = {}
        gdone = {-2: None, -1: None}
        plfree = {0: ("vD", 0), 1: ("vD", 0)}
        ctxfree = {0: ("vD", 0), 1: ("vD", 0)}
        bcast_free = ("vD", 0)
        expfree = {}
        ctxmm = {}

        for gi, grp in enumerate(groups):
            s = gi % 2
            kse, vse = "ld" + str(2 + s), "ld" + str(4 + s)
            if gi >= 2:
                C.wait("sync", "tD", gdone[gi - 2])
            if which == "a":
                h = grp[0]
                C.dma("sync", kse, kvK(s)[:, 0:KVLEN],
                      aga_k[:, h * 128:(h + 1) * 128, :].rearrange("r p t -> p r t"))
                C.dma("sync", vse, kvV(s)[:, 0:NTK * 128],
                      aga_v.rearrange("r (j p) f -> r j p f", p=128)[:, :, :, h * 128:(h + 1) * 128]
                      .rearrange("r j p f -> p r j f"))
            else:
                kvh = gi
                C.dma("sync", kse, kvK(s)[:, 0:CACHE_T],
                      agb2_k[kvh * RPK:(kvh + 1) * RPK].rearrange("r p t -> p r t") if RPK > 1
                      else agb2_k[kvh * RPK])
                C.dma("sync", kse, kvK(s)[:, CACHE_T:CACHE_T + NEW_T],
                      agb1_k[0:4, kvh * 128:(kvh + 1) * 128, :].rearrange("r p t -> p r t"))
                C.dma("sync", vse, kvV(s)[:, 0:(CACHE_T // 128) * 128],
                      agb2_v[kvh * RPK:(kvh + 1) * RPK].rearrange("r (j p) f -> p (r j) f", p=128))
                C.dma("sync", vse, kvV(s)[:, CACHE_T:CACHE_T + NEW_T],
                      agb1_v[0:4].rearrange("r (j p) f -> r j p f", p=128)[:, :, :, kvh * 128:(kvh + 1) * 128]
                      .rearrange("r j p f -> p r j f"))
            gmark[gi] = (C.cnt[kse], C.cnt[vse])
            for h in grp:
                for pan in range(NPAN):
                    qoff = h * T + pan * PANEL
                    C.wait("tensor", kse, gmark[gi][0])
                    C.wait("tensor", vse, gmark[gi][1])
                    for tau in range(NTK):
                        b = tau % 2
                        if (h == grp[0] and pan == 0) and tau < 2:
                            pass
                        if expfree.get((gi, b)) is not None:
                            C.wait("tensor", "sD", expfree[(gi, b)])
                        C.op("tensor", lambda e, b=b, tau=tau, qoff=qoff, s=s: e.matmul(
                            ps[b][:, 0:PANEL], kvK(s)[:, tau * 128:(tau + 1) * 128],
                            qt()[:, qoff:qoff + PANEL], start=True, stop=True), sem="tD")
                        smark = C.cnt["tD"]
                        C.wait("scalar", "tD", smark)
                        if ctxmm.get((gi, b)) is not None:
                            C.wait("scalar", "tD", ctxmm[(gi, b)])
                        C.op("scalar", lambda e, b=b, h=h: e.activation(
                            pt[b][:, 0:PANEL], ps[b][:, 0:PANEL], ACT.Exp,
                            bias=mbias[:, h:h + 1], scale=SCALE), sem="sD")
                        expfree[(gi, b)] = C.cnt["sD"]
                        C.wait("tensor", "sD", C.cnt["sD"])
                        if tau == 0:
                            C.wait("tensor", *plfree[pan])
                            C.wait("tensor", *ctxfree[pan])
                        C.op("tensor", lambda e, b=b, pan=pan, tau=tau: e.matmul(
                            pl[pan][0:1, 0:PANEL], onesb[:], pt[b][:, 0:PANEL],
                            start=(tau == 0), stop=(tau == NTK - 1)), sem="tD")
                        C.op("tensor", lambda e, b=b, pan=pan, tau=tau, s=s: e.matmul(
                            ps[2 + pan][:, 0:PANEL], kvV(s)[:, tau * 128:(tau + 1) * 128],
                            pt[b][:, 0:PANEL], start=(tau == 0), stop=(tau == NTK - 1)), sem="tD")
                        ctxmm[(gi, b)] = C.cnt["tD"]
                    lstop = C.cnt["tD"] - 1
                    C.wait("vector", "tD", C.cnt["tD"])
                    C.wait("vector", *bcast_free)
                    C.op("vector", lambda e, pan=pan: e.reciprocal(linv[0:1, 0:PANEL],
                                                                   pl[pan][0:1, 0:PANEL]), sem="vD")
                    plfree[pan] = ("vD", C.cnt["vD"])
                    C.wait("tensor", "vD", C.cnt["vD"])
                    C.op("tensor", lambda e: e.matmul(ps[4][:, 0:PANEL], onesrow[0:1, 0:128],
                                                      linv[0:1, 0:PANEL], start=True, stop=True), sem="tD")
                    C.wait("scalar", "tD", C.cnt["tD"])
                    C.op("scalar", lambda e: e.activation(ev[0][:, 0:PANEL], ps[4][:, 0:PANEL],
                                                          ACT.Identity), sem="sD")
                    C.wait("vector", "sD", C.cnt["sD"])
                    C.op("vector", lambda e, pan=pan, qoff=qoff: e.tensor_tensor(
                        qt()[:, qoff:qoff + PANEL], ps[2 + pan][:, 0:PANEL], ev[0][:, 0:PANEL],
                        ALU.mult), sem="vD")
                    ctxfree[pan] = ("vD", C.cnt["vD"])
                    bcast_free = ("vD", C.cnt["vD"])
            gdone[gi] = C.cnt["tD"]

    def ph_outproj(ctx, which, res_src, res_dst):
        C = ctx
        if which == "a":
            C.wait("sync", "cco", C.cnt["cco"])
            wsrc = g["agw_ow_o"]
            bias = True
        else:
            C.wait("sync", "ccwo", C.cnt["ccwo"])
            wsrc = g["agw_woT_o"]
            bias = False
        NFC = QF // 128
        NRND = NTT // TPP
        rsfree = {-2: ("tD", 0), -1: ("tD", 0)}
        xcfree = {-2: ("vD", 0), -1: ("vD", 0)}
        evfree = {-2: ("st", 0), -1: ("st", 0)}
        bankfree = {ii: ("vD", 0) for ii in range(TPP)}
        bankdone = {}
        it_r = 0
        it_x = 0
        for dp in range(DP):
            for rnd in range(NRND):
                for fc in range(NFC):
                    sr = it_r % 2
                    if it_r >= 2:
                        C.wait("sync", *rsfree[it_r - 2])
                    C.dma("sync", "ld" + str(2 + sr), rs[sr][:, 0:512],
                          wsrc[fc * 128:(fc + 1) * 128, dp * 512:(dp + 1) * 512])
                    ldmk = C.cnt["ld" + str(2 + sr)]
                    C.wait("tensor", "ld" + str(2 + sr), ldmk)
                    for ii in range(TPP):
                        i = rnd * TPP + ii
                        if fc == 0:
                            C.wait("tensor", *bankfree[ii])
                        C.op("tensor", lambda e, sr=sr, fc=fc, ii=ii, i=i: e.matmul(
                            ps[ii][:, 0:512], qt()[:, fc * T + i * 128:fc * T + (i + 1) * 128],
                            rs[sr][:, 0:512], start=(fc == 0), stop=(fc == NFC - 1 and not bias)),
                            sem="tD")
                    rsfree[it_r] = ("tD", C.cnt["tD"])
                    it_r += 1
                if bias:
                    for ii in range(TPP):
                        C.op("tensor", lambda e, ii=ii, dp=dp: e.matmul(
                            ps[ii][:, 0:512], onesrow_b[0:1, 0:128], obrow_b[0:1, dp * 512:(dp + 1) * 512],
                            start=False, stop=True), sem="tD")
                bankdone[(dp, rnd)] = C.cnt["tD"]
                for ii in range(TPP):
                    i = rnd * TPP + ii
                    sx = it_x % 2
                    if it_x >= 2:
                        C.wait("sync", *xcfree[it_x - 2])
                    C.dma("sync", "ld" + str(4 + sx), ev[2 + sx][:, 0:512],
                          res_src[i * 128:(i + 1) * 128, dp * 512:(dp + 1) * 512])
                    ldmk = C.cnt["ld" + str(4 + sx)]
                    C.wait("vector", "ld" + str(4 + sx), ldmk)
                    C.wait("vector", "tD", bankdone[(dp, rnd)])
                    C.wait("vector", *evfree[it_x - 2])
                    C.op("vector", lambda e, ii=ii, sx=sx: e.scalar_tensor_tensor(
                        ev[sx][:, 0:512], ps[ii][:, 0:512], 1.0, ev[2 + sx][:, 0:512],
                        ALU.mult, ALU.add), sem="vD")
                    xcfree[it_x] = ("vD", C.cnt["vD"])
                    bankfree[ii] = ("vD", C.cnt["vD"])
                    C.wait("sync", "vD", C.cnt["vD"])
                    C.dma("sync", "st", res_dst[i * 128:(i + 1) * 128, dp * 512:(dp + 1) * 512],
                          ev[sx][:, 0:512])
                    evfree[it_x] = ("st", C.cnt["st"])
                    it_x += 1

    def ph_ffn(ctx, res_src, res_dst):
        C = ctx
        C.wait("sync", "ccf", C.cnt["ccf"])
        C.wait("tensor", "ccf", C.cnt["ccf"])
        RPD = max(1, 512 // DQC)
        for pan in range(NPAN):
            ln_panel(C, res_src, pan)
            C.barrier()
            gfree = ("vD", 0)
            g_sD = 0
            u_vD = 0
            gmm_prev = 0
            umm_prev = 0
            pt0free = ("tD", 0)
            for nt in range(NT_I):
                if nt >= 1:
                    C.wait("sync", "tD", gmm_prev)
                C.dma("sync", "ld0", wb[0][:, 0:KC * 128], lhsT_view(agw_g, nt))
                l0 = C.cnt["ld0"]
                if nt >= 1:
                    C.wait("sync", "tD", umm_prev)
                C.dma("sync", "ld1", wb[1][:, 0:KC * 128], lhsT_view(agw_u, nt))
                l1 = C.cnt["ld1"]
                C.wait("tensor", "ld0", l0)
                if nt >= 1:
                    C.wait("tensor", "sD", g_sD)
                for kc in range(KC):
                    C.op("tensor", lambda e, kc=kc: e.matmul(
                        ps[0][:, 0:PANEL], wb[0][:, kc * 128:(kc + 1) * 128],
                        lnT()[:, kc * PANEL:(kc + 1) * PANEL],
                        start=(kc == 0), stop=(kc == KC - 1)), sem="tD")
                gmm = C.cnt["tD"]
                C.wait("tensor", "ld1", l1)
                if nt >= 1:
                    C.wait("tensor", "vD", u_vD)
                for kc in range(KC):
                    C.op("tensor", lambda e, kc=kc: e.matmul(
                        ps[1][:, 0:PANEL], wb[1][:, kc * 128:(kc + 1) * 128],
                        lnT()[:, kc * PANEL:(kc + 1) * PANEL],
                        start=(kc == 0), stop=(kc == KC - 1)), sem="tD")
                umm = C.cnt["tD"]
                gmm_prev, umm_prev = gmm, umm
                C.wait("scalar", "tD", gmm)
                if nt >= 1:
                    C.wait("scalar", "vD", u_vD)
                C.op("scalar", lambda e: e.activation(pt[0][:, 0:PANEL], ps[0][:, 0:PANEL], ACT.Silu),
                     sem="sD")
                g_sD = C.cnt["sD"]
                C.wait("vector", "sD", g_sD)
                C.wait("vector", "tD", umm)
                C.op("vector", lambda e, nt=nt: e.tensor_tensor(
                    hT()[:, nt * PANEL:(nt + 1) * PANEL], pt[0][:, 0:PANEL], ps[1][:, 0:PANEL],
                    ALU.mult), sem="vD")
                u_vD = C.cnt["vD"]
            C.barrier()
            it_r = 0
            it_x = 0
            rsfree = {-2: ("tD", 0), -1: ("tD", 0)}
            xcfree = {-2: ("vD", 0), -1: ("vD", 0)}
            evfree = {-2: ("st", 0), -1: ("st", 0)}
            bankfree = {ii: ("vD", 0) for ii in range(TPP)}
            for dp in range(DP):
                for kc in range(NT_I):
                    sr = it_r % 2
                    if it_r >= 2:
                        C.wait("sync", *rsfree[it_r - 2])
                    r0 = dp * RPD if DQC <= 512 else dp * 512 // DQC
                    C.dma("sync", "ld" + str(2 + sr), rs[sr][:, 0:512],
                          agw_d[r0:r0 + RPD, kc * 128:(kc + 1) * 128, :].rearrange("r p cc -> p r cc"))
                    ldmk = C.cnt["ld" + str(2 + sr)]
                    C.wait("tensor", "ld" + str(2 + sr), ldmk)
                    for ii in range(TPP):
                        if kc == 0:
                            C.wait("tensor", *bankfree[ii])
                        C.op("tensor", lambda e, sr=sr, kc=kc, ii=ii: e.matmul(
                            ps[2 + ii][:, 0:512], hT()[:, kc * PANEL + ii * 128:kc * PANEL + (ii + 1) * 128],
                            rs[sr][:, 0:512], start=(kc == 0), stop=(kc == NT_I - 1)), sem="tD")
                    rsfree[it_r] = ("tD", C.cnt["tD"])
                    it_r += 1
                bankdone = C.cnt["tD"]
                for ii in range(TPP):
                    i = pan * TPP + ii
                    sx = it_x % 2
                    if it_x >= 2:
                        C.wait("sync", *xcfree[it_x - 2])
                    C.dma("sync", "ld" + str(4 + sx), ev[2 + sx][:, 0:512],
                          res_src[i * 128:(i + 1) * 128, dp * 512:(dp + 1) * 512])
                    ldmk = C.cnt["ld" + str(4 + sx)]
                    C.wait("vector", "ld" + str(4 + sx), ldmk)
                    C.wait("vector", "tD", bankdone)
                    C.wait("vector", *evfree[it_x - 2])
                    C.op("vector", lambda e, ii=ii, sx=sx: e.scalar_tensor_tensor(
                        ev[sx][:, 0:512], ps[2 + ii][:, 0:512], 1.0, ev[2 + sx][:, 0:512],
                        ALU.mult, ALU.add), sem="vD")
                    xcfree[it_x] = ("vD", C.cnt["vD"])
                    bankfree[ii] = ("vD", C.cnt["vD"])
                    C.wait("sync", "vD", C.cnt["vD"])
                    C.dma("sync", "st", res_dst[i * 128:(i + 1) * 128, dp * 512:(dp + 1) * 512],
                          ev[sx][:, 0:512])
                    evfree[it_x] = ("st", C.cnt["st"])
                    it_x += 1
            C.barrier()

    def ph_bproj(ctx, res_src):
        C = ctx
        C.op("vector", lambda e: e.memset(mqrow[:], -3e38))
        C.op("vector", lambda e: e.memset(mkrow[:], -3e38))
        C.drain("vector")
        C.wait("tensor", "ccw", C.cnt["ccw"])
        C.wait("sync", "ccw", C.cnt["ccw"])
        for pan in range(NPAN):
            ln_panel(C, res_src, pan)
            C.barrier()
            jobs = ([("k", agw_wkT, h) for h in range(NFT_KV)] + [("v", agw_wvT, h) for h in range(NFT_KV)]
                    + [("q", agw_wqT, h) for h in range(NFT_Q)])
            bank_free = {-2: ("sD", 0), -1: ("sD", 0)}
            ptfree = {-2: ("sD", 0), -1: ("sD", 0)}
            sqfree = [("sD", 0)]
            vtfree = [("sD", 0)]
            trfree = {0: ("sD", 0), 1: ("sD", 0)}
            nrmfree = [("sD", 0)]
            linvfree = [("vD", 0)]
            ropefree = [("sD", 0)]

            def epi(C2, bank, ftk, pan2, it):
                kind, _, h = jobs[it]
                C2.wait("scalar", "tD", C2.cnt["tD"])
                C2.wait("scalar", *ptfree[it - 2])
                sl = pt[it % 2]
                C2.op("scalar", lambda e, sl=sl: e.activation(sl[:, 0:PANEL], bank[:, 0:PANEL],
                                                              ACT.Identity), sem="sD")
                evk = C2.cnt["sD"]
                bank_free[it] = ("sD", evk)
                if kind in ("q", "k"):
                    C2.wait("tensor", "sD", evk)
                    C2.wait("tensor", *ropefree[0])
                    C2.op("tensor", lambda e, sl=sl: e.matmul(ps[4][:, 0:PANEL], rmat[:],
                                                              sl[:, 0:PANEL], start=True, stop=True), sem="tD")
                    rmm = C2.cnt["tD"]
                    C2.wait("scalar", "tD", rmm)
                    if kind == "q":
                        dst = qt()[:, h * T + pan2 * PANEL: h * T + pan2 * PANEL + PANEL]
                    else:
                        dst = pt3[:, 0:PANEL]
                        C2.wait("scalar", *vtfree[0])
                    C2.op("scalar", lambda e, dst=dst: e.activation(dst, ps[4][:, 0:PANEL], ACT.Identity),
                          sem="sD")
                    ropefree[0] = ("sD", C2.cnt["sD"])
                    revk = C2.cnt["sD"]
                    if kind == "k":
                        C2.wait("sync", "sD", revk)
                        C2.dma("sync", "st", g["agb1_k_i"][h * 128:(h + 1) * 128,
                                                           pan2 * PANEL:pan2 * PANEL + PANEL], pt3[:, 0:PANEL])
                        vtfree[0] = ("st", C2.cnt["st"])
                    C2.wait("vector", "sD", revk)
                    C2.wait("vector", *sqfree[0])
                    srcv = dst
                    C2.op("vector", lambda e, srcv=srcv: e.tensor_tensor(pt2[:, 0:PANEL], srcv, srcv,
                                                                         ALU.mult), sem="vD")
                    C2.wait("tensor", "vD", C2.cnt["vD"])
                    C2.wait("tensor", *nrmfree[0])
                    C2.op("tensor", lambda e: e.matmul(pl[0][0:1, 0:PANEL], onesb[:], pt2[:, 0:PANEL],
                                                       start=True, stop=True), sem="tD")
                    sqfree[0] = ("tD", C2.cnt["tD"])
                    C2.wait("scalar", "tD", C2.cnt["tD"])
                    C2.wait("scalar", *linvfree[0])
                    C2.op("scalar", lambda e: e.activation(
                        linv[0:1, 0:PANEL], pl[0][0:1, 0:PANEL], ACT.Identity), sem="sD")
                    nrmfree[0] = ("sD", C2.cnt["sD"])
                    C2.wait("vector", "sD", C2.cnt["sD"])
                    C2.op("vector", lambda e: e.tensor_reduce(
                        mtmp[0:1, 2:3], linv[0:1, 0:PANEL], mybir.AxisListType.X, ALU.max))
                    C2.drain("vector")
                    mrow_dst = mqrow if kind == "q" else mkrow
                    C2.op("vector", lambda e, h=h, mrow_dst=mrow_dst: e.tensor_tensor(
                        mrow_dst[0:1, h:h + 1], mrow_dst[0:1, h:h + 1], mtmp[0:1, 2:3], ALU.max),
                        sem="vD")
                    C2.drain("vector")
                    linvfree[0] = ("vD", C2.cnt["vD"])
                    ptfree[it] = ("tD", rmm)
                else:
                    lasttr = 0
                    for j in range(TPP):
                        C2.wait("tensor", "sD", evk)
                        C2.wait("tensor", *trfree[j % 2])
                        C2.op("tensor", lambda e, j=j, sl=sl: e.transpose(
                            psT(4 + j % 2), sl[:, j * 128:(j + 1) * 128], identb[:]), sem="tD")
                        lasttr = C2.cnt["tD"]
                        C2.wait("scalar", "tD", lasttr)
                        if j == 0:
                            C2.wait("scalar", *vtfree[0])
                        C2.op("scalar", lambda e, j=j: e.activation(
                            pt3[:, j * 128:(j + 1) * 128], psT(4 + j % 2), ACT.Identity), sem="sD")
                        trfree[j % 2] = ("sD", C2.cnt["sD"])
                    C2.wait("sync", "sD", C2.cnt["sD"])
                    C2.dma("sync", "st",
                           g["agb1_v_i"][pan2 * PANEL:(pan2 + 1) * PANEL, h * 128:(h + 1) * 128]
                           .rearrange("(j p) c -> p j c", p=128),
                           pt3[:, 0:PANEL].rearrange("p (j cc) -> p j cc", cc=128))
                    vtfree[0] = ("st", C2.cnt["st"])
                    ptfree[it] = ("tD", lasttr)

            gemm_panel(C, pan, [(v, h) for _, v, h in jobs], 0, epi, lambda it: bank_free[it])
            C.barrier()
        C.wait("sync", "vD", C.cnt["vD"])
        C.dma("sync", "st", g["agb1_nk_i"][0:1, 0:NKV], mkrow[0:1, 0:NKV])
        C.marks["st_agb1"] = C.cnt["st"]

    def ph_trigger_agb1(ctx):
        C = ctx
        C.wait("gpsimd", "st", C.marks["st_agb1"])
        C.collective("ccb1", g["agb1_k_i"], g["agb1_k_o"])
        C.collective("ccb1", g["agb1_v_i"], g["agb1_v_o"])
        C.collective("ccb1", g["agb1_nk_i"], g["agb1_nk_o"])

    def run_all(ctx):
        ph_prep(ctx)
        ctx.barrier()
        ph_wcast(ctx)
        ph_trigger1(ctx)
        ph_qzsz(ctx)
        ctx.barrier()
        ph_dequant(ctx)
        ctx.barrier()
        ph_gather(ctx)
        ctx.barrier()
        ph_trigger2(ctx)
        ph_lnqkv(ctx)
        ph_trigger_aga(ctx)
        ctx.barrier()
        mprep(ctx, "a")
        ctx.barrier()
        ph_attn(ctx, "a")
        ctx.barrier()
        ph_outproj(ctx, "a", x_in, out_ext if dbg_stop == 1 else res1)
        ctx.barrier()
        if dbg_stop == 1:
            return
        ph_ffn(ctx, res1, out_ext if dbg_stop == 2 else res2)
        if dbg_stop == 2:
            return
        ph_bproj(ctx, res2)
        ph_trigger_agb1(ctx)
        ctx.barrier()
        mprep(ctx, "b")
        ctx.barrier()
        ph_attn(ctx, "b")
        ctx.barrier()
        ph_outproj(ctx, "b", res2, out_ext if dbg_stop == 3 else res3)
        ctx.barrier()
        if dbg_stop == 3:
            return
        ph_ffn(ctx, res3, out_ext)

    with es:
        with nc.Block() as block:
            starters = dict(sync=block.sync, tensor=block.tensor, vector=block.vector,
                            scalar=block.scalar, gpsimd=block.gpsimd)
            for role in ROLES:
                def make(role):
                    def fn(eng):
                        ctx = Ctx(nc, cfg, sems, B, dbg_stop)
                        ctx.role = role
                        ctx.eng = eng
                        run_all(ctx)
                        ctx.barrier()
                    return fn
                starters[role](make(role))
    return nc


def kernel_generic(cfg, I, dbg_stop=None):
    in_maps = stage_inputs(cfg, I)
    nc = build_nc(cfg, dbg_stop=dbg_stop)
    res = run_bass_kernel_spmd(nc, in_maps, core_ids=list(range(NCORES)))
    c = derived(cfg)
    out = np.concatenate([np.asarray(r["out"]) for r in res.results], axis=0)
    return out.reshape(cfg["B"], cfg["S"], cfg["D"])



def kernel(L0_attn_a_x, L0_attn_a_attn_qkvw, L0_attn_a_attn_qkvb, L0_attn_a_attn_ow,
           L0_attn_a_attn_ob, L0_attn_b_freqs_cos, L0_attn_b_freqs_sin, L0_attn_b_wq,
           L0_attn_b_wk, L0_attn_b_wv, L0_attn_b_wo, L0_attn_b_k_cache, L0_attn_b_v_cache,
           L0_attn_b_block_idx, L0_ffn_gate_qweight, L0_ffn_gate_scales, L0_ffn_gate_qzeros,
           L0_ffn_gate_g_idx, L0_ffn_up_qweight, L0_ffn_up_scales, L0_ffn_up_qzeros,
           L0_ffn_up_g_idx, L0_ffn_down_qweight, L0_ffn_down_scales, L0_ffn_down_qzeros,
           L0_ffn_down_g_idx):
    I = dict(
        x=L0_attn_a_x, qkvw=L0_attn_a_attn_qkvw, qkvb=L0_attn_a_attn_qkvb,
        ow=L0_attn_a_attn_ow, ob=L0_attn_a_attn_ob,
        fcos=L0_attn_b_freqs_cos, fsin=L0_attn_b_freqs_sin,
        wq=L0_attn_b_wq, wk=L0_attn_b_wk, wv=L0_attn_b_wv, wo=L0_attn_b_wo,
        k_cache=L0_attn_b_k_cache, v_cache=L0_attn_b_v_cache, block_idx=L0_attn_b_block_idx,
        gq=L0_ffn_gate_qweight, gsc=L0_ffn_gate_scales, gqz=L0_ffn_gate_qzeros,
        g_gidx=L0_ffn_gate_g_idx,
        uq=L0_ffn_up_qweight, usc=L0_ffn_up_scales, uqz=L0_ffn_up_qzeros,
        u_gidx=L0_ffn_up_g_idx,
        dq=L0_ffn_down_qweight, dsc=L0_ffn_down_scales, dqz=L0_ffn_down_qzeros,
        d_gidx=L0_ffn_down_g_idx,
    )
    out = kernel_generic(full_cfg(), I)
    return out.astype(np.float32)


# version 03
# speedup vs baseline: 1.3729x; 1.3024x over previous
import math
import numpy as np
import ml_dtypes
import concourse.bass as bass
import concourse.mybir as mybir
from concourse.bass_utils import run_bass_kernel_spmd
from contextlib import ExitStack

F32 = mybir.dt.float32
BF16 = mybir.dt.bfloat16
I32 = mybir.dt.int32
U32 = mybir.dt.uint32
ACT = mybir.ActivationFunctionType
ALU = mybir.AluOpType
BF = ml_dtypes.bfloat16

NCORES = 8


def full_cfg():
    return dict(
        B=2, S=4096, D=4096, NH=32, HD=128, NKV=8,
        INTER=11008, GROUP=128,
        NBLK=128, BLOCK=256, TOPK=16,
    )


def mini_cfg():
    return dict(
        B=2, S=512, D=1024, NH=8, HD=128, NKV=4,
        INTER=2048, GROUP=128,
        NBLK=16, BLOCK=128, TOPK=4,
    )


def derived(cfg):
    c = dict(cfg)
    c["T"] = cfg["B"] * cfg["S"] // NCORES
    c["NTT"] = c["T"] // 128
    c["PANEL"] = min(512, c["T"])
    c["NPAN"] = c["T"] // c["PANEL"]
    c["TPP"] = c["PANEL"] // 128
    c["KC"] = cfg["D"] // 128
    c["KCS"] = c["KC"] // NCORES
    c["NT_I"] = cfg["INTER"] // 128
    c["KVG"] = cfg["NH"] // cfg["NKV"]
    c["R_PER_KVH"] = NCORES // cfg["NKV"]
    c["CACHE_T"] = cfg["TOPK"] * cfg["BLOCK"]
    c["NEW_T"] = cfg["S"]
    c["KV_B"] = c["CACHE_T"] + c["NEW_T"]
    c["DP"] = cfg["D"] // 512
    c["QF"] = cfg["NH"] * cfg["HD"]
    c["KVF"] = cfg["NKV"] * cfg["HD"]
    c["WG_KC"] = c["KCS"]
    c["DQ_COLS"] = cfg["D"] // NCORES
    c["GRP_I"] = cfg["INTER"] // cfg["GROUP"]
    c["GRP_D"] = cfg["D"] // cfg["GROUP"]
    assert c["T"] % 128 == 0 and c["KCS"] >= 1 and cfg["D"] % 512 == 0
    assert c["CACHE_T"] % (c["R_PER_KVH"] * 128) == 0
    return c



def ftmaj(w):
    K, N = w.shape
    return np.ascontiguousarray(
        w.reshape(K // 128, 128, N // 128, 128).transpose(2, 1, 0, 3).reshape(N // 128 * 128, K))


def stage_inputs(cfg, I):
    c = derived(cfg)
    D, NH, HD, NKV = cfg["D"], cfg["NH"], cfg["HD"], cfg["NKV"]
    T = c["T"]
    x = np.asarray(I["x"]).reshape(-1, D)
    qkvw = np.asarray(I["qkvw"])
    qkvb = np.asarray(I["qkvb"]).reshape(3 * NH, HD).T
    ow = np.asarray(I["ow"])
    ob = np.asarray(I["ob"]).reshape(D // 128, 128).T
    wqT = np.ascontiguousarray(np.asarray(I["wq"]).T)
    wkT = np.ascontiguousarray(np.asarray(I["wk"]).T)
    wvT = np.ascontiguousarray(np.asarray(I["wv"]).T)
    woT = np.ascontiguousarray(np.asarray(I["wo"]).T)
    kcache = np.asarray(I["k_cache"])
    vcache = np.asarray(I["v_cache"])
    blkidx = np.asarray(I["block_idx"]).reshape(1, -1)
    fcos = np.asarray(I["fcos"]).reshape(1, HD // 2)
    fsin = np.asarray(I["fsin"]).reshape(1, HD // 2)

    gq, gsc, gqz = np.asarray(I["gq"]), np.asarray(I["gsc"]), np.asarray(I["gqz"])
    uq, usc, uqz = np.asarray(I["uq"]), np.asarray(I["usc"]), np.asarray(I["uqz"])
    dq, dsc, dqz = np.asarray(I["dq"]), np.asarray(I["dsc"]), np.asarray(I["dqz"])

    j = (np.arange(128) % 8)
    mask_nib = (np.int64(15) * (np.int64(1) << (4 * j))).astype(np.int64).astype(np.uint32).view(np.int32)[:, None]
    pow16 = (16.0 ** (-j.astype(np.float64))).astype(np.float32)[:, None]
    identf = np.eye(128, dtype=np.float32)
    identb = np.eye(128, dtype=np.float32).astype(BF)
    mup = np.zeros((128, 128), np.float32)
    mdn = np.zeros((128, 128), np.float32)
    ii = np.arange(64)
    mup[2 * ii, 2 * ii + 1] = 1.0
    mdn[2 * ii + 1, 2 * ii] = 1.0

    KR = D // NCORES
    QR = c["QF"] // NCORES
    in_maps = []
    for r in range(NCORES):
        kvh = r // c["R_PER_KVH"]
        tpart = r % c["R_PER_KVH"]
        ct = c["CACHE_T"] // c["R_PER_KVH"]
        m = dict(
            x=np.ascontiguousarray(x[r * T:(r + 1) * T]),
            qkvw=np.ascontiguousarray(ftmaj(qkvw[r * KR:(r + 1) * KR])),
            qkvb=np.ascontiguousarray(qkvb.astype(np.float32)),
            ow=np.ascontiguousarray(ow[r * KR:(r + 1) * KR]),
            ob=np.ascontiguousarray(ob.astype(np.float32)),
            ob_row=np.ascontiguousarray(np.asarray(I["ob"]).reshape(1, D).astype(np.float32)),
            wqT=np.ascontiguousarray(ftmaj(wqT[r * KR:(r + 1) * KR])),
            wkT=np.ascontiguousarray(ftmaj(wkT[r * KR:(r + 1) * KR])),
            wvT=np.ascontiguousarray(ftmaj(wvT[r * KR:(r + 1) * KR])),
            woT=np.ascontiguousarray(woT[r * QR:(r + 1) * QR]),
            kcache=np.ascontiguousarray(kcache[:, :, kvh, :].reshape(-1, HD)),
            vcache=np.ascontiguousarray(vcache[:, :, kvh, :].reshape(-1, HD)),
            blkidx=np.ascontiguousarray(
                blkidx[:, (tpart * ct) // cfg["BLOCK"]:((tpart + 1) * ct) // cfg["BLOCK"]].astype(np.int32)),
            csin=np.ascontiguousarray(np.stack([np.repeat(fcos.reshape(-1), 2),
                                                np.repeat(fsin.reshape(-1), 2)], axis=1).astype(np.float32)),
            gq=np.ascontiguousarray(gq[r * (KR // 8):(r + 1) * (KR // 8)]),
            gsc=np.ascontiguousarray(gsc[np.asarray(I["g_gidx"]).reshape(-1)[0::128][r * c["KCS"]:(r + 1) * c["KCS"]]]),
            gqz=np.ascontiguousarray(gqz[np.asarray(I["g_gidx"]).reshape(-1)[0::128][r * c["KCS"]:(r + 1) * c["KCS"]]]),
            uq=np.ascontiguousarray(uq[r * (KR // 8):(r + 1) * (KR // 8)]),
            usc=np.ascontiguousarray(usc[np.asarray(I["u_gidx"]).reshape(-1)[0::128][r * c["KCS"]:(r + 1) * c["KCS"]]]),
            uqz=np.ascontiguousarray(uqz[np.asarray(I["u_gidx"]).reshape(-1)[0::128][r * c["KCS"]:(r + 1) * c["KCS"]]]),
            dq=np.ascontiguousarray(dq[:, r * c["DQ_COLS"]:(r + 1) * c["DQ_COLS"]]),
            dsc=np.ascontiguousarray(dsc[np.asarray(I["d_gidx"]).reshape(-1)[0::128]][:, r * c["DQ_COLS"]:(r + 1) * c["DQ_COLS"]]),
            dqz=np.ascontiguousarray(dqz[np.asarray(I["d_gidx"]).reshape(-1)[0::128]][:, r * (c["DQ_COLS"] // 8):(r + 1) * (c["DQ_COLS"] // 8)]),
            mask_nib=mask_nib, pow16=pow16, identf=identf, identb=identb,
            mup=mup, mdn=mdn,
        )
        in_maps.append(m)
    return in_maps



ROLES = ("sync", "tensor", "vector", "scalar", "gpsimd")


class Ctx:
    def __init__(self, nc, cfg, sems, bufs, dbg):
        self.nc = nc
        self.cfg = cfg
        self.sems = sems
        self.bufs = bufs
        self.cnt = {k: 0 for k in sems}
        self.role = None
        self.eng = None
        self.dbg = dbg
        self.marks = {}
        self.outst = {k: 0 for k in sems}

    def bump(self, name, by):
        self.cnt[name] += by
        return self.cnt[name]

    def wait(self, role, name, val=None):
        v = self.cnt[name] if val is None else val
        if self.role == role:
            self.eng.wait_ge(self.sems[name], v)

    def dma(self, role, sem, out, in_, slow=False, fence=True):
        if fence and self.outst[sem] > 0:
            self.wait(role, sem, self.cnt[sem])
            self.outst[sem] = 0
        self.bump(sem, 16)
        self.outst[sem] += 1
        if self.role == role:
            self.eng.dma_start(out=out, in_=in_, allow_slow_non_contiguous=slow).then_inc(self.sems[sem], 16)

    def idma(self, role, sem, out, in_dram, idx_ap, fence=True):
        if fence and self.outst[sem] > 0:
            self.wait(role, sem, self.cnt[sem])
            self.outst[sem] = 0
        self.bump(sem, 16)
        self.outst[sem] += 1
        if self.role == role:
            self.eng.indirect_dma_start(
                out=out, out_offset=None, in_=in_dram,
                in_offset=bass.IndirectOffsetOnAxis(ap=idx_ap, axis=0),
            ).then_inc(self.sems[sem], 16)

    def op(self, role, fn, sem=None):
        if sem is not None:
            self.bump(sem, 1)
        if self.role == role:
            ins = fn(self.eng)
            if sem is not None:
                ins.then_inc(self.sems[sem], 1)

    def drain(self, role):
        if self.role == role:
            self.eng.drain()

    DMA_SEMS_SYNC = ("ld", "st", "ld0", "ld1", "ld2", "ld3", "ld4", "ld5")

    def barrier(self):
        for s in self.DMA_SEMS_SYNC:
            if self.cnt[s]:
                self.wait("sync", s, self.cnt[s])
                self.outst[s] = 0
        if self.cnt["gst"]:
            self.wait("gpsimd", "gst", self.cnt["gst"])
            self.outst["gst"] = 0
        eng_sem = dict(sync="qD", tensor="tD", vector="vD", scalar="sD", gpsimd="gD")
        for r, sname in eng_sem.items():
            self.bump(sname, 1)
            if self.role == r:
                if r != "sync":
                    self.eng.drain()
                self.eng.sem_inc(self.sems[sname], 1)
        for r in ROLES:
            for r2, sname in eng_sem.items():
                if r2 != r:
                    self.wait(r, sname, self.cnt[sname])

    def collective(self, sem, in_ap, out_ap, groups=None):
        self.bump(sem, 1)
        if self.role == "gpsimd":
            rg = [list(range(NCORES))] if groups is None else groups
            self.eng.collective_compute(
                "AllGather", ALU.bypass, replica_groups=rg,
                ins=[in_ap.opt()], outs=[out_ap.opt()],
            ).then_inc(self.sems[sem], 1)



def build_nc(cfg, dbg_stop=None):
    c = derived(cfg)
    D, NH, HD, NKV, INTER = cfg["D"], cfg["NH"], cfg["HD"], cfg["NKV"], cfg["INTER"]
    T, PANEL, NPAN, TPP, KC, KCS = c["T"], c["PANEL"], c["NPAN"], c["TPP"], c["KC"], c["KCS"]
    NTT, NT_I, QF, KVF = c["NTT"], c["NT_I"], c["QF"], c["KVF"]
    KVG, RPK = c["KVG"], c["R_PER_KVH"]
    CT = c["CACHE_T"] // RPK
    CACHE_T, NEW_T, KV_B = c["CACHE_T"], c["NEW_T"], c["KV_B"]
    NFT_QKV = 3 * D // 128
    NFT_Q, NFT_KV = QF // 128, KVF // 128
    DP = D // 512
    KR = D // NCORES
    QR = QF // NCORES
    DQC = c["DQ_COLS"]
    GRP_I = c["GRP_I"]
    SCALE = 1.0 / math.sqrt(HD)

    nc = bass.Bass("TRN2", target_bir_lowering=False, debug=False)

    def din(name, shape, dt=F32):
        return nc.dram_tensor(name, shape, dt, kind="ExternalInput").ap()

    x_in = din("x", [T, D])
    qkvw_in = din("qkvw", [NFT_QKV * 128, KCS * 128])
    qkvb_in = din("qkvb", [128, 3 * NH])
    ow_in = din("ow", [KR, D])
    ob_in = din("ob", [128, D // 128])
    obrow_in = din("ob_row", [1, D])
    wqT_in = din("wqT", [NFT_Q * 128, KCS * 128])
    wkT_in = din("wkT", [NFT_KV * 128, KCS * 128])
    wvT_in = din("wvT", [NFT_KV * 128, KCS * 128])
    woT_in = din("woT", [QR, D])
    kcache_in = din("kcache", [cfg["NBLK"] * cfg["BLOCK"], HD])
    vcache_in = din("vcache", [cfg["NBLK"] * cfg["BLOCK"], HD])
    blkidx_in = din("blkidx", [1, CT // cfg["BLOCK"]], I32)
    csin_in = din("csin", [128, 2])
    gq_in = din("gq", [KR // 8, INTER], I32)
    gsc_in = din("gsc", [KCS, INTER])
    gqz_in = din("gqz", [KCS, INTER // 8], I32)
    uq_in = din("uq", [KR // 8, INTER], I32)
    usc_in = din("usc", [KCS, INTER])
    uqz_in = din("uqz", [KCS, INTER // 8], I32)
    dq_in = din("dq", [INTER // 8, DQC], I32)
    dsc_in = din("dsc", [GRP_I, DQC])
    dqz_in = din("dqz", [GRP_I, DQC // 8], I32)
    mask_in = din("mask_nib", [128, 1], I32)
    pow16_in = din("pow16", [128, 1])
    identf_in = din("identf", [128, 128])
    identb_in = din("identb", [128, 128], BF16)
    mup_in = din("mup", [128, 128])
    mdn_in = din("mdn", [128, 128])

    out_ext = nc.dram_tensor("out", [T, D], F32, kind="ExternalOutput").ap()

    def dram(name, shape, dt):
        return nc.dram_tensor(name, shape, dt).ap()

    res1 = dram("res1", [T, D], F32)
    res2 = dram("res2", [T, D], F32)
    res3 = dram("res3", [T, D], F32)
    agw_qkvw_i = dram("agw_qkvw_i", [NFT_QKV * 128, KCS * 128], BF16)
    agw_qkvw_o = dram("agw_qkvw_o", [NCORES * NFT_QKV * 128, KCS * 128], BF16)
    agw_wqT_i = dram("agw_wqT_i", [NFT_Q * 128, KCS * 128], BF16)
    agw_wqT_o = dram("agw_wqT_o", [NCORES * NFT_Q * 128, KCS * 128], BF16)
    agw_wkT_i = dram("agw_wkT_i", [NFT_KV * 128, KCS * 128], BF16)
    agw_wkT_o = dram("agw_wkT_o", [NCORES * NFT_KV * 128, KCS * 128], BF16)
    agw_wvT_i = dram("agw_wvT_i", [NFT_KV * 128, KCS * 128], BF16)
    agw_wvT_o = dram("agw_wvT_o", [NCORES * NFT_KV * 128, KCS * 128], BF16)
    agw_ow_i = dram("agw_ow_i", [KR, D], BF16)
    agw_ow_o = dram("agw_ow_o", [D, D], BF16)
    agw_woT_i = dram("agw_woT_i", [QR, D], BF16)
    agw_woT_o = dram("agw_woT_o", [QF, D], BF16)
    agw_g_i = dram("agw_g_i", [NT_I * 128, KCS * 128], BF16)
    agw_g_o = dram("agw_g_o", [NCORES * NT_I * 128, KCS * 128], BF16)
    agw_u_i = dram("agw_u_i", [NT_I * 128, KCS * 128], BF16)
    agw_u_o = dram("agw_u_o", [NCORES * NT_I * 128, KCS * 128], BF16)
    agw_d_i = dram("agw_d_i", [INTER, DQC], BF16)
    agw_d_o = dram("agw_d_o", [NCORES * INTER, DQC], BF16)
    aga_k_i = dram("aga_k_i", [D, T], BF16)
    aga_k_o = dram("aga_k_o", [4 * D, T], BF16)
    aga_v_i = dram("aga_v_i", [T, D], BF16)
    aga_v_o = dram("aga_v_o", [4 * T, D], BF16)
    aga_nk_i = dram("aga_nk_i", [1, NH], F32)
    aga_nk_o = dram("aga_nk_o", [4 * 1, NH], F32)
    agb1_k_i = dram("agb1_k_i", [KVF, T], BF16)
    agb1_k_o = dram("agb1_k_o", [NCORES * KVF, T], BF16)
    agb1_v_i = dram("agb1_v_i", [T, KVF], BF16)
    agb1_v_o = dram("agb1_v_o", [NCORES * T, KVF], BF16)
    agb1_nk_i = dram("agb1_nk_i", [1, NKV], F32)
    agb1_nk_o = dram("agb1_nk_o", [NCORES * 1, NKV], F32)
    agb2_k_i = dram("agb2_k_i", [HD, CT], BF16)
    agb2_k_o = dram("agb2_k_o", [NCORES * HD, CT], BF16)
    agb2_v_i = dram("agb2_v_i", [CT, HD], BF16)
    agb2_v_o = dram("agb2_v_o", [NCORES * CT, HD], BF16)
    agb2_nk_i = dram("agb2_nk_i", [1, 1], F32)
    agb2_nk_o = dram("agb2_nk_o", [NCORES * 1, 1], F32)
    szg_d = dram("szg_d", [KCS, INTER], F32)
    szu_d = dram("szu_d", [KCS, INTER], F32)
    szd_d = dram("szd_d", [GRP_I, DQC], F32)

    es = ExitStack()
    sb = lambda name, shape, dt: es.enter_context(nc.sbuf_tensor(name, shape, dt))
    ps_ = lambda name, shape: es.enter_context(nc.psum_tensor(name, shape, F32))

    POOLB = 161 * 1024
    pool = sb("pool", [128, POOLB], mybir.dt.uint8)

    def carve(off, dt, shape):
        nbytes = shape[1] * {F32: 4, BF16: 2, I32: 4, U32: 4}[dt]
        assert off % 64 == 0 and off + nbytes <= POOLB, (off, nbytes)
        return pool[0:shape[0], off:off + nbytes].bitcast(dt)

    wb = [sb("wb0", [128, KC * 128], BF16), sb("wb1", [128, KC * 128], BF16)]
    rs = [sb("rs0", [128, 512], BF16), sb("rs1", [128, 512], BF16)]
    pt = [sb("pt0", [128, 512], BF16), sb("pt1", [128, 512], BF16)]
    pt2 = sb("pt2", [128, 512], BF16)
    pt3 = sb("pt3", [128, 512], BF16)
    obrow_b = sb("obrow_b", [1, cfg["D"]], BF16)
    onesrow_b = sb("onesrow_b", [1, 128], BF16)
    ev = [sb("ev0", [128, 512], F32), sb("ev1", [128, 512], F32),
          sb("ev2", [128, 512], F32), sb("ev3", [128, 512], F32)]
    identf = sb("identf_s", [128, 128], F32)
    identb = sb("identb_s", [128, 128], BF16)
    mupt = sb("mup_s", [128, 128], F32)
    mdnt = sb("mdn_s", [128, 128], F32)
    rmat = sb("rmat", [128, 128], BF16)
    rmatf = sb("rmatf", [128, 128], F32)
    qkvb_t = sb("qkvb_s", [128, 3 * NH], F32)
    ob_t = sb("ob_s", [128, D // 128], F32)
    maskt = sb("maskt", [128, 1], I32)
    pow16t = sb("pow16t", [128, 1], F32)
    cs_t = sb("cs_t", [128, 2], F32)
    onesb = sb("onesb", [128, 1], BF16)
    onesf = sb("onesf", [128, 1], F32)
    mqrow = sb("mqrow", [1, 64], F32)
    mkrow = sb("mkrow", [1, 64], F32)
    mtmp = sb("mtmp", [1, 256], F32)
    mbias = sb("mbias", [128, NH], F32)
    lnsc = sb("lnsc", [128, 8], F32)
    gidx = sb("gidx", [128, CT // 128], U32)
    gidxf = sb("gidxf", [128, CT // 128], F32)
    iotap = sb("iotap", [128, 1], I32)
    onesrow = sb("onesrow", [1, 128], F32)
    blki = sb("blki", [1, CT // cfg["BLOCK"]], F32)
    blkii = sb("blkii", [1, CT // cfg["BLOCK"]], I32)
    blkb = sb("blkb", [128, CT // cfg["BLOCK"]], F32)

    linv = sb("linv", [128, 512], F32)

    ps = [ps_(f"ps{i}", [128, 512]) for i in range(6)]
    pl = [ps_(f"pl{i}", [1, 512]) for i in range(2)]

    sems = {}
    for s in ("ld", "st", "gst", "tD", "vD", "sD", "gD", "qD",
              "ld0", "ld1", "ld2", "ld3", "ld4", "ld5",
              "ccq", "cco", "cca", "ccf", "ccw", "ccb2", "ccwo", "ccb1"):
        sems[s] = es.enter_context(nc.semaphore(s))

    bufs = dict(locals())
    return _emit(nc, c, es, bufs, sems, dbg_stop)


def _emit(nc, c, es, B, sems, dbg_stop):
    cfg = c
    D, NH, HD, NKV, INTER = c["D"], c["NH"], c["HD"], c["NKV"], c["INTER"]
    T, PANEL, NPAN, TPP, KC, KCS = c["T"], c["PANEL"], c["NPAN"], c["TPP"], c["KC"], c["KCS"]
    NTT, NT_I, QF, KVF = c["NTT"], c["NT_I"], c["QF"], c["KVF"]
    KVG, RPK = c["KVG"], c["R_PER_KVH"]
    CT = c["CACHE_T"] // RPK
    CACHE_T, NEW_T, KV_B = c["CACHE_T"], c["NEW_T"], c["KV_B"]
    BLOCK = c["BLOCK"]
    NFT_QKV, NFT_Q, NFT_KV = 3 * D // 128, QF // 128, KVF // 128
    DP = D // 512
    KR, QR, DQC, GRP_I = D // NCORES, QF // NCORES, c["DQ_COLS"], c["GRP_I"]
    SCALE = 1.0 / math.sqrt(HD)
    NPP = PANEL

    g = B
    x_in, out_ext = g["x_in"], g["out_ext"]
    res1, res2, res3 = g["res1"], g["res2"], g["res3"]
    pool, wb, rs, pt, ev = g["pool"], g["wb"], g["rs"], g["pt"], g["ev"]
    identf, identb, mupt, mdnt, rmat, rmatf = g["identf"], g["identb"], g["mupt"], g["mdnt"], g["rmat"], g["rmatf"]
    qkvb_t, ob_t, maskt, pow16t, cs_t = g["qkvb_t"], g["ob_t"], g["maskt"], g["pow16t"], g["cs_t"]
    onesb, onesf, mbias = g["onesb"], g["onesf"], g["mbias"]
    mqrow, mkrow, mtmp = g["mqrow"], g["mkrow"], g["mtmp"]
    lnsc, gidx, blki, blkb, linv = g["lnsc"], g["gidx"], g["blki"], g["blkb"], g["linv"]
    gidxf, iotap, onesrow, blkii = g["gidxf"], g["iotap"], g["onesrow"], g["blkii"]
    pt2, pt3, obrow_b, onesrow_b = g["pt2"], g["pt3"], g["obrow_b"], g["onesrow_b"]
    ps, pl = g["ps"], g["pl"]

    LNT_OFF, QT_OFF, KV_OFF, XT_OFF, XNB_OFF = 0, 32 * 1024, 96 * 1024, 121 * 1024, 153 * 1024
    def lnT():
        return pool[0:128, LNT_OFF:LNT_OFF + KC * PANEL * 2].bitcast(BF16)
    def qt():
        return pool[0:128, QT_OFF:QT_OFF + NH * T * 2].bitcast(BF16)
    def hT():
        return pool[0:128, QT_OFF:QT_OFF + NT_I * PANEL * 2].bitcast(BF16)
    def kvK(s):
        off = KV_OFF + s * KV_B * 2
        return pool[0:128, off:off + KV_B * 2].bitcast(BF16)
    def kvV(s):
        off = KV_OFF + (2 + s) * ((KV_B // 128) * 128 * 2)
        return pool[0:128, off:off + (KV_B // 128) * 128 * 2].bitcast(BF16)
    def xt(s):
        off = XT_OFF + s * D * 4
        return pool[0:128, off:off + D * 4].bitcast(F32)
    def xnb():
        return pool[0:128, XNB_OFF:XNB_OFF + D * 2].bitcast(BF16)
    DQN = min(2816, INTER)
    DQNB = DQN * 4
    def dq_q():
        return pool[0:128, 0:DQNB].bitcast(I32)
    def dq_s():
        return pool[0:128, DQNB:2 * DQNB].bitcast(F32)
    def dq_z():
        return pool[0:128, 2 * DQNB:3 * DQNB].bitcast(F32)
    def dq_o():
        return pool[0:128, 3 * DQNB:3 * DQNB + DQN * 2].bitcast(BF16)
    def dqf():
        return pool[0:128, 3 * DQNB + DQN * 2:4 * DQNB + DQN * 2].bitcast(F32)
    GOFF = 4 * DQNB + DQN * 2
    def gk(s):
        off = GOFF + s * HD * 4
        return pool[0:128, off:off + HD * 4].bitcast(F32)
    def gkv(s):
        off = GOFF + (2 + s) * HD * 4
        return pool[0:128, off:off + HD * 4].bitcast(F32)
    def gkb(s):
        off = GOFF + 4 * HD * 4 + s * HD * 2
        return pool[0:128, off:off + HD * 2].bitcast(BF16)
    def gvb(s):
        off = GOFF + 4 * HD * 4 + (2 + s) * HD * 2
        return pool[0:128, off:off + HD * 2].bitcast(BF16)
    def gsq(s):
        off = GOFF + 4 * HD * 4 + 4 * HD * 2 + s * HD * 2
        return pool[0:128, off:off + HD * 2].bitcast(BF16)

    agw_qkvw = g["agw_qkvw_o"].rearrange("(r f p) k -> r f p k", r=NCORES, p=128)
    agw_wqT = g["agw_wqT_o"].rearrange("(r f p) k -> r f p k", r=NCORES, p=128)
    agw_wkT = g["agw_wkT_o"].rearrange("(r f p) k -> r f p k", r=NCORES, p=128)
    agw_wvT = g["agw_wvT_o"].rearrange("(r f p) k -> r f p k", r=NCORES, p=128)
    agw_g = g["agw_g_o"].rearrange("(r f p) k -> r f p k", r=NCORES, p=128)
    agw_u = g["agw_u_o"].rearrange("(r f p) k -> r f p k", r=NCORES, p=128)
    agw_d = g["agw_d_o"].rearrange("(r i) d -> r i d", r=NCORES)
    aga_k = g["aga_k_o"].rearrange("(r f) t -> r f t", r=4)
    aga_v = g["aga_v_o"].rearrange("(r t) f -> r t f", r=4)
    aga_nk = g["aga_nk_o"]
    agb1_k = g["agb1_k_o"].rearrange("(r f) t -> r f t", r=NCORES)
    agb1_v = g["agb1_v_o"].rearrange("(r t) f -> r t f", r=NCORES)
    agb1_nk = g["agb1_nk_o"]
    agb2_k = g["agb2_k_o"].rearrange("(r f) t -> r f t", r=NCORES)
    agb2_v = g["agb2_v_o"].rearrange("(r t) f -> r t f", r=NCORES)
    agb2_nk = g["agb2_nk_o"]

    def lhsT_view(agw_view, ft):
        return agw_view[:, ft].rearrange("r p k -> p r k")

    RES = [x_in, res1, res2, res3, out_ext]

    def psT(i):
        return ps[i][:, 0:64].bitcast(BF16)

    def ph_prep(ctx):
        C = ctx
        C.dma("sync", "ld", identf[:], g["identf_in"])
        C.dma("sync", "ld", identb[:], g["identb_in"], fence=False)
        C.dma("sync", "ld", mupt[:], g["mup_in"], fence=False)
        C.dma("sync", "ld", mdnt[:], g["mdn_in"], fence=False)
        C.dma("sync", "ld", qkvb_t[:], g["qkvb_in"], fence=False)
        C.dma("sync", "ld", ob_t[:], g["ob_in"], fence=False)
        C.dma("sync", "ld", maskt[:], g["mask_in"], fence=False)
        C.dma("sync", "ld", pow16t[:], g["pow16_in"], fence=False)
        C.dma("sync", "ld", cs_t[:], g["csin_in"], fence=False)
        C.dma("sync", "ld", blkii[:], g["blkidx_in"], fence=False)
        C.dma("gpsimd", "gst", obrow_b[0:1, 0:D], g["obrow_in"])
        ld_small = C.cnt["ld"]

        C.wait("vector", "ld", ld_small)
        C.op("vector", lambda e: e.memset(onesf[:], 1.0))
        C.op("vector", lambda e: e.memset(onesb[:], 1.0))
        C.op("vector", lambda e: e.tensor_scalar(rmatf[:], identf[:], cs_t[:, 0:1], None, ALU.mult))
        C.op("vector", lambda e: e.tensor_scalar(mupt[:], mupt[:], cs_t[:, 1:2], None, ALU.mult))
        C.op("vector", lambda e: e.tensor_scalar(mdnt[:], mdnt[:], cs_t[:, 1:2], None, ALU.mult))
        C.drain("vector")
        C.op("vector", lambda e: e.tensor_tensor(rmatf[:], rmatf[:], mupt[:], ALU.add))
        C.drain("vector")
        C.op("vector", lambda e: e.tensor_tensor(rmatf[:], rmatf[:], mdnt[:], ALU.subtract), sem="vD")
        vD_rmat = C.cnt["vD"]
        C.wait("scalar", "vD", vD_rmat)
        C.op("scalar", lambda e: e.copy(rmat[:], rmatf[:]), sem="sD")

        C.op("vector", lambda e: e.memset(onesrow[:], 1.0))
        C.op("vector", lambda e: e.memset(onesrow_b[:], 1.0))
        C.op("vector", lambda e: e.tensor_copy(blki[:], blkii[:]), sem="vD")
        vD_ones = C.cnt["vD"]

        TPB = BLOCK // 128
        NTAU = CT // 128
        C.wait("tensor", "ld", ld_small)
        C.wait("tensor", "vD", vD_ones)
        C.op("tensor", lambda e: e.matmul(ps[5][:, 0:CT // BLOCK], onesrow[:],
                                          blki[:], start=True, stop=True), sem="tD")
        tD_blk = C.cnt["tD"]
        C.wait("vector", "tD", tD_blk)
        C.op("vector", lambda e: e.tensor_scalar(blkb[:], ps[5][:, 0:CT // BLOCK], float(BLOCK), None, ALU.mult))
        C.op("gpsimd", lambda e: e.iota(iotap[:], pattern=[[0, 1]], base=0,
                                        channel_multiplier=1), sem="gD")
        gD_iota = C.cnt["gD"]
        C.wait("vector", "gD", gD_iota)
        C.drain("vector")
        for tau in range(NTAU):
            sel = tau // TPB
            off = (tau % TPB) * 128
            C.op("vector", lambda e, tau=tau, sel=sel, off=off: e.scalar_tensor_tensor(
                gidxf[:, tau:tau + 1], iotap[:], float(off),
                blkb[:, sel:sel + 1], ALU.add, ALU.add))
        C.drain("vector")
        C.op("vector", lambda e: e.tensor_copy(gidx[:, 0:NTAU], gidxf[:, 0:NTAU]), sem="vD")
        vD_gidx = C.cnt["vD"]
        C.marks.update(ld_small=ld_small, vD_gidx=vD_gidx)

    def ph_gather(ctx):
        C = ctx
        NTAU = CT // 128
        C.op("vector", lambda e: e.memset(mtmp[0:1, 0:1], -3e38))
        C.drain("vector")
        C.wait("gpsimd", "vD", C.marks["vD_gidx"])
        tD0, sD0, vD0, st0 = C.cnt["tD"], C.cnt["sD"], C.cnt["vD"], C.cnt["st"]
        for tau in range(NTAU):
            s = tau % 2
            grp, gi = tau // 4, tau % 4
            if tau >= 2:
                C.wait("gpsimd", "tD", tD0 + 2 * (tau - 2) + 1)
                C.wait("gpsimd", "sD", sD0 + 2 * (tau - 2) + 2 + (tau - 2) // 4)
            C.idma("gpsimd", "gst", gk(s)[:, 0:HD], g["kcache_in"], gidx[:, tau:tau + 1])
            C.idma("gpsimd", "gst", gkv(s)[:, 0:HD], g["vcache_in"], gidx[:, tau:tau + 1], fence=False)
            gk_mark = C.cnt["gst"]
            C.wait("tensor", "gst", gk_mark)
            if tau >= 2:
                C.wait("tensor", "sD", sD0 + 2 * (tau - 2) + 1 + (tau - 2) // 4)
                C.wait("tensor", "st", st0 + 32 * (tau - 1))
            C.op("tensor", lambda e, s=s: e.transpose(ps[4 + s][:, 0:128], gk(s)[:, 0:HD], identf[:]),
                 sem="tD")
            tr_mark = C.cnt["tD"]
            C.wait("scalar", "tD", tr_mark)
            if tau >= 2:
                C.wait("scalar", "st", st0 + 32 * (tau - 1))
            C.op("scalar", lambda e, s=s: e.activation(gkb(s)[:, 0:HD], ps[4 + s][:, 0:128], ACT.Identity),
                 sem="sD")
            kevac = C.cnt["sD"]
            C.wait("scalar", "gst", gk_mark)
            C.op("scalar", lambda e, s=s: e.activation(gvb(s)[:, 0:HD], gkv(s)[:, 0:HD], ACT.Identity), sem="sD")
            vevac = C.cnt["sD"]
            C.wait("vector", "sD", kevac)
            if tau >= 2:
                C.wait("vector", "tD", tD0 + 2 * (tau - 2) + 2)
            C.op("vector", lambda e, s=s: e.tensor_tensor(gsq(s)[:, 0:128], gkb(s)[:, 0:HD], gkb(s)[:, 0:HD],
                                                          ALU.mult), sem="vD")
            C.wait("tensor", "vD", C.cnt["vD"])
            if gi == 0 and grp > 0:
                C.wait("tensor", "sD", sD0 + 2 * tau + grp)
            C.op("tensor", lambda e, s=s, gi=gi: e.matmul(pl[0][0:1, gi * 128:(gi + 1) * 128], onesb[:],
                                                          gsq(s)[:, 0:128], start=True, stop=True), sem="tD")
            C.wait("sync", "sD", kevac)
            C.dma("sync", "st", g["agb2_k_i"][0:HD, tau * 128:(tau + 1) * 128], gkb(s)[:, 0:HD])
            C.wait("sync", "sD", vevac)
            C.dma("sync", "st", g["agb2_v_i"][tau * 128:(tau + 1) * 128, 0:HD], gvb(s)[:, 0:HD])
            if gi == 3 or tau == NTAU - 1:
                C.wait("scalar", "tD", C.cnt["tD"])
                C.op("scalar", lambda e, gi=gi: e.activation(
                    linv[0:1, 0:(gi + 1) * 128], pl[0][0:1, 0:(gi + 1) * 128], ACT.Identity), sem="sD")
                C.wait("vector", "sD", C.cnt["sD"])
                C.op("vector", lambda e, gi=gi: e.tensor_reduce(
                    mtmp[0:1, 1:2], linv[0:1, 0:(gi + 1) * 128], mybir.AxisListType.X, ALU.max))
                C.drain("vector")
                C.op("vector", lambda e: e.tensor_tensor(mtmp[0:1, 0:1], mtmp[0:1, 0:1], mtmp[0:1, 1:2],
                                                         ALU.max), sem="vD")
                C.drain("vector")
        C.wait("sync", "vD", C.cnt["vD"])
        C.dma("sync", "st", g["agb2_nk_i"][0:1, 0:1], mtmp[0:1, 0:1])
        C.marks["st_agb2"] = C.cnt["st"]

    ZW = min(5504, max(INTER, DQC))
    Z0 = GOFF + 4 * 1024
    def zrow(R):
        return pool[0:R, Z0:Z0 + ZW * 4].bitcast(F32)
    def scrow(R):
        return pool[0:R, Z0 + ZW * 4:Z0 + 2 * ZW * 4].bitcast(F32)
    def qzt(R):
        return pool[0:R, Z0 + 2 * ZW * 4:Z0 + 2 * ZW * 4 + (ZW // 8) * 4].bitcast(I32)
    def zqi2(R):
        return pool[0:R, Z0 + 2 * ZW * 4 + 704 * 4:Z0 + 2 * ZW * 4 + 1408 * 4].bitcast(I32)
    def castb(s):
        off = Z0 + 2 * ZW * 4 + 1408 * 4 + s * 8 * 1024
        return pool[0:128, off:off + 8 * 1024].bitcast(BF16)

    def ph_wcast(ctx):
        C = ctx
        jobs = []
        for src, dst, npiece, width in (
            (g["qkvw_in"], g["agw_qkvw_i"], NFT_QKV, KCS * 128),
            (g["ow_in"], g["agw_ow_i"], KR // 128, D),
            (g["wqT_in"], g["agw_wqT_i"], NFT_Q, KCS * 128),
            (g["wkT_in"], g["agw_wkT_i"], NFT_KV, KCS * 128),
            (g["wvT_in"], g["agw_wvT_i"], NFT_KV, KCS * 128),
            (g["woT_in"], g["agw_woT_i"], QR // 128, D),
        ):
            for p in range(npiece):
                jobs.append((src[p * 128:(p + 1) * 128, 0:width], dst[p * 128:(p + 1) * 128, 0:width], width))
        sD0, st0 = C.cnt["sD"], C.cnt["st"]
        ldm = []
        for it, (srcap, dstap, width) in enumerate(jobs):
            s = it % 2
            if it >= 2:
                C.wait("sync", "sD", sD0 + it - 1)
            C.dma("sync", "ld" + str(s), xt(s)[:, 0:width], srcap)
            ldm.append(C.cnt["ld" + str(s)])
            C.wait("scalar", "ld" + str(s), ldm[it])
            if it >= 2:
                C.wait("scalar", "st", st0 + 16 * (it - 1))
            C.op("scalar", lambda e, s=s, w=width: e.activation(castb(s)[:, 0:w], xt(s)[:, 0:w], ACT.Identity),
                 sem="sD")
            C.wait("sync", "sD", C.cnt["sD"])
            C.dma("sync", "st", dstap, castb(s)[:, 0:width])
        C.marks["st_wcast"] = C.cnt["st"]
        n1 = NFT_QKV
        n2 = n1 + KR // 128
        C.marks["st_qkvw"] = st0 + 16 * n1
        C.marks["st_ow"] = st0 + 16 * n2

    def ph_qzsz(ctx):
        C = ctx
        for (qz_in, sc_in, sz_d, R, N) in (
            (g["gqz_in"], g["gsc_in"], g["szg_d"], KCS, INTER),
            (g["uqz_in"], g["usc_in"], g["szu_d"], KCS, INTER),
            (g["dqz_in"], g["dsc_in"], g["szd_d"], GRP_I, DQC),
        ):
            for c0 in range(0, N, ZW):
                w = min(ZW, N - c0)
                C.dma("sync", "ld", qzt(R)[0:R, 0:w // 8], qz_in[0:R, c0 // 8:(c0 + w) // 8])
                C.dma("sync", "ld", scrow(R)[0:R, 0:w], sc_in[0:R, c0:c0 + w])
                C.wait("vector", "ld", C.cnt["ld"])
                C.wait("vector", "st", C.cnt["st"])
                for jj in range(8):
                    C.op("vector", lambda e, jj=jj, R=R, w=w: e.tensor_scalar(
                        zqi2(R)[0:R, 0:w // 8], qzt(R)[0:R, 0:w // 8], 4 * jj, None, ALU.logical_shift_right))
                    C.drain("vector")
                    C.op("vector", lambda e, R=R, w=w: e.tensor_scalar(
                        zqi2(R)[0:R, 0:w // 8], zqi2(R)[0:R, 0:w // 8], 15, None, ALU.bitwise_and))
                    C.drain("vector")
                    C.op("vector", lambda e, jj=jj, R=R, w=w: e.tensor_copy(
                        zrow(R)[0:R, 0:w].rearrange("r (n e) -> r n e", e=8)[:, :, jj:jj + 1],
                        zqi2(R)[0:R, 0:w // 8].unsqueeze(2)))
                    C.drain("vector")
                C.op("vector", lambda e, R=R, w=w: e.tensor_scalar(zrow(R)[0:R, 0:w], zrow(R)[0:R, 0:w],
                                                                   1.0, None, ALU.add))
                C.drain("vector")
                C.op("vector", lambda e, R=R, w=w: e.tensor_tensor(zrow(R)[0:R, 0:w], zrow(R)[0:R, 0:w],
                                                                   scrow(R)[0:R, 0:w], ALU.mult), sem="vD")
                C.wait("sync", "vD", C.cnt["vD"])
                C.dma("sync", "st", sz_d[0:R, c0:c0 + w], zrow(R)[0:R, 0:w])
        C.marks["st_sz"] = C.cnt["st"]

    def ph_trigger1(ctx):
        C = ctx
        C.wait("gpsimd", "st", C.marks["st_qkvw"])
        C.collective("ccq", g["agw_qkvw_i"], g["agw_qkvw_o"])
        C.wait("gpsimd", "st", C.marks["st_ow"])
        C.collective("cco", g["agw_ow_i"], g["agw_ow_o"])

    def ph_dequant(ctx):
        C = ctx
        agg_i = g["agw_g_i"].rearrange("(f p) k -> f p k", p=128).rearrange("f p k -> p f k")
        agu_i = g["agw_u_i"].rearrange("(f p) k -> f p k", p=128).rearrange("f p k -> p f k")
        jobs = []
        for (q_in, sz_d, sc_in, dsti, R, N, isdown) in (
            (g["gq_in"], g["szg_d"], g["gsc_in"], agg_i, KCS, INTER, False),
            (g["uq_in"], g["szu_d"], g["usc_in"], agu_i, KCS, INTER, False),
            (g["dq_in"], g["szd_d"], g["dsc_in"], None, GRP_I, DQC, True),
        ):
            for j in range(R):
                for c0 in range(0, N, DQN):
                    w = min(DQN, N - c0)
                    jobs.append((q_in, sz_d, sc_in, dsti, j, c0, w, isdown))
        vD0 = C.cnt["vD"]
        st_dq0 = C.cnt["st"]
        for it, (q_in, sz_d, sc_in, dsti, j, c0, w, isdown) in enumerate(jobs):
            if it >= 1:
                C.wait("sync", "vD", vD0 + 2 * it)
            C.dma("sync", "ld0", dq_q()[:, 0:w],
                  q_in[16 * j:16 * (j + 1), c0:c0 + w].unsqueeze(1).broadcast_to([16, 8, w]))
            C.dma("sync", "ld0", dq_s()[:, 0:w],
                  sc_in[j:j + 1, c0:c0 + w].broadcast_to([128, w]), fence=False)
            C.dma("sync", "ld0", dq_z()[:, 0:w],
                  sz_d[j:j + 1, c0:c0 + w].broadcast_to([128, w]), fence=False)
            ldmk = C.cnt["ld0"]
            C.wait("vector", "ld0", ldmk)
            if it >= 1:
                C.wait("vector", "st", st_dq0 + 16 * it)
            C.op("vector", lambda e, w=w: e.tensor_tensor(
                dq_q()[:, 0:w], dq_q()[:, 0:w], maskt[:, 0:1].broadcast_to([128, w]), ALU.bitwise_and))
            C.drain("vector")
            C.op("vector", lambda e, w=w: e.scalar_tensor_tensor(
                dqf()[:, 0:w], dq_q()[:, 0:w], pow16t[:, 0:1], dq_s()[:, 0:w],
                ALU.mult, ALU.mult), sem="vD")
            C.drain("vector")
            C.op("vector", lambda e, w=w: e.tensor_tensor(
                dq_o()[:, 0:w], dqf()[:, 0:w], dq_z()[:, 0:w], ALU.subtract), sem="vD")
            C.wait("sync", "vD", C.cnt["vD"])
            if isdown:
                dst = g["agw_d_i"][j * 128:(j + 1) * 128, 0:w]
            else:
                ft0, ft1 = c0 // 128, (c0 + w) // 128
                dst = dsti[:, ft0:ft1, j * 128:(j + 1) * 128]
            C.dma("sync", "st", dst, dq_o()[:, 0:w] if isdown else
                  dq_o()[:, 0:w].rearrange("p (f z) -> p f z", z=128))
        C.marks["st_dq"] = C.cnt["st"]

    def ph_trigger2(ctx):
        C = ctx
        C.wait("gpsimd", "st", C.marks["st_dq"])
        C.collective("ccf", g["agw_g_i"], g["agw_g_o"])
        C.collective("ccf", g["agw_u_i"], g["agw_u_o"])
        C.collective("ccf", g["agw_d_i"], g["agw_d_o"])
        C.collective("ccw", g["agw_wqT_i"], g["agw_wqT_o"])
        C.collective("ccw", g["agw_wkT_i"], g["agw_wkT_o"])
        C.collective("ccw", g["agw_wvT_i"], g["agw_wvT_o"])
        C.wait("gpsimd", "st", C.marks["st_agb2"])
        C.collective("ccb2", g["agb2_k_i"], g["agb2_k_o"])
        C.collective("ccb2", g["agb2_v_i"], g["agb2_v_o"])
        C.collective("ccb2", g["agb2_nk_i"], g["agb2_nk_o"])
        C.collective("ccwo", g["agw_woT_i"], g["agw_woT_o"])

    def ln_panel(ctx, src_rows, pan):
        C = ctx
        norm_marks, ev_marks = {}, {}
        for tt in range(TPP):
            i = pan * TPP + tt
            s = tt % 2
            if tt >= 2:
                C.wait("sync", "sD", norm_marks[tt - 2])
            C.dma("sync", "ld" + str(s), xt(s)[:, 0:D], src_rows[i * 128:(i + 1) * 128, 0:D])
            ldmk = C.cnt["ld" + str(s)]
            C.wait("scalar", "ld" + str(s), ldmk)
            C.op("scalar", lambda e, s=s: e.activation(xnb()[:, 0:D], xt(s)[:, 0:D], ACT.Identity,
                                                       accum_out=lnsc[:, 0:1]))
            C.drain("scalar")
            C.op("scalar", lambda e, s=s: e.activation(xnb()[:, 0:D], xt(s)[:, 0:D], ACT.Square,
                                                       accum_out=lnsc[:, 1:2]), sem="sD")
            C.wait("vector", "sD", C.cnt["sD"])
            C.op("vector", lambda e: e.tensor_scalar(lnsc[:, 2:3], lnsc[:, 0:1], 1.0 / D, None, ALU.mult))
            C.op("vector", lambda e: e.tensor_scalar(lnsc[:, 3:4], lnsc[:, 1:2], 1.0 / D, None, ALU.mult))
            C.drain("vector")
            C.op("vector", lambda e: e.tensor_tensor(lnsc[:, 4:5], lnsc[:, 2:3], lnsc[:, 2:3], ALU.mult))
            C.drain("vector")
            C.op("vector", lambda e: e.tensor_tensor(lnsc[:, 4:5], lnsc[:, 3:4], lnsc[:, 4:5], ALU.subtract))
            C.drain("vector")
            C.op("vector", lambda e: e.tensor_scalar(lnsc[:, 4:5], lnsc[:, 4:5], 1e-5, None, ALU.add), sem="vD")
            C.wait("scalar", "vD", C.cnt["vD"])
            C.op("scalar", lambda e: e.activation(lnsc[:, 5:6], lnsc[:, 4:5], ACT.Sqrt), sem="sD")
            C.wait("vector", "sD", C.cnt["sD"])
            C.op("vector", lambda e: e.reciprocal(lnsc[:, 6:7], lnsc[:, 5:6]))
            C.drain("vector")
            C.op("vector", lambda e: e.tensor_tensor(lnsc[:, 7:8], lnsc[:, 6:7], lnsc[:, 6:7], ALU.mult))
            C.drain("vector")
            C.op("vector", lambda e: e.tensor_tensor(lnsc[:, 7:8], lnsc[:, 4:5], lnsc[:, 7:8], ALU.mult))
            C.drain("vector")
            C.op("vector", lambda e: e.tensor_scalar(lnsc[:, 7:8], lnsc[:, 7:8], -0.5, 1.5, ALU.mult, ALU.add))
            C.drain("vector")
            C.op("vector", lambda e: e.tensor_tensor(lnsc[:, 6:7], lnsc[:, 6:7], lnsc[:, 7:8], ALU.mult))
            C.drain("vector")
            C.op("vector", lambda e: e.tensor_tensor(lnsc[:, 7:8], lnsc[:, 2:3], lnsc[:, 6:7], ALU.mult))
            C.drain("vector")
            C.op("vector", lambda e: e.tensor_scalar(lnsc[:, 7:8], lnsc[:, 7:8], -1.0, None, ALU.mult), sem="vD")
            C.wait("scalar", "vD", C.cnt["vD"])
            C.op("scalar", lambda e, s=s: e.activation(xnb()[:, 0:D], xt(s)[:, 0:D], ACT.Identity,
                                                       bias=lnsc[:, 7:8], scale=lnsc[:, 6:7]), sem="sD")
            norm_marks[tt] = C.cnt["sD"]
            nrm = C.cnt["sD"]
            for kc in range(KC):
                b = kc % 2
                C.wait("tensor", "sD", nrm if kc < 2 else ev_marks[(tt, kc - 2)])
                C.op("tensor", lambda e, b=b, kc=kc: e.transpose(psT(4 + b),
                                                                 xnb()[:, kc * 128:(kc + 1) * 128], identb[:]),
                     sem="tD")
                C.wait("scalar", "tD", C.cnt["tD"])
                C.op("scalar", lambda e, b=b, kc=kc, tt=tt: e.activation(
                    lnT()[:, kc * PANEL + tt * 128: kc * PANEL + (tt + 1) * 128],
                    psT(4 + b), ACT.Identity), sem="sD")
                ev_marks[(tt, kc)] = C.cnt["sD"]
        C.marks["lnT_sD"] = C.cnt["sD"]

    def gemm_panel(ctx, pan, jobs, rhs_off, epi, epi_bank_free, nkc=None, extra_k1=None):
        C = ctx
        NKC = KC if nkc is None else nkc
        mm_done = {}
        for it, (view, ft) in enumerate(jobs):
            s = it % 2
            if it >= 2:
                C.wait("sync", "tD", mm_done[it - 2])
            C.dma("sync", "ld" + str(s), wb[s][:, 0:NKC * 128], lhsT_view(view, ft))
            ldmk = C.cnt["ld" + str(s)]
            C.wait("tensor", "ld" + str(s), ldmk)
            if it >= 2:
                C.wait("tensor", *epi_bank_free(it - 2))
            bank = ps[it % 2]
            for kc in range(NKC):
                C.op("tensor", lambda e, kc=kc, s=s, bank=bank: e.matmul(
                    bank[:, 0:PANEL], wb[s][:, kc * 128:(kc + 1) * 128],
                    lnT()[:, kc * PANEL + rhs_off:kc * PANEL + rhs_off + PANEL],
                    start=(kc == 0), stop=(kc == NKC - 1 and extra_k1 is None)), sem="tD")
            if extra_k1 is not None:
                extra_k1(C, it, bank)
            mm_done[it] = C.cnt["tD"]
            epi(C, bank, ft, pan, it)

    def ph_lnqkv(ctx):
        C = ctx
        C.op("vector", lambda e: e.memset(mqrow[:], -3e38))
        C.op("vector", lambda e: e.memset(mkrow[:], -3e38))
        C.drain("vector")
        C.wait("tensor", "ccq", C.cnt["ccq"])
        C.wait("sync", "ccq", C.cnt["ccq"])
        for pan in range(NPAN):
            ln_panel(C, x_in, pan)
            C.barrier()
            fts = ([("k", NH + h) for h in range(NH)] + [("v", 2 * NH + h) for h in range(NH)]
                   + [("q", h) for h in range(NH)])
            bank_free = {-2: ("sD", 0), -1: ("sD", 0)}
            ptfree = {-2: ("sD", 0), -1: ("sD", 0)}
            sqfree = [("sD", 0)]
            vtfree = [("sD", 0)]
            trfree = {0: ("sD", 0), 1: ("sD", 0)}
            nrmfree = [("sD", 0)]
            linvfree = [("vD", 0)]

            def epi(C2, bank, ftk, pan2, it):
                kind, ft = fts[it]
                h = ft % NH
                C2.wait("scalar", "tD", C2.cnt["tD"])
                if kind == "q":
                    dst = qt()[:, h * T + pan2 * PANEL: h * T + pan2 * PANEL + PANEL]
                    C2.op("scalar", lambda e, dst=dst, ft=ft: e.activation(
                        dst, bank[:, 0:PANEL], ACT.Identity, bias=qkvb_t[:, ft:ft + 1]), sem="sD")
                    evk = C2.cnt["sD"]
                    bank_free[it] = ("sD", evk)
                    ptfree[it] = ptfree[it - 2]
                    C2.wait("vector", "sD", evk)
                    C2.wait("vector", *sqfree[0])
                    C2.op("vector", lambda e, dst=dst: e.tensor_tensor(pt2[:, 0:PANEL], dst, dst, ALU.mult),
                          sem="vD")
                    C2.wait("tensor", "vD", C2.cnt["vD"])
                    C2.wait("tensor", *nrmfree[0])
                    C2.op("tensor", lambda e: e.matmul(pl[0][0:1, 0:PANEL], onesb[:], pt2[:, 0:PANEL],
                                                       start=True, stop=True), sem="tD")
                    sqfree[0] = ("tD", C2.cnt["tD"])
                    C2.wait("scalar", "tD", C2.cnt["tD"])
                    C2.wait("scalar", *linvfree[0])
                    C2.op("scalar", lambda e: e.activation(
                        linv[0:1, 0:PANEL], pl[0][0:1, 0:PANEL], ACT.Identity), sem="sD")
                    nrmfree[0] = ("sD", C2.cnt["sD"])
                    C2.wait("vector", "sD", C2.cnt["sD"])
                    C2.op("vector", lambda e: e.tensor_reduce(
                        mtmp[0:1, 2:3], linv[0:1, 0:PANEL], mybir.AxisListType.X, ALU.max))
                    C2.drain("vector")
                    C2.op("vector", lambda e, h=h: e.tensor_tensor(
                        mqrow[0:1, h:h + 1], mqrow[0:1, h:h + 1], mtmp[0:1, 2:3], ALU.max), sem="vD")
                    C2.drain("vector")
                    linvfree[0] = ("vD", C2.cnt["vD"])
                elif kind == "k":
                    C2.wait("scalar", *ptfree[it - 2])
                    sl = pt[it % 2]
                    C2.op("scalar", lambda e, sl=sl, ft=ft: e.activation(
                        sl[:, 0:PANEL], bank[:, 0:PANEL], ACT.Identity, bias=qkvb_t[:, ft:ft + 1]), sem="sD")
                    evk = C2.cnt["sD"]
                    bank_free[it] = ("sD", evk)
                    C2.wait("sync", "sD", evk)
                    C2.dma("sync", "st", g["aga_k_i"][h * 128:(h + 1) * 128, pan2 * PANEL:pan2 * PANEL + PANEL],
                           sl[:, 0:PANEL])
                    stk = C2.cnt["st"]
                    C2.wait("vector", "sD", evk)
                    C2.wait("vector", *sqfree[0])
                    C2.op("vector", lambda e, sl=sl: e.tensor_tensor(pt2[:, 0:PANEL], sl[:, 0:PANEL],
                                                                     sl[:, 0:PANEL], ALU.mult), sem="vD")
                    vsq = C2.cnt["vD"]
                    C2.wait("tensor", "vD", vsq)
                    C2.wait("tensor", *nrmfree[0])
                    C2.op("tensor", lambda e: e.matmul(pl[0][0:1, 0:PANEL], onesb[:], pt2[:, 0:PANEL],
                                                       start=True, stop=True), sem="tD")
                    sqfree[0] = ("tD", C2.cnt["tD"])
                    C2.wait("scalar", "tD", C2.cnt["tD"])
                    C2.wait("scalar", *linvfree[0])
                    C2.op("scalar", lambda e: e.activation(
                        linv[0:1, 0:PANEL], pl[0][0:1, 0:PANEL], ACT.Identity), sem="sD")
                    nrmfree[0] = ("sD", C2.cnt["sD"])
                    C2.wait("vector", "sD", C2.cnt["sD"])
                    C2.op("vector", lambda e: e.tensor_reduce(
                        mtmp[0:1, 2:3], linv[0:1, 0:PANEL], mybir.AxisListType.X, ALU.max))
                    C2.drain("vector")
                    C2.op("vector", lambda e, h=h: e.tensor_tensor(
                        mkrow[0:1, h:h + 1], mkrow[0:1, h:h + 1], mtmp[0:1, 2:3], ALU.max), sem="vD")
                    C2.drain("vector")
                    linvfree[0] = ("vD", C2.cnt["vD"])
                    ptfree[it] = ("st", stk)
                else:
                    C2.wait("scalar", *ptfree[it - 2])
                    sl = pt[it % 2]
                    C2.op("scalar", lambda e, sl=sl, ft=ft: e.activation(
                        sl[:, 0:PANEL], bank[:, 0:PANEL], ACT.Identity, bias=qkvb_t[:, ft:ft + 1]), sem="sD")
                    evk = C2.cnt["sD"]
                    bank_free[it] = ("sD", evk)
                    lasttr = 0
                    for j in range(TPP):
                        C2.wait("tensor", "sD", evk)
                        C2.wait("tensor", *trfree[j % 2])
                        C2.op("tensor", lambda e, j=j, sl=sl: e.transpose(
                            psT(4 + j % 2), sl[:, j * 128:(j + 1) * 128], identb[:]), sem="tD")
                        lasttr = C2.cnt["tD"]
                        C2.wait("scalar", "tD", lasttr)
                        if j == 0:
                            C2.wait("scalar", *vtfree[0])
                        C2.op("scalar", lambda e, j=j: e.activation(
                            pt3[:, j * 128:(j + 1) * 128], psT(4 + j % 2), ACT.Identity), sem="sD")
                        trfree[j % 2] = ("sD", C2.cnt["sD"])
                    C2.wait("sync", "sD", C2.cnt["sD"])
                    C2.dma("sync", "st",
                           g["aga_v_i"][pan2 * PANEL:(pan2 + 1) * PANEL, h * 128:(h + 1) * 128]
                           .rearrange("(j p) c -> p j c", p=128),
                           pt3[:, 0:PANEL].rearrange("p (j cc) -> p j cc", cc=128))
                    vtfree[0] = ("st", C2.cnt["st"])
                    ptfree[it] = ("tD", lasttr)

            gemm_panel(C, pan, [(agw_qkvw, ft) for _, ft in fts], 0, epi,
                       lambda it: bank_free[it])
            C.barrier()
        C.wait("sync", "vD", C.cnt["vD"])
        C.dma("sync", "st", g["aga_nk_i"][0:1, 0:NH], mkrow[0:1, 0:NH])
        C.marks["st_aga"] = C.cnt["st"]

    def ph_trigger_aga(ctx):
        C = ctx
        C.wait("gpsimd", "st", C.marks["st_aga"])
        grp = [[0, 1, 2, 3], [4, 5, 6, 7]]
        C.collective("cca", g["aga_k_i"], g["aga_k_o"], groups=grp)
        C.collective("cca", g["aga_v_i"], g["aga_v_o"], groups=grp)
        C.collective("cca", g["aga_nk_i"], g["aga_nk_o"], groups=grp)

    def mprep(ctx, which):
        C = ctx
        if which == "a":
            C.wait("sync", "cca", C.cnt["cca"])
            C.dma("sync", "ld", mtmp[0:1, 64:64 + NH * 4],
                  g["aga_nk_o"].rearrange("r h -> h r"), slow=True)
            C.wait("vector", "ld", C.cnt["ld"])
            C.op("vector", lambda e: e.tensor_reduce(
                mkrow[0:1, 0:NH], mtmp[0:1, 64:64 + NH * 4].rearrange("o (h r) -> o h r", r=4),
                mybir.AxisListType.X, ALU.max), sem="vD")
            C.drain("vector")
        else:
            C.wait("sync", "ccb1", C.cnt["ccb1"])
            C.wait("sync", "ccb2", C.cnt["ccb2"])
            C.dma("sync", "ld", mtmp[0:1, 64:64 + NKV * 4],
                  g["agb1_nk_o"][0:4].rearrange("r h -> h r"), slow=True)
            C.dma("sync", "ld", mtmp[0:1, 160:160 + NCORES], g["agb2_nk_o"].rearrange("r o -> o r"),
                  fence=False, slow=True)
            C.wait("vector", "ld", C.cnt["ld"])
            C.op("vector", lambda e: e.tensor_reduce(
                mkrow[0:1, 0:NKV], mtmp[0:1, 64:64 + NKV * 4].rearrange("o (h r) -> o h r", r=4),
                mybir.AxisListType.X, ALU.max))
            C.drain("vector")
            C.op("vector", lambda e: e.tensor_reduce(
                mtmp[0:1, 0:NKV], mtmp[0:1, 160:160 + NCORES].rearrange("o (h r) -> o h r", r=RPK),
                mybir.AxisListType.X, ALU.max))
            C.drain("vector")
            C.op("vector", lambda e: e.tensor_tensor(
                mkrow[0:1, 0:NKV], mkrow[0:1, 0:NKV], mtmp[0:1, 0:NKV], ALU.max))
            C.drain("vector")
            C.op("vector", lambda e: e.tensor_copy(
                mkrow[0:1, 32:32 + NH].rearrange("o (k gg) -> o k gg", gg=KVG),
                mkrow[0:1, 0:NKV].unsqueeze(2).broadcast_to([1, NKV, KVG])))
            C.drain("vector")
            C.op("vector", lambda e: e.tensor_copy(mkrow[0:1, 0:NH], mkrow[0:1, 32:32 + NH]), sem="vD")
            C.drain("vector")
        C.op("vector", lambda e: e.tensor_tensor(mtmp[0:1, 0:NH], mqrow[0:1, 0:NH], mkrow[0:1, 0:NH],
                                                 ALU.mult), sem="vD")
        C.wait("scalar", "vD", C.cnt["vD"])
        C.op("scalar", lambda e: e.activation(mtmp[0:1, 32:32 + NH], mtmp[0:1, 0:NH], ACT.Sqrt), sem="sD")
        C.wait("vector", "sD", C.cnt["sD"])
        C.op("vector", lambda e: e.tensor_scalar(mtmp[0:1, 0:NH], mtmp[0:1, 32:32 + NH], -SCALE, None,
                                                 ALU.mult), sem="vD")
        C.wait("tensor", "vD", C.cnt["vD"])
        C.op("tensor", lambda e: e.matmul(ps[4][:, 0:NH], onesrow[0:1, 0:128], mtmp[0:1, 0:NH],
                                          start=True, stop=True), sem="tD")
        C.wait("scalar", "tD", C.cnt["tD"])
        C.op("scalar", lambda e: e.activation(mbias[:, 0:NH], ps[4][:, 0:NH], ACT.Identity), sem="sD")

    def ph_attn(ctx, which):
        C = ctx
        if which == "a":
            groups = [[h] for h in range(NH)]
            KVLEN = cfg["S"]
        else:
            groups = [list(range(kvh * KVG, (kvh + 1) * KVG)) for kvh in range(NKV)]
            KVLEN = KV_B
        NTK = KVLEN // 128
        gmark = {}
        gdone = {-2: None, -1: None}
        plfree = {0: ("vD", 0), 1: ("vD", 0)}
        ctxfree = {0: ("vD", 0), 1: ("vD", 0)}
        laccfree = {0: ("tD", 0), 1: ("tD", 0)}
        bcast_free = ("vD", 0)
        expfree = {}
        ctxmm = {}
        vacc = {}

        for gi, grp in enumerate(groups):
            s = gi % 2
            kse, vse = "ld" + str(2 + s), "ld" + str(4 + s)
            if gi >= 2:
                C.wait("sync", "tD", gdone[gi - 2])
            if which == "a":
                h = grp[0]
                C.dma("sync", kse, kvK(s)[:, 0:KVLEN],
                      aga_k[:, h * 128:(h + 1) * 128, :].rearrange("r p t -> p r t"))
                C.dma("sync", vse, kvV(s)[:, 0:NTK * 128],
                      aga_v.rearrange("r (j p) f -> r j p f", p=128)[:, :, :, h * 128:(h + 1) * 128]
                      .rearrange("r j p f -> p r j f"))
            else:
                kvh = gi
                C.dma("sync", kse, kvK(s)[:, 0:CACHE_T],
                      agb2_k[kvh * RPK:(kvh + 1) * RPK].rearrange("r p t -> p r t") if RPK > 1
                      else agb2_k[kvh * RPK])
                C.dma("sync", kse, kvK(s)[:, CACHE_T:CACHE_T + NEW_T],
                      agb1_k[0:4, kvh * 128:(kvh + 1) * 128, :].rearrange("r p t -> p r t"))
                C.dma("sync", vse, kvV(s)[:, 0:(CACHE_T // 128) * 128],
                      agb2_v[kvh * RPK:(kvh + 1) * RPK].rearrange("r (j p) f -> p (r j) f", p=128))
                C.dma("sync", vse, kvV(s)[:, CACHE_T:CACHE_T + NEW_T],
                      agb1_v[0:4].rearrange("r (j p) f -> r j p f", p=128)[:, :, :, kvh * 128:(kvh + 1) * 128]
                      .rearrange("r j p f -> p r j f"))
            gmark[gi] = (C.cnt[kse], C.cnt[vse])
            for h in grp:
                for pan in range(NPAN):
                    qoff = h * T + pan * PANEL
                    lacc = ev[2 + pan]
                    C.wait("tensor", kse, gmark[gi][0])
                    C.wait("tensor", vse, gmark[gi][1])

                    def emit_ctx(pb, ptau):
                        C.wait("tensor", "sD", expfree[pb])
                        if ptau == 0:
                            C.wait("tensor", *ctxfree[pan])
                        C.op("tensor", lambda e, pb=pb, pan=pan, ptau=ptau, s=s: e.matmul(
                            ps[2 + pan][:, 0:PANEL], kvV(s)[:, ptau * 128:(ptau + 1) * 128],
                            pt[pb][:, 0:PANEL], start=(ptau == 0), stop=(ptau == NTK - 1)),
                            sem="tD")
                        ctxmm[pb] = C.cnt["tD"]

                    pend = None
                    for tau in range(NTK):
                        b = tau % 2
                        if expfree.get(b) is not None:
                            C.wait("tensor", "sD", expfree[b])
                        C.op("tensor", lambda e, b=b, tau=tau, qoff=qoff, s=s: e.matmul(
                            ps[b][:, 0:PANEL], kvK(s)[:, tau * 128:(tau + 1) * 128],
                            qt()[:, qoff:qoff + PANEL], start=True, stop=True), sem="tD")
                        smark = C.cnt["tD"]
                        if pend is not None:
                            emit_ctx(*pend)
                        C.wait("scalar", "tD", smark)
                        if ctxmm.get(b) is not None:
                            C.wait("scalar", "tD", ctxmm[b])
                        if vacc.get(b) is not None:
                            C.wait("scalar", "vD", vacc[b])
                        C.op("scalar", lambda e, b=b, h=h: e.activation(
                            pt[b][:, 0:PANEL], ps[b][:, 0:PANEL], ACT.Exp,
                            bias=mbias[:, h:h + 1], scale=SCALE), sem="sD")
                        expfree[b] = C.cnt["sD"]
                        C.wait("vector", "sD", expfree[b])
                        if tau == 0:
                            C.wait("vector", *laccfree[pan])
                            C.op("vector", lambda e, b=b, lacc=lacc: e.tensor_copy(
                                lacc[:, 0:PANEL], pt[b][:, 0:PANEL]), sem="vD")
                        else:
                            C.op("vector", lambda e, b=b, lacc=lacc: e.scalar_tensor_tensor(
                                lacc[:, 0:PANEL], pt[b][:, 0:PANEL], 1.0, lacc[:, 0:PANEL],
                                ALU.mult, ALU.add), sem="vD")
                        vacc[b] = C.cnt["vD"]
                        pend = (b, tau)
                    emit_ctx(*pend)
                    C.wait("tensor", "vD", vacc[pend[0]])
                    C.wait("tensor", *plfree[pan])
                    C.op("tensor", lambda e, pan=pan, lacc=lacc: e.matmul(
                        pl[pan][0:1, 0:PANEL], onesf[:], lacc[:, 0:PANEL],
                        start=True, stop=True), sem="tD")
                    laccfree[pan] = ("tD", C.cnt["tD"])
                    C.wait("vector", "tD", C.cnt["tD"])
                    C.wait("vector", *bcast_free)
                    C.op("vector", lambda e, pan=pan: e.reciprocal(linv[0:1, 0:PANEL],
                                                                   pl[pan][0:1, 0:PANEL]), sem="vD")
                    plfree[pan] = ("vD", C.cnt["vD"])
                    C.wait("tensor", "vD", C.cnt["vD"])
                    C.op("tensor", lambda e: e.matmul(ps[4][:, 0:PANEL], onesrow[0:1, 0:128],
                                                      linv[0:1, 0:PANEL], start=True, stop=True), sem="tD")
                    C.wait("scalar", "tD", C.cnt["tD"])
                    C.op("scalar", lambda e: e.activation(ev[0][:, 0:PANEL], ps[4][:, 0:PANEL],
                                                          ACT.Identity), sem="sD")
                    C.wait("vector", "sD", C.cnt["sD"])
                    C.op("vector", lambda e, pan=pan, qoff=qoff: e.tensor_tensor(
                        qt()[:, qoff:qoff + PANEL], ps[2 + pan][:, 0:PANEL], ev[0][:, 0:PANEL],
                        ALU.mult), sem="vD")
                    ctxfree[pan] = ("vD", C.cnt["vD"])
                    bcast_free = ("vD", C.cnt["vD"])
            gdone[gi] = C.cnt["tD"]

    def ph_outproj(ctx, which, res_src, res_dst):
        C = ctx
        if which == "a":
            C.wait("sync", "cco", C.cnt["cco"])
            wsrc = g["agw_ow_o"]
            bias = True
        else:
            C.wait("sync", "ccwo", C.cnt["ccwo"])
            wsrc = g["agw_woT_o"]
            bias = False
        NFC = QF // 128
        NRND = NTT // TPP
        rsfree = {-2: ("tD", 0), -1: ("tD", 0)}
        xcfree = {-2: ("vD", 0), -1: ("vD", 0)}
        evfree = {-2: ("st", 0), -1: ("st", 0)}
        bankfree = {ii: ("vD", 0) for ii in range(TPP)}
        bankdone = {}
        it_r = 0
        it_x = 0
        for dp in range(DP):
            for rnd in range(NRND):
                for fc in range(NFC):
                    sr = it_r % 2
                    if it_r >= 2:
                        C.wait("sync", *rsfree[it_r - 2])
                    C.dma("sync", "ld" + str(2 + sr), rs[sr][:, 0:512],
                          wsrc[fc * 128:(fc + 1) * 128, dp * 512:(dp + 1) * 512])
                    ldmk = C.cnt["ld" + str(2 + sr)]
                    C.wait("tensor", "ld" + str(2 + sr), ldmk)
                    for ii in range(TPP):
                        i = rnd * TPP + ii
                        if fc == 0:
                            C.wait("tensor", *bankfree[ii])
                        C.op("tensor", lambda e, sr=sr, fc=fc, ii=ii, i=i: e.matmul(
                            ps[ii][:, 0:512], qt()[:, fc * T + i * 128:fc * T + (i + 1) * 128],
                            rs[sr][:, 0:512], start=(fc == 0), stop=(fc == NFC - 1 and not bias)),
                            sem="tD")
                    rsfree[it_r] = ("tD", C.cnt["tD"])
                    it_r += 1
                if bias:
                    for ii in range(TPP):
                        C.op("tensor", lambda e, ii=ii, dp=dp: e.matmul(
                            ps[ii][:, 0:512], onesrow_b[0:1, 0:128], obrow_b[0:1, dp * 512:(dp + 1) * 512],
                            start=False, stop=True), sem="tD")
                bankdone[(dp, rnd)] = C.cnt["tD"]
                for ii in range(TPP):
                    i = rnd * TPP + ii
                    sx = it_x % 2
                    if it_x >= 2:
                        C.wait("sync", *xcfree[it_x - 2])
                    C.dma("sync", "ld" + str(4 + sx), ev[2 + sx][:, 0:512],
                          res_src[i * 128:(i + 1) * 128, dp * 512:(dp + 1) * 512])
                    ldmk = C.cnt["ld" + str(4 + sx)]
                    C.wait("vector", "ld" + str(4 + sx), ldmk)
                    C.wait("vector", "tD", bankdone[(dp, rnd)])
                    C.wait("vector", *evfree[it_x - 2])
                    C.op("vector", lambda e, ii=ii, sx=sx: e.scalar_tensor_tensor(
                        ev[sx][:, 0:512], ps[ii][:, 0:512], 1.0, ev[2 + sx][:, 0:512],
                        ALU.mult, ALU.add), sem="vD")
                    xcfree[it_x] = ("vD", C.cnt["vD"])
                    bankfree[ii] = ("vD", C.cnt["vD"])
                    C.wait("sync", "vD", C.cnt["vD"])
                    C.dma("sync", "st", res_dst[i * 128:(i + 1) * 128, dp * 512:(dp + 1) * 512],
                          ev[sx][:, 0:512])
                    evfree[it_x] = ("st", C.cnt["st"])
                    it_x += 1

    def ph_ffn(ctx, res_src, res_dst):
        C = ctx
        C.wait("sync", "ccf", C.cnt["ccf"])
        C.wait("tensor", "ccf", C.cnt["ccf"])
        RPD = max(1, 512 // DQC)
        for pan in range(NPAN):
            ln_panel(C, res_src, pan)
            C.barrier()
            gfree = ("vD", 0)
            g_sD = 0
            u_vD = 0
            gmm_prev = 0
            umm_prev = 0
            pt0free = ("tD", 0)
            for nt in range(NT_I):
                if nt >= 1:
                    C.wait("sync", "tD", gmm_prev)
                C.dma("sync", "ld0", wb[0][:, 0:KC * 128], lhsT_view(agw_g, nt))
                l0 = C.cnt["ld0"]
                if nt >= 1:
                    C.wait("sync", "tD", umm_prev)
                C.dma("sync", "ld1", wb[1][:, 0:KC * 128], lhsT_view(agw_u, nt))
                l1 = C.cnt["ld1"]
                C.wait("tensor", "ld0", l0)
                if nt >= 1:
                    C.wait("tensor", "sD", g_sD)
                for kc in range(KC):
                    C.op("tensor", lambda e, kc=kc: e.matmul(
                        ps[0][:, 0:PANEL], wb[0][:, kc * 128:(kc + 1) * 128],
                        lnT()[:, kc * PANEL:(kc + 1) * PANEL],
                        start=(kc == 0), stop=(kc == KC - 1)), sem="tD")
                gmm = C.cnt["tD"]
                C.wait("tensor", "ld1", l1)
                if nt >= 1:
                    C.wait("tensor", "vD", u_vD)
                for kc in range(KC):
                    C.op("tensor", lambda e, kc=kc: e.matmul(
                        ps[1][:, 0:PANEL], wb[1][:, kc * 128:(kc + 1) * 128],
                        lnT()[:, kc * PANEL:(kc + 1) * PANEL],
                        start=(kc == 0), stop=(kc == KC - 1)), sem="tD")
                umm = C.cnt["tD"]
                gmm_prev, umm_prev = gmm, umm
                C.wait("scalar", "tD", gmm)
                if nt >= 1:
                    C.wait("scalar", "vD", u_vD)
                C.op("scalar", lambda e: e.activation(pt[0][:, 0:PANEL], ps[0][:, 0:PANEL], ACT.Silu),
                     sem="sD")
                g_sD = C.cnt["sD"]
                C.wait("vector", "sD", g_sD)
                C.wait("vector", "tD", umm)
                C.op("vector", lambda e, nt=nt: e.tensor_tensor(
                    hT()[:, nt * PANEL:(nt + 1) * PANEL], pt[0][:, 0:PANEL], ps[1][:, 0:PANEL],
                    ALU.mult), sem="vD")
                u_vD = C.cnt["vD"]
            C.barrier()
            it_r = 0
            it_x = 0
            rsfree = {-2: ("tD", 0), -1: ("tD", 0)}
            xcfree = {-2: ("vD", 0), -1: ("vD", 0)}
            evfree = {-2: ("st", 0), -1: ("st", 0)}
            bankfree = {ii: ("vD", 0) for ii in range(TPP)}
            for dp in range(DP):
                for kc in range(NT_I):
                    sr = it_r % 2
                    if it_r >= 2:
                        C.wait("sync", *rsfree[it_r - 2])
                    r0 = dp * RPD if DQC <= 512 else dp * 512 // DQC
                    C.dma("sync", "ld" + str(2 + sr), rs[sr][:, 0:512],
                          agw_d[r0:r0 + RPD, kc * 128:(kc + 1) * 128, :].rearrange("r p cc -> p r cc"))
                    ldmk = C.cnt["ld" + str(2 + sr)]
                    C.wait("tensor", "ld" + str(2 + sr), ldmk)
                    for ii in range(TPP):
                        if kc == 0:
                            C.wait("tensor", *bankfree[ii])
                        C.op("tensor", lambda e, sr=sr, kc=kc, ii=ii: e.matmul(
                            ps[2 + ii][:, 0:512], hT()[:, kc * PANEL + ii * 128:kc * PANEL + (ii + 1) * 128],
                            rs[sr][:, 0:512], start=(kc == 0), stop=(kc == NT_I - 1)), sem="tD")
                    rsfree[it_r] = ("tD", C.cnt["tD"])
                    it_r += 1
                bankdone = C.cnt["tD"]
                for ii in range(TPP):
                    i = pan * TPP + ii
                    sx = it_x % 2
                    if it_x >= 2:
                        C.wait("sync", *xcfree[it_x - 2])
                    C.dma("sync", "ld" + str(4 + sx), ev[2 + sx][:, 0:512],
                          res_src[i * 128:(i + 1) * 128, dp * 512:(dp + 1) * 512])
                    ldmk = C.cnt["ld" + str(4 + sx)]
                    C.wait("vector", "ld" + str(4 + sx), ldmk)
                    C.wait("vector", "tD", bankdone)
                    C.wait("vector", *evfree[it_x - 2])
                    C.op("vector", lambda e, ii=ii, sx=sx: e.scalar_tensor_tensor(
                        ev[sx][:, 0:512], ps[2 + ii][:, 0:512], 1.0, ev[2 + sx][:, 0:512],
                        ALU.mult, ALU.add), sem="vD")
                    xcfree[it_x] = ("vD", C.cnt["vD"])
                    bankfree[ii] = ("vD", C.cnt["vD"])
                    C.wait("sync", "vD", C.cnt["vD"])
                    C.dma("sync", "st", res_dst[i * 128:(i + 1) * 128, dp * 512:(dp + 1) * 512],
                          ev[sx][:, 0:512])
                    evfree[it_x] = ("st", C.cnt["st"])
                    it_x += 1
            C.barrier()

    def ph_bproj(ctx, res_src):
        C = ctx
        C.op("vector", lambda e: e.memset(mqrow[:], -3e38))
        C.op("vector", lambda e: e.memset(mkrow[:], -3e38))
        C.drain("vector")
        C.wait("tensor", "ccw", C.cnt["ccw"])
        C.wait("sync", "ccw", C.cnt["ccw"])
        for pan in range(NPAN):
            ln_panel(C, res_src, pan)
            C.barrier()
            jobs = ([("k", agw_wkT, h) for h in range(NFT_KV)] + [("v", agw_wvT, h) for h in range(NFT_KV)]
                    + [("q", agw_wqT, h) for h in range(NFT_Q)])
            bank_free = {-2: ("sD", 0), -1: ("sD", 0)}
            ptfree = {-2: ("sD", 0), -1: ("sD", 0)}
            sqfree = [("sD", 0)]
            vtfree = [("sD", 0)]
            trfree = {0: ("sD", 0), 1: ("sD", 0)}
            nrmfree = [("sD", 0)]
            linvfree = [("vD", 0)]
            ropefree = [("sD", 0)]

            def epi(C2, bank, ftk, pan2, it):
                kind, _, h = jobs[it]
                C2.wait("scalar", "tD", C2.cnt["tD"])
                C2.wait("scalar", *ptfree[it - 2])
                sl = pt[it % 2]
                C2.op("scalar", lambda e, sl=sl: e.activation(sl[:, 0:PANEL], bank[:, 0:PANEL],
                                                              ACT.Identity), sem="sD")
                evk = C2.cnt["sD"]
                bank_free[it] = ("sD", evk)
                if kind in ("q", "k"):
                    C2.wait("tensor", "sD", evk)
                    C2.wait("tensor", *ropefree[0])
                    C2.op("tensor", lambda e, sl=sl: e.matmul(ps[4][:, 0:PANEL], rmat[:],
                                                              sl[:, 0:PANEL], start=True, stop=True), sem="tD")
                    rmm = C2.cnt["tD"]
                    C2.wait("scalar", "tD", rmm)
                    if kind == "q":
                        dst = qt()[:, h * T + pan2 * PANEL: h * T + pan2 * PANEL + PANEL]
                    else:
                        dst = pt3[:, 0:PANEL]
                        C2.wait("scalar", *vtfree[0])
                    C2.op("scalar", lambda e, dst=dst: e.activation(dst, ps[4][:, 0:PANEL], ACT.Identity),
                          sem="sD")
                    ropefree[0] = ("sD", C2.cnt["sD"])
                    revk = C2.cnt["sD"]
                    if kind == "k":
                        C2.wait("sync", "sD", revk)
                        C2.dma("sync", "st", g["agb1_k_i"][h * 128:(h + 1) * 128,
                                                           pan2 * PANEL:pan2 * PANEL + PANEL], pt3[:, 0:PANEL])
                        vtfree[0] = ("st", C2.cnt["st"])
                    C2.wait("vector", "sD", revk)
                    C2.wait("vector", *sqfree[0])
                    srcv = dst
                    C2.op("vector", lambda e, srcv=srcv: e.tensor_tensor(pt2[:, 0:PANEL], srcv, srcv,
                                                                         ALU.mult), sem="vD")
                    C2.wait("tensor", "vD", C2.cnt["vD"])
                    C2.wait("tensor", *nrmfree[0])
                    C2.op("tensor", lambda e: e.matmul(pl[0][0:1, 0:PANEL], onesb[:], pt2[:, 0:PANEL],
                                                       start=True, stop=True), sem="tD")
                    sqfree[0] = ("tD", C2.cnt["tD"])
                    C2.wait("scalar", "tD", C2.cnt["tD"])
                    C2.wait("scalar", *linvfree[0])
                    C2.op("scalar", lambda e: e.activation(
                        linv[0:1, 0:PANEL], pl[0][0:1, 0:PANEL], ACT.Identity), sem="sD")
                    nrmfree[0] = ("sD", C2.cnt["sD"])
                    C2.wait("vector", "sD", C2.cnt["sD"])
                    C2.op("vector", lambda e: e.tensor_reduce(
                        mtmp[0:1, 2:3], linv[0:1, 0:PANEL], mybir.AxisListType.X, ALU.max))
                    C2.drain("vector")
                    mrow_dst = mqrow if kind == "q" else mkrow
                    C2.op("vector", lambda e, h=h, mrow_dst=mrow_dst: e.tensor_tensor(
                        mrow_dst[0:1, h:h + 1], mrow_dst[0:1, h:h + 1], mtmp[0:1, 2:3], ALU.max),
                        sem="vD")
                    C2.drain("vector")
                    linvfree[0] = ("vD", C2.cnt["vD"])
                    ptfree[it] = ("tD", rmm)
                else:
                    lasttr = 0
                    for j in range(TPP):
                        C2.wait("tensor", "sD", evk)
                        C2.wait("tensor", *trfree[j % 2])
                        C2.op("tensor", lambda e, j=j, sl=sl: e.transpose(
                            psT(4 + j % 2), sl[:, j * 128:(j + 1) * 128], identb[:]), sem="tD")
                        lasttr = C2.cnt["tD"]
                        C2.wait("scalar", "tD", lasttr)
                        if j == 0:
                            C2.wait("scalar", *vtfree[0])
                        C2.op("scalar", lambda e, j=j: e.activation(
                            pt3[:, j * 128:(j + 1) * 128], psT(4 + j % 2), ACT.Identity), sem="sD")
                        trfree[j % 2] = ("sD", C2.cnt["sD"])
                    C2.wait("sync", "sD", C2.cnt["sD"])
                    C2.dma("sync", "st",
                           g["agb1_v_i"][pan2 * PANEL:(pan2 + 1) * PANEL, h * 128:(h + 1) * 128]
                           .rearrange("(j p) c -> p j c", p=128),
                           pt3[:, 0:PANEL].rearrange("p (j cc) -> p j cc", cc=128))
                    vtfree[0] = ("st", C2.cnt["st"])
                    ptfree[it] = ("tD", lasttr)

            gemm_panel(C, pan, [(v, h) for _, v, h in jobs], 0, epi, lambda it: bank_free[it])
            C.barrier()
        C.wait("sync", "vD", C.cnt["vD"])
        C.dma("sync", "st", g["agb1_nk_i"][0:1, 0:NKV], mkrow[0:1, 0:NKV])
        C.marks["st_agb1"] = C.cnt["st"]

    def ph_trigger_agb1(ctx):
        C = ctx
        C.wait("gpsimd", "st", C.marks["st_agb1"])
        C.collective("ccb1", g["agb1_k_i"], g["agb1_k_o"])
        C.collective("ccb1", g["agb1_v_i"], g["agb1_v_o"])
        C.collective("ccb1", g["agb1_nk_i"], g["agb1_nk_o"])

    def run_all(ctx):
        ph_prep(ctx)
        ctx.barrier()
        ph_wcast(ctx)
        ph_trigger1(ctx)
        ph_qzsz(ctx)
        ctx.barrier()
        ph_dequant(ctx)
        ctx.barrier()
        ph_gather(ctx)
        ctx.barrier()
        ph_trigger2(ctx)
        ph_lnqkv(ctx)
        ph_trigger_aga(ctx)
        ctx.barrier()
        mprep(ctx, "a")
        ctx.barrier()
        ph_attn(ctx, "a")
        ctx.barrier()
        ph_outproj(ctx, "a", x_in, out_ext if dbg_stop == 1 else res1)
        ctx.barrier()
        if dbg_stop == 1:
            return
        ph_ffn(ctx, res1, out_ext if dbg_stop == 2 else res2)
        if dbg_stop == 2:
            return
        ph_bproj(ctx, res2)
        ph_trigger_agb1(ctx)
        ctx.barrier()
        mprep(ctx, "b")
        ctx.barrier()
        ph_attn(ctx, "b")
        ctx.barrier()
        ph_outproj(ctx, "b", res2, out_ext if dbg_stop == 3 else res3)
        ctx.barrier()
        if dbg_stop == 3:
            return
        ph_ffn(ctx, res3, out_ext)

    with es:
        with nc.Block() as block:
            starters = dict(sync=block.sync, tensor=block.tensor, vector=block.vector,
                            scalar=block.scalar, gpsimd=block.gpsimd)
            for role in ROLES:
                def make(role):
                    def fn(eng):
                        ctx = Ctx(nc, cfg, sems, B, dbg_stop)
                        ctx.role = role
                        ctx.eng = eng
                        run_all(ctx)
                        ctx.barrier()
                    return fn
                starters[role](make(role))
    return nc


def kernel_generic(cfg, I, dbg_stop=None):
    in_maps = stage_inputs(cfg, I)
    nc = build_nc(cfg, dbg_stop=dbg_stop)
    res = run_bass_kernel_spmd(nc, in_maps, core_ids=list(range(NCORES)))
    c = derived(cfg)
    out = np.concatenate([np.asarray(r["out"]) for r in res.results], axis=0)
    return out.reshape(cfg["B"], cfg["S"], cfg["D"])



def kernel(L0_attn_a_x, L0_attn_a_attn_qkvw, L0_attn_a_attn_qkvb, L0_attn_a_attn_ow,
           L0_attn_a_attn_ob, L0_attn_b_freqs_cos, L0_attn_b_freqs_sin, L0_attn_b_wq,
           L0_attn_b_wk, L0_attn_b_wv, L0_attn_b_wo, L0_attn_b_k_cache, L0_attn_b_v_cache,
           L0_attn_b_block_idx, L0_ffn_gate_qweight, L0_ffn_gate_scales, L0_ffn_gate_qzeros,
           L0_ffn_gate_g_idx, L0_ffn_up_qweight, L0_ffn_up_scales, L0_ffn_up_qzeros,
           L0_ffn_up_g_idx, L0_ffn_down_qweight, L0_ffn_down_scales, L0_ffn_down_qzeros,
           L0_ffn_down_g_idx):
    I = dict(
        x=L0_attn_a_x, qkvw=L0_attn_a_attn_qkvw, qkvb=L0_attn_a_attn_qkvb,
        ow=L0_attn_a_attn_ow, ob=L0_attn_a_attn_ob,
        fcos=L0_attn_b_freqs_cos, fsin=L0_attn_b_freqs_sin,
        wq=L0_attn_b_wq, wk=L0_attn_b_wk, wv=L0_attn_b_wv, wo=L0_attn_b_wo,
        k_cache=L0_attn_b_k_cache, v_cache=L0_attn_b_v_cache, block_idx=L0_attn_b_block_idx,
        gq=L0_ffn_gate_qweight, gsc=L0_ffn_gate_scales, gqz=L0_ffn_gate_qzeros,
        g_gidx=L0_ffn_gate_g_idx,
        uq=L0_ffn_up_qweight, usc=L0_ffn_up_scales, uqz=L0_ffn_up_qzeros,
        u_gidx=L0_ffn_up_g_idx,
        dq=L0_ffn_down_qweight, dsc=L0_ffn_down_scales, dqz=L0_ffn_down_qzeros,
        d_gidx=L0_ffn_down_g_idx,
    )
    out = kernel_generic(full_cfg(), I)
    return out.astype(np.float32)


# version 06 (final)
# speedup vs baseline: 1.4049x; 1.0233x over previous
import math
import numpy as np
import ml_dtypes
import concourse.bass as bass
import concourse.mybir as mybir
from concourse.bass_utils import run_bass_kernel_spmd
from contextlib import ExitStack

F32 = mybir.dt.float32
BF16 = mybir.dt.bfloat16
I32 = mybir.dt.int32
U32 = mybir.dt.uint32
ACT = mybir.ActivationFunctionType
ALU = mybir.AluOpType
BF = ml_dtypes.bfloat16

NCORES = 8


def full_cfg():
    return dict(
        B=2, S=4096, D=4096, NH=32, HD=128, NKV=8,
        INTER=11008, GROUP=128,
        NBLK=128, BLOCK=256, TOPK=16,
    )


def mini_cfg():
    return dict(
        B=2, S=512, D=1024, NH=8, HD=128, NKV=4,
        INTER=2048, GROUP=128,
        NBLK=16, BLOCK=128, TOPK=4,
    )


def derived(cfg):
    c = dict(cfg)
    c["T"] = cfg["B"] * cfg["S"] // NCORES
    c["NTT"] = c["T"] // 128
    c["PANEL"] = min(512, c["T"])
    c["NPAN"] = c["T"] // c["PANEL"]
    c["TPP"] = c["PANEL"] // 128
    c["KC"] = cfg["D"] // 128
    c["KCS"] = c["KC"] // NCORES
    c["NT_I"] = cfg["INTER"] // 128
    c["KVG"] = cfg["NH"] // cfg["NKV"]
    c["R_PER_KVH"] = NCORES // cfg["NKV"]
    c["CACHE_T"] = cfg["TOPK"] * cfg["BLOCK"]
    c["NEW_T"] = cfg["S"]
    c["KV_B"] = c["CACHE_T"] + c["NEW_T"]
    c["DP"] = cfg["D"] // 512
    c["QF"] = cfg["NH"] * cfg["HD"]
    c["KVF"] = cfg["NKV"] * cfg["HD"]
    c["WG_KC"] = c["KCS"]
    c["DQ_COLS"] = cfg["D"] // NCORES
    c["GRP_I"] = cfg["INTER"] // cfg["GROUP"]
    c["GRP_D"] = cfg["D"] // cfg["GROUP"]
    assert c["T"] % 128 == 0 and c["KCS"] >= 1 and cfg["D"] % 512 == 0
    assert c["CACHE_T"] % (c["R_PER_KVH"] * 128) == 0
    return c



def ftmaj(w):
    K, N = w.shape
    return np.ascontiguousarray(
        w.reshape(K // 128, 128, N // 128, 128).transpose(2, 1, 0, 3).reshape(N // 128 * 128, K))


def stage_inputs(cfg, I):
    c = derived(cfg)
    D, NH, HD, NKV = cfg["D"], cfg["NH"], cfg["HD"], cfg["NKV"]
    T = c["T"]
    x = np.asarray(I["x"]).reshape(-1, D)
    qkvw = np.asarray(I["qkvw"])
    qkvb = np.asarray(I["qkvb"]).reshape(3 * NH, HD).T
    ow = np.asarray(I["ow"])
    ob = np.asarray(I["ob"]).reshape(D // 128, 128).T
    wqT = np.ascontiguousarray(np.asarray(I["wq"]).T)
    wkT = np.ascontiguousarray(np.asarray(I["wk"]).T)
    wvT = np.ascontiguousarray(np.asarray(I["wv"]).T)
    woT = np.ascontiguousarray(np.asarray(I["wo"]).T)
    kcache = np.asarray(I["k_cache"])
    vcache = np.asarray(I["v_cache"])
    blkidx = np.asarray(I["block_idx"]).reshape(1, -1)
    fcos = np.asarray(I["fcos"]).reshape(1, HD // 2)
    fsin = np.asarray(I["fsin"]).reshape(1, HD // 2)

    gq, gsc, gqz = np.asarray(I["gq"]), np.asarray(I["gsc"]), np.asarray(I["gqz"])
    uq, usc, uqz = np.asarray(I["uq"]), np.asarray(I["usc"]), np.asarray(I["uqz"])
    dq, dsc, dqz = np.asarray(I["dq"]), np.asarray(I["dsc"]), np.asarray(I["dqz"])

    j = (np.arange(128) % 8)
    mask_nib = (np.int64(15) * (np.int64(1) << (4 * j))).astype(np.int64).astype(np.uint32).view(np.int32)[:, None]
    pow16 = (16.0 ** (-j.astype(np.float64))).astype(np.float32)[:, None]
    identf = np.eye(128, dtype=np.float32)
    identb = np.eye(128, dtype=np.float32).astype(BF)
    mup = np.zeros((128, 128), np.float32)
    mdn = np.zeros((128, 128), np.float32)
    ii = np.arange(64)
    mup[2 * ii, 2 * ii + 1] = 1.0
    mdn[2 * ii + 1, 2 * ii] = 1.0

    KR = D // NCORES
    QR = c["QF"] // NCORES
    in_maps = []
    for r in range(NCORES):
        kvh = r // c["R_PER_KVH"]
        tpart = r % c["R_PER_KVH"]
        ct = c["CACHE_T"] // c["R_PER_KVH"]
        m = dict(
            x=np.ascontiguousarray(x[r * T:(r + 1) * T]),
            qkvw=np.ascontiguousarray(ftmaj(qkvw[r * KR:(r + 1) * KR])),
            qkvb=np.ascontiguousarray(qkvb.astype(np.float32)),
            ow=np.ascontiguousarray(ow[r * KR:(r + 1) * KR]),
            ob=np.ascontiguousarray(ob.astype(np.float32)),
            ob_row=np.ascontiguousarray(np.asarray(I["ob"]).reshape(1, D).astype(np.float32)),
            wqT=np.ascontiguousarray(ftmaj(wqT[r * KR:(r + 1) * KR])),
            wkT=np.ascontiguousarray(ftmaj(wkT[r * KR:(r + 1) * KR])),
            wvT=np.ascontiguousarray(ftmaj(wvT[r * KR:(r + 1) * KR])),
            woT=np.ascontiguousarray(woT[r * QR:(r + 1) * QR]),
            kcache=np.ascontiguousarray(kcache[:, :, kvh, :].reshape(-1, HD)),
            vcache=np.ascontiguousarray(vcache[:, :, kvh, :].reshape(-1, HD)),
            blkidx=np.ascontiguousarray(
                blkidx[:, (tpart * ct) // cfg["BLOCK"]:((tpart + 1) * ct) // cfg["BLOCK"]]
                .astype(np.int32).reshape(-1, 1)),
            csin=np.ascontiguousarray(np.stack([np.repeat(fcos.reshape(-1), 2),
                                                np.repeat(fsin.reshape(-1), 2)], axis=1).astype(np.float32)),
            gq=np.ascontiguousarray(gq[r * (KR // 8):(r + 1) * (KR // 8)]),
            gsc=np.ascontiguousarray(gsc[np.asarray(I["g_gidx"]).reshape(-1)[0::128][r * c["KCS"]:(r + 1) * c["KCS"]]]),
            gqz=np.ascontiguousarray(gqz[np.asarray(I["g_gidx"]).reshape(-1)[0::128][r * c["KCS"]:(r + 1) * c["KCS"]]]),
            uq=np.ascontiguousarray(uq[r * (KR // 8):(r + 1) * (KR // 8)]),
            usc=np.ascontiguousarray(usc[np.asarray(I["u_gidx"]).reshape(-1)[0::128][r * c["KCS"]:(r + 1) * c["KCS"]]]),
            uqz=np.ascontiguousarray(uqz[np.asarray(I["u_gidx"]).reshape(-1)[0::128][r * c["KCS"]:(r + 1) * c["KCS"]]]),
            dq=np.ascontiguousarray(dq[:, r * c["DQ_COLS"]:(r + 1) * c["DQ_COLS"]]),
            dsc=np.ascontiguousarray(dsc[np.asarray(I["d_gidx"]).reshape(-1)[0::128]][:, r * c["DQ_COLS"]:(r + 1) * c["DQ_COLS"]]),
            dqz=np.ascontiguousarray(dqz[np.asarray(I["d_gidx"]).reshape(-1)[0::128]][:, r * (c["DQ_COLS"] // 8):(r + 1) * (c["DQ_COLS"] // 8)]),
            mask_nib=mask_nib, pow16=pow16, identf=identf, identb=identb,
            mup=mup, mdn=mdn,
        )
        in_maps.append(m)
    return in_maps



ROLES = ("sync", "tensor", "vector", "scalar", "gpsimd")


class Ctx:
    def __init__(self, nc, cfg, sems, bufs, dbg):
        self.nc = nc
        self.cfg = cfg
        self.sems = sems
        self.bufs = bufs
        self.cnt = {k: 0 for k in sems}
        self.role = None
        self.eng = None
        self.dbg = dbg
        self.marks = {}
        self.outst = {k: 0 for k in sems}

    def bump(self, name, by):
        self.cnt[name] += by
        return self.cnt[name]

    def wait(self, role, name, val=None):
        v = self.cnt[name] if val is None else val
        if self.role == role:
            self.eng.wait_ge(self.sems[name], v)

    def dma(self, role, sem, out, in_, slow=False, fence=True):
        if fence and self.outst[sem] > 0:
            self.wait(role, sem, self.cnt[sem])
            self.outst[sem] = 0
        self.bump(sem, 16)
        self.outst[sem] += 1
        if self.role == role:
            self.eng.dma_start(out=out, in_=in_, allow_slow_non_contiguous=slow).then_inc(self.sems[sem], 16)

    def idma(self, role, sem, out, in_dram, idx_ap, fence=True, element_offset=0):
        if fence and self.outst[sem] > 0:
            self.wait(role, sem, self.cnt[sem])
            self.outst[sem] = 0
        self.bump(sem, 16)
        self.outst[sem] += 1
        if self.role == role:
            self.eng.indirect_dma_start(
                out=out, out_offset=None, in_=in_dram,
                in_offset=bass.IndirectOffsetOnAxis(ap=idx_ap, axis=0),
                element_offset=element_offset,
            ).then_inc(self.sems[sem], 16)

    def op(self, role, fn, sem=None):
        if sem is not None:
            self.bump(sem, 1)
        if self.role == role:
            ins = fn(self.eng)
            if sem is not None:
                ins.then_inc(self.sems[sem], 1)

    def drain(self, role):
        if self.role == role:
            self.eng.drain()

    DMA_SEMS_SYNC = ("ld", "st", "ld0", "ld1", "ld2", "ld3", "ld4", "ld5")

    def barrier(self):
        for s in self.DMA_SEMS_SYNC:
            if self.cnt[s]:
                self.wait("sync", s, self.cnt[s])
                self.outst[s] = 0
        if self.cnt["gst"]:
            self.wait("gpsimd", "gst", self.cnt["gst"])
            self.outst["gst"] = 0
        eng_sem = dict(sync="qD", tensor="tD", vector="vD", scalar="sD", gpsimd="gD")
        for r, sname in eng_sem.items():
            self.bump(sname, 1)
            if self.role == r:
                if r != "sync":
                    self.eng.drain()
                self.eng.sem_inc(self.sems[sname], 1)
        for r in ROLES:
            for r2, sname in eng_sem.items():
                if r2 != r:
                    self.wait(r, sname, self.cnt[sname])

    def collective(self, sem, in_ap, out_ap, groups=None):
        self.bump(sem, 1)
        if self.role == "gpsimd":
            rg = [list(range(NCORES))] if groups is None else groups
            self.eng.collective_compute(
                "AllGather", ALU.bypass, replica_groups=rg,
                ins=[in_ap.opt()], outs=[out_ap.opt()],
            ).then_inc(self.sems[sem], 1)



def build_nc(cfg, dbg_stop=None):
    c = derived(cfg)
    D, NH, HD, NKV, INTER = cfg["D"], cfg["NH"], cfg["HD"], cfg["NKV"], cfg["INTER"]
    T, PANEL, NPAN, TPP, KC, KCS = c["T"], c["PANEL"], c["NPAN"], c["TPP"], c["KC"], c["KCS"]
    NTT, NT_I, QF, KVF = c["NTT"], c["NT_I"], c["QF"], c["KVF"]
    KVG, RPK = c["KVG"], c["R_PER_KVH"]
    CT = c["CACHE_T"] // RPK
    CACHE_T, NEW_T, KV_B = c["CACHE_T"], c["NEW_T"], c["KV_B"]
    NFT_QKV = 3 * D // 128
    NFT_Q, NFT_KV = QF // 128, KVF // 128
    DP = D // 512
    KR = D // NCORES
    QR = QF // NCORES
    DQC = c["DQ_COLS"]
    GRP_I = c["GRP_I"]
    SCALE = 1.0 / math.sqrt(HD)

    nc = bass.Bass("TRN2", target_bir_lowering=False, debug=False)

    def din(name, shape, dt=F32):
        return nc.dram_tensor(name, shape, dt, kind="ExternalInput").ap()

    x_in = din("x", [T, D])
    qkvw_in = din("qkvw", [NFT_QKV * 128, KCS * 128])
    qkvb_in = din("qkvb", [128, 3 * NH])
    ow_in = din("ow", [KR, D])
    ob_in = din("ob", [128, D // 128])
    obrow_in = din("ob_row", [1, D])
    wqT_in = din("wqT", [NFT_Q * 128, KCS * 128])
    wkT_in = din("wkT", [NFT_KV * 128, KCS * 128])
    wvT_in = din("wvT", [NFT_KV * 128, KCS * 128])
    woT_in = din("woT", [QR, D])
    kcache_in = din("kcache", [cfg["NBLK"] * cfg["BLOCK"], HD])
    vcache_in = din("vcache", [cfg["NBLK"] * cfg["BLOCK"], HD])
    blkidx_in = din("blkidx", [CT // cfg["BLOCK"], 1], I32)
    csin_in = din("csin", [128, 2])
    gq_in = din("gq", [KR // 8, INTER], I32)
    gsc_in = din("gsc", [KCS, INTER])
    gqz_in = din("gqz", [KCS, INTER // 8], I32)
    uq_in = din("uq", [KR // 8, INTER], I32)
    usc_in = din("usc", [KCS, INTER])
    uqz_in = din("uqz", [KCS, INTER // 8], I32)
    dq_in = din("dq", [INTER // 8, DQC], I32)
    dsc_in = din("dsc", [GRP_I, DQC])
    dqz_in = din("dqz", [GRP_I, DQC // 8], I32)
    mask_in = din("mask_nib", [128, 1], I32)
    pow16_in = din("pow16", [128, 1])
    identf_in = din("identf", [128, 128])
    identb_in = din("identb", [128, 128], BF16)
    mup_in = din("mup", [128, 128])
    mdn_in = din("mdn", [128, 128])

    out_ext = nc.dram_tensor("out", [T, D], F32, kind="ExternalOutput").ap()

    def dram(name, shape, dt):
        return nc.dram_tensor(name, shape, dt).ap()

    res1 = dram("res1", [T, D], F32)
    res2 = dram("res2", [T, D], F32)
    res3 = dram("res3", [T, D], F32)
    agw_qkvw_i = dram("agw_qkvw_i", [NFT_QKV * 128, KCS * 128], BF16)
    agw_qkvw_o = dram("agw_qkvw_o", [NCORES * NFT_QKV * 128, KCS * 128], BF16)
    agw_wqT_i = dram("agw_wqT_i", [NFT_Q * 128, KCS * 128], BF16)
    agw_wqT_o = dram("agw_wqT_o", [NCORES * NFT_Q * 128, KCS * 128], BF16)
    agw_wkT_i = dram("agw_wkT_i", [NFT_KV * 128, KCS * 128], BF16)
    agw_wkT_o = dram("agw_wkT_o", [NCORES * NFT_KV * 128, KCS * 128], BF16)
    agw_wvT_i = dram("agw_wvT_i", [NFT_KV * 128, KCS * 128], BF16)
    agw_wvT_o = dram("agw_wvT_o", [NCORES * NFT_KV * 128, KCS * 128], BF16)
    agw_ow_i = dram("agw_ow_i", [KR, D], BF16)
    agw_ow_o = dram("agw_ow_o", [D, D], BF16)
    agw_woT_i = dram("agw_woT_i", [QR, D], BF16)
    agw_woT_o = dram("agw_woT_o", [QF, D], BF16)
    agw_g_i = dram("agw_g_i", [NT_I * 128, KCS * 128], BF16)
    agw_g_o = dram("agw_g_o", [NCORES * NT_I * 128, KCS * 128], BF16)
    agw_u_i = dram("agw_u_i", [NT_I * 128, KCS * 128], BF16)
    agw_u_o = dram("agw_u_o", [NCORES * NT_I * 128, KCS * 128], BF16)
    agw_d_i = dram("agw_d_i", [INTER, DQC], BF16)
    agw_d_o = dram("agw_d_o", [NCORES * INTER, DQC], BF16)
    aga_k_i = dram("aga_k_i", [D, T], BF16)
    aga_k_o = dram("aga_k_o", [4 * D, T], BF16)
    aga_v_i = dram("aga_v_i", [T, D], BF16)
    aga_v_o = dram("aga_v_o", [4 * T, D], BF16)
    aga_nk_i = dram("aga_nk_i", [1, NH], F32)
    aga_nk_o = dram("aga_nk_o", [4 * 1, NH], F32)
    agb1_k_i = dram("agb1_k_i", [KVF, T], BF16)
    agb1_k_o = dram("agb1_k_o", [NCORES * KVF, T], BF16)
    agb1_v_i = dram("agb1_v_i", [T, KVF], BF16)
    agb1_v_o = dram("agb1_v_o", [NCORES * T, KVF], BF16)
    agb1_nk_i = dram("agb1_nk_i", [1, NKV], F32)
    agb1_nk_o = dram("agb1_nk_o", [NCORES * 1, NKV], F32)
    agb2_k_i = dram("agb2_k_i", [HD, CT], BF16)
    agb2_k_o = dram("agb2_k_o", [NCORES * HD, CT], BF16)
    agb2_v_i = dram("agb2_v_i", [CT, HD], BF16)
    agb2_v_o = dram("agb2_v_o", [NCORES * CT, HD], BF16)
    agb2_nk_i = dram("agb2_nk_i", [1, 1], F32)
    agb2_nk_o = dram("agb2_nk_o", [NCORES * 1, 1], F32)
    szg_d = dram("szg_d", [KCS, INTER], F32)
    szu_d = dram("szu_d", [KCS, INTER], F32)
    szd_d = dram("szd_d", [GRP_I, DQC], F32)
    ksel_d = dram("ksel_d", [c["CACHE_T"] // c["R_PER_KVH"], cfg["HD"]], F32)
    vsel_d = dram("vsel_d", [c["CACHE_T"] // c["R_PER_KVH"], cfg["HD"]], F32)

    es = ExitStack()
    sb = lambda name, shape, dt: es.enter_context(nc.sbuf_tensor(name, shape, dt))
    ps_ = lambda name, shape: es.enter_context(nc.psum_tensor(name, shape, F32))

    POOLB = 161 * 1024
    pool = sb("pool", [128, POOLB], mybir.dt.uint8)

    def carve(off, dt, shape):
        nbytes = shape[1] * {F32: 4, BF16: 2, I32: 4, U32: 4}[dt]
        assert off % 64 == 0 and off + nbytes <= POOLB, (off, nbytes)
        return pool[0:shape[0], off:off + nbytes].bitcast(dt)

    wb = [sb("wb0", [128, KC * 128], BF16), sb("wb1", [128, KC * 128], BF16)]
    rs = [sb("rs0", [128, 512], BF16), sb("rs1", [128, 512], BF16)]
    pt = [sb("pt0", [128, 512], BF16), sb("pt1", [128, 512], BF16)]
    pt2 = sb("pt2", [128, 512], BF16)
    pt3 = sb("pt3", [128, 512], BF16)
    obrow_b = sb("obrow_b", [1, cfg["D"]], BF16)
    onesrow_b = sb("onesrow_b", [1, 128], BF16)
    ev = [sb("ev0", [128, 512], F32), sb("ev1", [128, 512], F32),
          sb("ev2", [128, 512], F32), sb("ev3", [128, 512], F32)]
    identf = sb("identf_s", [128, 128], F32)
    identb = sb("identb_s", [128, 128], BF16)
    mupt = sb("mup_s", [128, 128], F32)
    mdnt = sb("mdn_s", [128, 128], F32)
    rmat = sb("rmat", [128, 128], BF16)
    rmatf = sb("rmatf", [128, 128], F32)
    qkvb_t = sb("qkvb_s", [128, 3 * NH], F32)
    ob_t = sb("ob_s", [128, D // 128], F32)
    maskt = sb("maskt", [128, 1], I32)
    pow16t = sb("pow16t", [128, 1], F32)
    cs_t = sb("cs_t", [128, 2], F32)
    onesb = sb("onesb", [128, 1], BF16)
    onesf = sb("onesf", [128, 1], F32)
    mqrow = sb("mqrow", [1, 64], F32)
    mkrow = sb("mkrow", [1, 64], F32)
    mtmp = sb("mtmp", [1, 256], F32)
    mbias = sb("mbias", [128, NH], F32)
    lnsc = sb("lnsc", [128, 8], F32)
    onesrow = sb("onesrow", [1, 128], F32)
    blkp = sb("blkp", [CT // cfg["BLOCK"], 1], I32)

    linv = sb("linv", [128, 512], F32)

    ps = [ps_(f"ps{i}", [128, 512]) for i in range(6)]
    pl = [ps_(f"pl{i}", [1, 512]) for i in range(2)]

    sems = {}
    for s in ("ld", "st", "gst", "tD", "vD", "sD", "gD", "qD",
              "ld0", "ld1", "ld2", "ld3", "ld4", "ld5",
              "ccq", "cco", "cca", "ccf", "ccw", "ccb2", "ccwo", "ccb1"):
        sems[s] = es.enter_context(nc.semaphore(s))

    bufs = dict(locals())
    return _emit(nc, c, es, bufs, sems, dbg_stop)


def _emit(nc, c, es, B, sems, dbg_stop):
    cfg = c
    D, NH, HD, NKV, INTER = c["D"], c["NH"], c["HD"], c["NKV"], c["INTER"]
    T, PANEL, NPAN, TPP, KC, KCS = c["T"], c["PANEL"], c["NPAN"], c["TPP"], c["KC"], c["KCS"]
    NTT, NT_I, QF, KVF = c["NTT"], c["NT_I"], c["QF"], c["KVF"]
    KVG, RPK = c["KVG"], c["R_PER_KVH"]
    CT = c["CACHE_T"] // RPK
    CACHE_T, NEW_T, KV_B = c["CACHE_T"], c["NEW_T"], c["KV_B"]
    BLOCK = c["BLOCK"]
    NFT_QKV, NFT_Q, NFT_KV = 3 * D // 128, QF // 128, KVF // 128
    DP = D // 512
    KR, QR, DQC, GRP_I = D // NCORES, QF // NCORES, c["DQ_COLS"], c["GRP_I"]
    SCALE = 1.0 / math.sqrt(HD)
    NPP = PANEL

    g = B
    x_in, out_ext = g["x_in"], g["out_ext"]
    res1, res2, res3 = g["res1"], g["res2"], g["res3"]
    pool, wb, rs, pt, ev = g["pool"], g["wb"], g["rs"], g["pt"], g["ev"]
    identf, identb, mupt, mdnt, rmat, rmatf = g["identf"], g["identb"], g["mupt"], g["mdnt"], g["rmat"], g["rmatf"]
    qkvb_t, ob_t, maskt, pow16t, cs_t = g["qkvb_t"], g["ob_t"], g["maskt"], g["pow16t"], g["cs_t"]
    onesb, onesf, mbias = g["onesb"], g["onesf"], g["mbias"]
    mqrow, mkrow, mtmp = g["mqrow"], g["mkrow"], g["mtmp"]
    lnsc, linv = g["lnsc"], g["linv"]
    onesrow, blkp = g["onesrow"], g["blkp"]
    ksel_d, vsel_d = g["ksel_d"], g["vsel_d"]
    pt2, pt3, obrow_b, onesrow_b = g["pt2"], g["pt3"], g["obrow_b"], g["onesrow_b"]
    ps, pl = g["ps"], g["pl"]

    LNT_OFF, QT_OFF, KV_OFF, XT_OFF, XNB_OFF = 0, 32 * 1024, 96 * 1024, 121 * 1024, 153 * 1024
    def lnT():
        return pool[0:128, LNT_OFF:LNT_OFF + KC * PANEL * 2].bitcast(BF16)
    def qt():
        return pool[0:128, QT_OFF:QT_OFF + NH * T * 2].bitcast(BF16)
    def hT():
        return pool[0:128, QT_OFF:QT_OFF + NT_I * PANEL * 2].bitcast(BF16)
    def kvK(s):
        off = KV_OFF + s * KV_B * 2
        return pool[0:128, off:off + KV_B * 2].bitcast(BF16)
    def kvV(s):
        off = KV_OFF + (2 + s) * ((KV_B // 128) * 128 * 2)
        return pool[0:128, off:off + (KV_B // 128) * 128 * 2].bitcast(BF16)
    def xt(s):
        off = XT_OFF + s * D * 4
        return pool[0:128, off:off + D * 4].bitcast(F32)
    def xnb():
        return pool[0:128, XNB_OFF:XNB_OFF + D * 2].bitcast(BF16)
    DQN = min(2816, INTER)
    DQNB = DQN * 4
    DQSL = 4 * DQNB + DQN * 2
    def dq_q(sl):
        return pool[0:128, sl * DQSL:sl * DQSL + DQNB].bitcast(I32)
    def dq_s(sl):
        return pool[0:128, sl * DQSL + DQNB:sl * DQSL + 2 * DQNB].bitcast(F32)
    def dq_z(sl):
        return pool[0:128, sl * DQSL + 2 * DQNB:sl * DQSL + 3 * DQNB].bitcast(F32)
    def dq_o(sl):
        return pool[0:128, sl * DQSL + 3 * DQNB:sl * DQSL + 3 * DQNB + DQN * 2].bitcast(BF16)
    def dqf(sl):
        return pool[0:128, sl * DQSL + 3 * DQNB + DQN * 2:sl * DQSL + 4 * DQNB + DQN * 2].bitcast(F32)
    GOFF = 4 * DQNB + DQN * 2
    def gk(s):
        off = GOFF + s * HD * 4
        return pool[0:128, off:off + HD * 4].bitcast(F32)
    def gkv(s):
        off = GOFF + (2 + s) * HD * 4
        return pool[0:128, off:off + HD * 4].bitcast(F32)
    def gkb(s):
        off = GOFF + 4 * HD * 4 + s * HD * 2
        return pool[0:128, off:off + HD * 2].bitcast(BF16)
    def gvb(s):
        off = GOFF + 4 * HD * 4 + (2 + s) * HD * 2
        return pool[0:128, off:off + HD * 2].bitcast(BF16)
    def gsq(s):
        off = GOFF + 4 * HD * 4 + 4 * HD * 2 + s * HD * 2
        return pool[0:128, off:off + HD * 2].bitcast(BF16)

    agw_qkvw = g["agw_qkvw_o"].rearrange("(r f p) k -> r f p k", r=NCORES, p=128)
    agw_wqT = g["agw_wqT_o"].rearrange("(r f p) k -> r f p k", r=NCORES, p=128)
    agw_wkT = g["agw_wkT_o"].rearrange("(r f p) k -> r f p k", r=NCORES, p=128)
    agw_wvT = g["agw_wvT_o"].rearrange("(r f p) k -> r f p k", r=NCORES, p=128)
    agw_g = g["agw_g_o"].rearrange("(r f p) k -> r f p k", r=NCORES, p=128)
    agw_u = g["agw_u_o"].rearrange("(r f p) k -> r f p k", r=NCORES, p=128)
    agw_d = g["agw_d_o"].rearrange("(r i) d -> r i d", r=NCORES)
    aga_k = g["aga_k_o"].rearrange("(r f) t -> r f t", r=4)
    aga_v = g["aga_v_o"].rearrange("(r t) f -> r t f", r=4)
    aga_nk = g["aga_nk_o"]
    agb1_k = g["agb1_k_o"].rearrange("(r f) t -> r f t", r=NCORES)
    agb1_v = g["agb1_v_o"].rearrange("(r t) f -> r t f", r=NCORES)
    agb1_nk = g["agb1_nk_o"]
    agb2_k = g["agb2_k_o"].rearrange("(r f) t -> r f t", r=NCORES)
    agb2_v = g["agb2_v_o"].rearrange("(r t) f -> r t f", r=NCORES)
    agb2_nk = g["agb2_nk_o"]

    def lhsT_view(agw_view, ft):
        return agw_view[:, ft].rearrange("r p k -> p r k")

    RES = [x_in, res1, res2, res3, out_ext]

    def psT(i):
        return ps[i][:, 0:64].bitcast(BF16)

    def ph_prep(ctx):
        C = ctx
        C.dma("sync", "ld", identf[:], g["identf_in"])
        C.dma("sync", "ld", identb[:], g["identb_in"], fence=False)
        C.dma("sync", "ld", mupt[:], g["mup_in"], fence=False)
        C.dma("sync", "ld", mdnt[:], g["mdn_in"], fence=False)
        C.dma("sync", "ld", qkvb_t[:], g["qkvb_in"], fence=False)
        C.dma("sync", "ld", ob_t[:], g["ob_in"], fence=False)
        C.dma("sync", "ld", maskt[:], g["mask_in"], fence=False)
        C.dma("sync", "ld", pow16t[:], g["pow16_in"], fence=False)
        C.dma("sync", "ld", cs_t[:], g["csin_in"], fence=False)
        C.dma("sync", "ld", blkp[:], g["blkidx_in"], fence=False)
        C.dma("gpsimd", "gst", obrow_b[0:1, 0:D], g["obrow_in"])
        ld_small = C.cnt["ld"]

        C.wait("vector", "ld", ld_small)
        C.op("vector", lambda e: e.memset(onesf[:], 1.0))
        C.op("vector", lambda e: e.memset(onesb[:], 1.0))
        C.op("vector", lambda e: e.tensor_scalar(rmatf[:], identf[:], cs_t[:, 0:1], None, ALU.mult))
        C.op("vector", lambda e: e.tensor_scalar(mupt[:], mupt[:], cs_t[:, 1:2], None, ALU.mult))
        C.op("vector", lambda e: e.tensor_scalar(mdnt[:], mdnt[:], cs_t[:, 1:2], None, ALU.mult))
        C.drain("vector")
        C.op("vector", lambda e: e.tensor_tensor(rmatf[:], rmatf[:], mupt[:], ALU.add))
        C.drain("vector")
        C.op("vector", lambda e: e.tensor_tensor(rmatf[:], rmatf[:], mdnt[:], ALU.subtract), sem="vD")
        vD_rmat = C.cnt["vD"]
        C.wait("scalar", "vD", vD_rmat)
        C.op("scalar", lambda e: e.copy(rmat[:], rmatf[:]), sem="sD")

        C.op("vector", lambda e: e.memset(onesrow[:], 1.0))
        C.op("vector", lambda e: e.memset(onesrow_b[:], 1.0))
        C.marks.update(ld_small=ld_small)

    def ph_gather(ctx):
        C = ctx
        NTAU = CT // 128
        NBLK_SEL = CT // cfg["BLOCK"]
        BW = cfg["BLOCK"] * HD
        GP = 4
        PW = BW // GP
        def gsb(sl):
            return pool[0:NBLK_SEL, sl * PW * 4:(sl + 1) * PW * 4].bitcast(F32)
        C.wait("gpsimd", "ld", C.marks["ld_small"])
        kc_v = g["kcache_in"].rearrange("(b t) h -> b (t h)", t=cfg["BLOCK"])
        vc_v = g["vcache_in"].rearrange("(b t) h -> b (t h)", t=cfg["BLOCK"])
        ks_v = ksel_d.rearrange("(b t) h -> b (t h)", t=cfg["BLOCK"])
        vs_v = vsel_d.rearrange("(b t) h -> b (t h)", t=cfg["BLOCK"])
        st_g0 = C.cnt["st"]
        ii = 0
        for srcv, dstv in ((kc_v, ks_v), (vc_v, vs_v)):
            for j in range(GP):
                sl = ii % 2
                if ii >= 2:
                    C.wait("gpsimd", "st", st_g0 + 16 * (ii - 1))
                C.idma("gpsimd", "gst", gsb(sl)[:, 0:PW], srcv,
                       blkp[:, 0:1], element_offset=j * PW)
                C.wait("sync", "gst", C.cnt["gst"])
                C.dma("sync", "st", dstv[:, j * PW:(j + 1) * PW], gsb(sl)[:, 0:PW])
                ii += 1
        C.wait("sync", "st", C.cnt["st"])
        C.op("vector", lambda e: e.memset(mtmp[0:1, 0:1], -3e38))
        C.drain("vector")
        tD0, sD0, vD0, st0 = C.cnt["tD"], C.cnt["sD"], C.cnt["vD"], C.cnt["st"]
        for tau in range(NTAU):
            s = tau % 2
            grp, gi = tau // 4, tau % 4
            if tau >= 2:
                C.wait("sync", "tD", tD0 + 2 * (tau - 2) + 1)
                C.wait("sync", "sD", sD0 + 2 * (tau - 2) + 2 + (tau - 2) // 4)
            C.dma("sync", "ld0", gk(s)[:, 0:HD], ksel_d[tau * 128:(tau + 1) * 128, 0:HD])
            C.dma("sync", "ld0", gkv(s)[:, 0:HD], vsel_d[tau * 128:(tau + 1) * 128, 0:HD], fence=False)
            gk_mark = C.cnt["ld0"]
            C.wait("tensor", "ld0", gk_mark)
            if tau >= 2:
                C.wait("tensor", "sD", sD0 + 2 * (tau - 2) + 1 + (tau - 2) // 4)
                C.wait("tensor", "st", st0 + 32 * (tau - 1))
            C.op("tensor", lambda e, s=s: e.transpose(ps[4 + s][:, 0:128], gk(s)[:, 0:HD], identf[:]),
                 sem="tD")
            tr_mark = C.cnt["tD"]
            C.wait("scalar", "tD", tr_mark)
            if tau >= 2:
                C.wait("scalar", "st", st0 + 32 * (tau - 1))
            C.op("scalar", lambda e, s=s: e.activation(gkb(s)[:, 0:HD], ps[4 + s][:, 0:128], ACT.Identity),
                 sem="sD")
            kevac = C.cnt["sD"]
            C.wait("scalar", "ld0", gk_mark)
            C.op("scalar", lambda e, s=s: e.activation(gvb(s)[:, 0:HD], gkv(s)[:, 0:HD], ACT.Identity), sem="sD")
            vevac = C.cnt["sD"]
            C.wait("vector", "sD", kevac)
            if tau >= 2:
                C.wait("vector", "tD", tD0 + 2 * (tau - 2) + 2)
            C.op("vector", lambda e, s=s: e.tensor_tensor(gsq(s)[:, 0:128], gkb(s)[:, 0:HD], gkb(s)[:, 0:HD],
                                                          ALU.mult), sem="vD")
            C.wait("tensor", "vD", C.cnt["vD"])
            if gi == 0 and grp > 0:
                C.wait("tensor", "sD", sD0 + 2 * tau + grp)
            C.op("tensor", lambda e, s=s, gi=gi: e.matmul(pl[0][0:1, gi * 128:(gi + 1) * 128], onesb[:],
                                                          gsq(s)[:, 0:128], start=True, stop=True), sem="tD")
            C.wait("sync", "sD", kevac)
            C.dma("sync", "st", g["agb2_k_i"][0:HD, tau * 128:(tau + 1) * 128], gkb(s)[:, 0:HD])
            C.wait("sync", "sD", vevac)
            C.dma("sync", "st", g["agb2_v_i"][tau * 128:(tau + 1) * 128, 0:HD], gvb(s)[:, 0:HD])
            if gi == 3 or tau == NTAU - 1:
                C.wait("scalar", "tD", C.cnt["tD"])
                C.op("scalar", lambda e, gi=gi: e.activation(
                    linv[0:1, 0:(gi + 1) * 128], pl[0][0:1, 0:(gi + 1) * 128], ACT.Identity), sem="sD")
                C.wait("vector", "sD", C.cnt["sD"])
                C.op("vector", lambda e, gi=gi: e.tensor_reduce(
                    mtmp[0:1, 1:2], linv[0:1, 0:(gi + 1) * 128], mybir.AxisListType.X, ALU.max))
                C.drain("vector")
                C.op("vector", lambda e: e.tensor_tensor(mtmp[0:1, 0:1], mtmp[0:1, 0:1], mtmp[0:1, 1:2],
                                                         ALU.max), sem="vD")
                C.drain("vector")
        C.wait("sync", "vD", C.cnt["vD"])
        C.dma("sync", "st", g["agb2_nk_i"][0:1, 0:1], mtmp[0:1, 0:1])
        C.marks["st_agb2"] = C.cnt["st"]

    ZW = min(5504, max(INTER, DQC))
    Z0 = GOFF + 4 * 1024
    def zrow(R):
        return pool[0:R, Z0:Z0 + ZW * 4].bitcast(F32)
    def scrow(R):
        return pool[0:R, Z0 + ZW * 4:Z0 + 2 * ZW * 4].bitcast(F32)
    def qzt(R):
        return pool[0:R, Z0 + 2 * ZW * 4:Z0 + 2 * ZW * 4 + (ZW // 8) * 4].bitcast(I32)
    def zqi2(R):
        return pool[0:R, Z0 + 2 * ZW * 4 + 704 * 4:Z0 + 2 * ZW * 4 + 1408 * 4].bitcast(I32)
    def castb(s):
        off = Z0 + 2 * ZW * 4 + 1408 * 4 + s * 8 * 1024
        return pool[0:128, off:off + 8 * 1024].bitcast(BF16)

    def ph_wcast(ctx):
        C = ctx
        jobs = []
        for src, dst, npiece, width in (
            (g["qkvw_in"], g["agw_qkvw_i"], NFT_QKV, KCS * 128),
            (g["ow_in"], g["agw_ow_i"], KR // 128, D),
            (g["wqT_in"], g["agw_wqT_i"], NFT_Q, KCS * 128),
            (g["wkT_in"], g["agw_wkT_i"], NFT_KV, KCS * 128),
            (g["wvT_in"], g["agw_wvT_i"], NFT_KV, KCS * 128),
            (g["woT_in"], g["agw_woT_i"], QR // 128, D),
        ):
            for p in range(npiece):
                jobs.append((src[p * 128:(p + 1) * 128, 0:width], dst[p * 128:(p + 1) * 128, 0:width], width))
        sD0, st0 = C.cnt["sD"], C.cnt["st"]
        ldm = []
        for it, (srcap, dstap, width) in enumerate(jobs):
            s = it % 2
            if it >= 2:
                C.wait("sync", "sD", sD0 + it - 1)
            C.dma("sync", "ld" + str(s), xt(s)[:, 0:width], srcap)
            ldm.append(C.cnt["ld" + str(s)])
            C.wait("scalar", "ld" + str(s), ldm[it])
            if it >= 2:
                C.wait("scalar", "st", st0 + 16 * (it - 1))
            C.op("scalar", lambda e, s=s, w=width: e.activation(castb(s)[:, 0:w], xt(s)[:, 0:w], ACT.Identity),
                 sem="sD")
            C.wait("sync", "sD", C.cnt["sD"])
            C.dma("sync", "st", dstap, castb(s)[:, 0:width])
        C.marks["st_wcast"] = C.cnt["st"]
        n1 = NFT_QKV
        n2 = n1 + KR // 128
        C.marks["st_qkvw"] = st0 + 16 * n1
        C.marks["st_ow"] = st0 + 16 * n2

    def ph_qzsz(ctx):
        C = ctx
        for (qz_in, sc_in, sz_d, R, N) in (
            (g["gqz_in"], g["gsc_in"], g["szg_d"], KCS, INTER),
            (g["uqz_in"], g["usc_in"], g["szu_d"], KCS, INTER),
            (g["dqz_in"], g["dsc_in"], g["szd_d"], GRP_I, DQC),
        ):
            for c0 in range(0, N, ZW):
                w = min(ZW, N - c0)
                C.dma("sync", "ld", qzt(R)[0:R, 0:w // 8], qz_in[0:R, c0 // 8:(c0 + w) // 8])
                C.dma("sync", "ld", scrow(R)[0:R, 0:w], sc_in[0:R, c0:c0 + w])
                C.wait("vector", "ld", C.cnt["ld"])
                C.wait("vector", "st", C.cnt["st"])
                for jj in range(8):
                    C.op("vector", lambda e, jj=jj, R=R, w=w: e.tensor_scalar(
                        zqi2(R)[0:R, 0:w // 8], qzt(R)[0:R, 0:w // 8], 4 * jj, None, ALU.logical_shift_right))
                    C.drain("vector")
                    C.op("vector", lambda e, R=R, w=w: e.tensor_scalar(
                        zqi2(R)[0:R, 0:w // 8], zqi2(R)[0:R, 0:w // 8], 15, None, ALU.bitwise_and))
                    C.drain("vector")
                    C.op("vector", lambda e, jj=jj, R=R, w=w: e.tensor_copy(
                        zrow(R)[0:R, 0:w].rearrange("r (n e) -> r n e", e=8)[:, :, jj:jj + 1],
                        zqi2(R)[0:R, 0:w // 8].unsqueeze(2)))
                    C.drain("vector")
                C.op("vector", lambda e, R=R, w=w: e.tensor_scalar(zrow(R)[0:R, 0:w], zrow(R)[0:R, 0:w],
                                                                   1.0, None, ALU.add))
                C.drain("vector")
                C.op("vector", lambda e, R=R, w=w: e.tensor_tensor(zrow(R)[0:R, 0:w], zrow(R)[0:R, 0:w],
                                                                   scrow(R)[0:R, 0:w], ALU.mult), sem="vD")
                C.wait("sync", "vD", C.cnt["vD"])
                C.dma("sync", "st", sz_d[0:R, c0:c0 + w], zrow(R)[0:R, 0:w])
        C.marks["st_sz"] = C.cnt["st"]

    def ph_trigger1(ctx):
        C = ctx
        C.wait("gpsimd", "st", C.marks["st_qkvw"])
        C.collective("ccq", g["agw_qkvw_i"], g["agw_qkvw_o"])
        C.wait("gpsimd", "st", C.marks["st_ow"])
        C.collective("cco", g["agw_ow_i"], g["agw_ow_o"])

    def ph_dequant(ctx):
        C = ctx
        agg_i = g["agw_g_i"].rearrange("(f p) k -> f p k", p=128).rearrange("f p k -> p f k")
        agu_i = g["agw_u_i"].rearrange("(f p) k -> f p k", p=128).rearrange("f p k -> p f k")
        jobs = []
        for (q_in, sz_d, sc_in, dsti, R, N, isdown) in (
            (g["gq_in"], g["szg_d"], g["gsc_in"], agg_i, KCS, INTER, False),
            (g["uq_in"], g["szu_d"], g["usc_in"], agu_i, KCS, INTER, False),
            (g["dq_in"], g["szd_d"], g["dsc_in"], None, GRP_I, DQC, True),
        ):
            for j in range(R):
                for c0 in range(0, N, DQN):
                    w = min(DQN, N - c0)
                    jobs.append((q_in, sz_d, sc_in, dsti, j, c0, w, isdown))
        vD0 = C.cnt["vD"]
        st_dq0 = C.cnt["st"]
        pend_st = None
        for it, (q_in, sz_d, sc_in, dsti, j, c0, w, isdown) in enumerate(jobs):
            sl = it % 2
            lq = "ld" + str(sl)
            if it >= 2:
                C.wait("sync", "vD", vD0 + 2 * (it - 1))
            C.dma("sync", lq, dq_q(sl)[:, 0:w],
                  q_in[16 * j:16 * (j + 1), c0:c0 + w].unsqueeze(1).broadcast_to([16, 8, w]))
            C.dma("sync", lq, dq_s(sl)[:, 0:w],
                  sc_in[j:j + 1, c0:c0 + w].broadcast_to([128, w]), fence=False)
            C.dma("sync", lq, dq_z(sl)[:, 0:w],
                  sz_d[j:j + 1, c0:c0 + w].broadcast_to([128, w]), fence=False)
            ldmk = C.cnt[lq]
            if pend_st is not None:
                vmk, pdst, psrc = pend_st
                C.wait("sync", "vD", vmk)
                C.dma("sync", "st", pdst, psrc)
            C.wait("vector", lq, ldmk)
            if it >= 2:
                C.wait("vector", "st", st_dq0 + 16 * (it - 1))
            C.op("vector", lambda e, w=w, sl=sl: e.tensor_tensor(
                dq_q(sl)[:, 0:w], dq_q(sl)[:, 0:w], maskt[:, 0:1].broadcast_to([128, w]),
                ALU.bitwise_and))
            C.drain("vector")
            C.op("vector", lambda e, w=w, sl=sl: e.scalar_tensor_tensor(
                dqf(sl)[:, 0:w], dq_q(sl)[:, 0:w], pow16t[:, 0:1], dq_s(sl)[:, 0:w],
                ALU.mult, ALU.mult), sem="vD")
            C.drain("vector")
            C.op("vector", lambda e, w=w, sl=sl: e.tensor_tensor(
                dq_o(sl)[:, 0:w], dqf(sl)[:, 0:w], dq_z(sl)[:, 0:w], ALU.subtract), sem="vD")
            if isdown:
                dst = g["agw_d_i"][j * 128:(j + 1) * 128, 0:w]
            else:
                ft0, ft1 = c0 // 128, (c0 + w) // 128
                dst = dsti[:, ft0:ft1, j * 128:(j + 1) * 128]
            pend_st = (C.cnt["vD"], dst, dq_o(sl)[:, 0:w] if isdown else
                       dq_o(sl)[:, 0:w].rearrange("p (f z) -> p f z", z=128))
        vmk, pdst, psrc = pend_st
        C.wait("sync", "vD", vmk)
        C.dma("sync", "st", pdst, psrc)
        C.marks["st_dq"] = C.cnt["st"]

    def ph_trigger2(ctx):
        C = ctx
        C.wait("gpsimd", "st", C.marks["st_dq"])
        C.collective("ccf", g["agw_g_i"], g["agw_g_o"])
        C.collective("ccf", g["agw_u_i"], g["agw_u_o"])
        C.collective("ccf", g["agw_d_i"], g["agw_d_o"])
        C.collective("ccw", g["agw_wqT_i"], g["agw_wqT_o"])
        C.collective("ccw", g["agw_wkT_i"], g["agw_wkT_o"])
        C.collective("ccw", g["agw_wvT_i"], g["agw_wvT_o"])
        C.wait("gpsimd", "st", C.marks["st_agb2"])
        C.collective("ccb2", g["agb2_k_i"], g["agb2_k_o"])
        C.collective("ccb2", g["agb2_v_i"], g["agb2_v_o"])
        C.collective("ccb2", g["agb2_nk_i"], g["agb2_nk_o"])
        C.collective("ccwo", g["agw_woT_i"], g["agw_woT_o"])

    def ln_panel(ctx, src_rows, pan):
        C = ctx
        norm_marks, ev_marks = {}, {}
        for tt in range(TPP):
            i = pan * TPP + tt
            s = tt % 2
            if tt >= 2:
                C.wait("sync", "sD", norm_marks[tt - 2])
            C.dma("sync", "ld" + str(s), xt(s)[:, 0:D], src_rows[i * 128:(i + 1) * 128, 0:D])
            ldmk = C.cnt["ld" + str(s)]
            C.wait("scalar", "ld" + str(s), ldmk)
            C.op("scalar", lambda e, s=s: e.activation(xnb()[:, 0:D], xt(s)[:, 0:D], ACT.Identity,
                                                       accum_out=lnsc[:, 0:1]))
            C.drain("scalar")
            C.op("scalar", lambda e, s=s: e.activation(xnb()[:, 0:D], xt(s)[:, 0:D], ACT.Square,
                                                       accum_out=lnsc[:, 1:2]), sem="sD")
            C.wait("vector", "sD", C.cnt["sD"])
            C.op("vector", lambda e: e.tensor_scalar(lnsc[:, 2:3], lnsc[:, 0:1], 1.0 / D, None, ALU.mult))
            C.op("vector", lambda e: e.tensor_scalar(lnsc[:, 3:4], lnsc[:, 1:2], 1.0 / D, None, ALU.mult))
            C.drain("vector")
            C.op("vector", lambda e: e.tensor_tensor(lnsc[:, 4:5], lnsc[:, 2:3], lnsc[:, 2:3], ALU.mult))
            C.drain("vector")
            C.op("vector", lambda e: e.tensor_tensor(lnsc[:, 4:5], lnsc[:, 3:4], lnsc[:, 4:5], ALU.subtract))
            C.drain("vector")
            C.op("vector", lambda e: e.tensor_scalar(lnsc[:, 4:5], lnsc[:, 4:5], 1e-5, None, ALU.add), sem="vD")
            C.wait("scalar", "vD", C.cnt["vD"])
            C.op("scalar", lambda e: e.activation(lnsc[:, 5:6], lnsc[:, 4:5], ACT.Sqrt), sem="sD")
            C.wait("vector", "sD", C.cnt["sD"])
            C.op("vector", lambda e: e.reciprocal(lnsc[:, 6:7], lnsc[:, 5:6]))
            C.drain("vector")
            C.op("vector", lambda e: e.tensor_tensor(lnsc[:, 7:8], lnsc[:, 6:7], lnsc[:, 6:7], ALU.mult))
            C.drain("vector")
            C.op("vector", lambda e: e.tensor_tensor(lnsc[:, 7:8], lnsc[:, 4:5], lnsc[:, 7:8], ALU.mult))
            C.drain("vector")
            C.op("vector", lambda e: e.tensor_scalar(lnsc[:, 7:8], lnsc[:, 7:8], -0.5, 1.5, ALU.mult, ALU.add))
            C.drain("vector")
            C.op("vector", lambda e: e.tensor_tensor(lnsc[:, 6:7], lnsc[:, 6:7], lnsc[:, 7:8], ALU.mult))
            C.drain("vector")
            C.op("vector", lambda e: e.tensor_tensor(lnsc[:, 7:8], lnsc[:, 2:3], lnsc[:, 6:7], ALU.mult))
            C.drain("vector")
            C.op("vector", lambda e: e.tensor_scalar(lnsc[:, 7:8], lnsc[:, 7:8], -1.0, None, ALU.mult), sem="vD")
            C.wait("scalar", "vD", C.cnt["vD"])
            C.op("scalar", lambda e, s=s: e.activation(xnb()[:, 0:D], xt(s)[:, 0:D], ACT.Identity,
                                                       bias=lnsc[:, 7:8], scale=lnsc[:, 6:7]), sem="sD")
            norm_marks[tt] = C.cnt["sD"]
            nrm = C.cnt["sD"]
            for kc in range(KC):
                b = kc % 2
                C.wait("tensor", "sD", nrm if kc < 2 else ev_marks[(tt, kc - 2)])
                C.op("tensor", lambda e, b=b, kc=kc: e.transpose(psT(4 + b),
                                                                 xnb()[:, kc * 128:(kc + 1) * 128], identb[:]),
                     sem="tD")
                C.wait("scalar", "tD", C.cnt["tD"])
                C.op("scalar", lambda e, b=b, kc=kc, tt=tt: e.activation(
                    lnT()[:, kc * PANEL + tt * 128: kc * PANEL + (tt + 1) * 128],
                    psT(4 + b), ACT.Identity), sem="sD")
                ev_marks[(tt, kc)] = C.cnt["sD"]
        C.marks["lnT_sD"] = C.cnt["sD"]

    def gemm_panel(ctx, pan, jobs, rhs_off, epi, epi_bank_free, nkc=None, extra_k1=None):
        C = ctx
        NKC = KC if nkc is None else nkc
        mm_done = {}
        for it, (view, ft) in enumerate(jobs):
            s = it % 2
            if it >= 2:
                C.wait("sync", "tD", mm_done[it - 2])
            C.dma("sync", "ld" + str(s), wb[s][:, 0:NKC * 128], lhsT_view(view, ft))
            ldmk = C.cnt["ld" + str(s)]
            C.wait("tensor", "ld" + str(s), ldmk)
            if it >= 2:
                C.wait("tensor", *epi_bank_free(it - 2))
            bank = ps[it % 2]
            for kc in range(NKC):
                C.op("tensor", lambda e, kc=kc, s=s, bank=bank: e.matmul(
                    bank[:, 0:PANEL], wb[s][:, kc * 128:(kc + 1) * 128],
                    lnT()[:, kc * PANEL + rhs_off:kc * PANEL + rhs_off + PANEL],
                    start=(kc == 0), stop=(kc == NKC - 1 and extra_k1 is None)), sem="tD")
            if extra_k1 is not None:
                extra_k1(C, it, bank)
            mm_done[it] = C.cnt["tD"]
            epi(C, bank, ft, pan, it)

    def ph_lnqkv(ctx):
        C = ctx
        C.op("vector", lambda e: e.memset(mqrow[:], -3e38))
        C.op("vector", lambda e: e.memset(mkrow[:], -3e38))
        C.drain("vector")
        C.wait("tensor", "ccq", C.cnt["ccq"])
        C.wait("sync", "ccq", C.cnt["ccq"])
        for pan in range(NPAN):
            ln_panel(C, x_in, pan)
            C.barrier()
            fts = ([("k", NH + h) for h in range(NH)] + [("v", 2 * NH + h) for h in range(NH)]
                   + [("q", h) for h in range(NH)])
            bank_free = {-2: ("sD", 0), -1: ("sD", 0)}
            ptfree = {-2: ("sD", 0), -1: ("sD", 0)}
            sqfree = [("sD", 0)]
            vtfree = [("sD", 0)]
            trfree = {0: ("sD", 0), 1: ("sD", 0)}
            nrmfree = [("sD", 0)]
            linvfree = [("vD", 0)]

            def epi(C2, bank, ftk, pan2, it):
                kind, ft = fts[it]
                h = ft % NH
                C2.wait("scalar", "tD", C2.cnt["tD"])
                if kind == "q":
                    dst = qt()[:, h * T + pan2 * PANEL: h * T + pan2 * PANEL + PANEL]
                    C2.op("scalar", lambda e, dst=dst, ft=ft: e.activation(
                        dst, bank[:, 0:PANEL], ACT.Identity, bias=qkvb_t[:, ft:ft + 1]), sem="sD")
                    evk = C2.cnt["sD"]
                    bank_free[it] = ("sD", evk)
                    ptfree[it] = ptfree[it - 2]
                    C2.wait("vector", "sD", evk)
                    C2.wait("vector", *sqfree[0])
                    C2.op("vector", lambda e, dst=dst: e.tensor_tensor(pt2[:, 0:PANEL], dst, dst, ALU.mult),
                          sem="vD")
                    C2.wait("tensor", "vD", C2.cnt["vD"])
                    C2.wait("tensor", *nrmfree[0])
                    C2.op("tensor", lambda e: e.matmul(pl[0][0:1, 0:PANEL], onesb[:], pt2[:, 0:PANEL],
                                                       start=True, stop=True), sem="tD")
                    sqfree[0] = ("tD", C2.cnt["tD"])
                    C2.wait("scalar", "tD", C2.cnt["tD"])
                    C2.wait("scalar", *linvfree[0])
                    C2.op("scalar", lambda e: e.activation(
                        linv[0:1, 0:PANEL], pl[0][0:1, 0:PANEL], ACT.Identity), sem="sD")
                    nrmfree[0] = ("sD", C2.cnt["sD"])
                    C2.wait("vector", "sD", C2.cnt["sD"])
                    C2.op("vector", lambda e: e.tensor_reduce(
                        mtmp[0:1, 2:3], linv[0:1, 0:PANEL], mybir.AxisListType.X, ALU.max))
                    C2.drain("vector")
                    C2.op("vector", lambda e, h=h: e.tensor_tensor(
                        mqrow[0:1, h:h + 1], mqrow[0:1, h:h + 1], mtmp[0:1, 2:3], ALU.max), sem="vD")
                    C2.drain("vector")
                    linvfree[0] = ("vD", C2.cnt["vD"])
                elif kind == "k":
                    C2.wait("scalar", *ptfree[it - 2])
                    sl = pt[it % 2]
                    C2.op("scalar", lambda e, sl=sl, ft=ft: e.activation(
                        sl[:, 0:PANEL], bank[:, 0:PANEL], ACT.Identity, bias=qkvb_t[:, ft:ft + 1]), sem="sD")
                    evk = C2.cnt["sD"]
                    bank_free[it] = ("sD", evk)
                    C2.wait("sync", "sD", evk)
                    C2.dma("sync", "st", g["aga_k_i"][h * 128:(h + 1) * 128, pan2 * PANEL:pan2 * PANEL + PANEL],
                           sl[:, 0:PANEL])
                    stk = C2.cnt["st"]
                    C2.wait("vector", "sD", evk)
                    C2.wait("vector", *sqfree[0])
                    C2.op("vector", lambda e, sl=sl: e.tensor_tensor(pt2[:, 0:PANEL], sl[:, 0:PANEL],
                                                                     sl[:, 0:PANEL], ALU.mult), sem="vD")
                    vsq = C2.cnt["vD"]
                    C2.wait("tensor", "vD", vsq)
                    C2.wait("tensor", *nrmfree[0])
                    C2.op("tensor", lambda e: e.matmul(pl[0][0:1, 0:PANEL], onesb[:], pt2[:, 0:PANEL],
                                                       start=True, stop=True), sem="tD")
                    sqfree[0] = ("tD", C2.cnt["tD"])
                    C2.wait("scalar", "tD", C2.cnt["tD"])
                    C2.wait("scalar", *linvfree[0])
                    C2.op("scalar", lambda e: e.activation(
                        linv[0:1, 0:PANEL], pl[0][0:1, 0:PANEL], ACT.Identity), sem="sD")
                    nrmfree[0] = ("sD", C2.cnt["sD"])
                    C2.wait("vector", "sD", C2.cnt["sD"])
                    C2.op("vector", lambda e: e.tensor_reduce(
                        mtmp[0:1, 2:3], linv[0:1, 0:PANEL], mybir.AxisListType.X, ALU.max))
                    C2.drain("vector")
                    C2.op("vector", lambda e, h=h: e.tensor_tensor(
                        mkrow[0:1, h:h + 1], mkrow[0:1, h:h + 1], mtmp[0:1, 2:3], ALU.max), sem="vD")
                    C2.drain("vector")
                    linvfree[0] = ("vD", C2.cnt["vD"])
                    ptfree[it] = ("st", stk)
                else:
                    C2.wait("scalar", *ptfree[it - 2])
                    sl = pt[it % 2]
                    C2.op("scalar", lambda e, sl=sl, ft=ft: e.activation(
                        sl[:, 0:PANEL], bank[:, 0:PANEL], ACT.Identity, bias=qkvb_t[:, ft:ft + 1]), sem="sD")
                    evk = C2.cnt["sD"]
                    bank_free[it] = ("sD", evk)
                    lasttr = 0
                    for j in range(TPP):
                        C2.wait("tensor", "sD", evk)
                        C2.wait("tensor", *trfree[j % 2])
                        C2.op("tensor", lambda e, j=j, sl=sl: e.transpose(
                            psT(4 + j % 2), sl[:, j * 128:(j + 1) * 128], identb[:]), sem="tD")
                        lasttr = C2.cnt["tD"]
                        C2.wait("scalar", "tD", lasttr)
                        if j == 0:
                            C2.wait("scalar", *vtfree[0])
                        C2.op("scalar", lambda e, j=j: e.activation(
                            pt3[:, j * 128:(j + 1) * 128], psT(4 + j % 2), ACT.Identity), sem="sD")
                        trfree[j % 2] = ("sD", C2.cnt["sD"])
                    C2.wait("sync", "sD", C2.cnt["sD"])
                    C2.dma("sync", "st",
                           g["aga_v_i"][pan2 * PANEL:(pan2 + 1) * PANEL, h * 128:(h + 1) * 128]
                           .rearrange("(j p) c -> p j c", p=128),
                           pt3[:, 0:PANEL].rearrange("p (j cc) -> p j cc", cc=128))
                    vtfree[0] = ("st", C2.cnt["st"])
                    ptfree[it] = ("tD", lasttr)
                    if pan2 == NPAN - 1 and it == 2 * NH - 1:
                        C2.wait("sync", "vD", C2.cnt["vD"])
                        C2.dma("sync", "st", g["aga_nk_i"][0:1, 0:NH], mkrow[0:1, 0:NH])
                        C2.marks["st_aga"] = C2.cnt["st"]

            gemm_panel(C, pan, [(agw_qkvw, ft) for _, ft in fts], 0, epi,
                       lambda it: bank_free[it])
            if pan < NPAN - 1:
                C.barrier()

    def ph_trigger_aga(ctx):
        C = ctx
        C.wait("gpsimd", "st", C.marks["st_aga"])
        grp = [[0, 1, 2, 3], [4, 5, 6, 7]]
        C.collective("cca", g["aga_k_i"], g["aga_k_o"], groups=grp)
        C.collective("cca", g["aga_v_i"], g["aga_v_o"], groups=grp)
        C.collective("cca", g["aga_nk_i"], g["aga_nk_o"], groups=grp)

    def mprep(ctx, which):
        C = ctx
        if which == "a":
            C.wait("sync", "cca", C.cnt["cca"])
            C.dma("sync", "ld", mtmp[0:1, 64:64 + NH * 4],
                  g["aga_nk_o"].rearrange("r h -> h r"), slow=True)
            C.wait("vector", "ld", C.cnt["ld"])
            C.op("vector", lambda e: e.tensor_reduce(
                mkrow[0:1, 0:NH], mtmp[0:1, 64:64 + NH * 4].rearrange("o (h r) -> o h r", r=4),
                mybir.AxisListType.X, ALU.max), sem="vD")
            C.drain("vector")
        else:
            C.wait("sync", "ccb1", C.cnt["ccb1"])
            C.wait("sync", "ccb2", C.cnt["ccb2"])
            C.dma("sync", "ld", mtmp[0:1, 64:64 + NKV * 4],
                  g["agb1_nk_o"][0:4].rearrange("r h -> h r"), slow=True)
            C.dma("sync", "ld", mtmp[0:1, 160:160 + NCORES], g["agb2_nk_o"].rearrange("r o -> o r"),
                  fence=False, slow=True)
            C.wait("vector", "ld", C.cnt["ld"])
            C.op("vector", lambda e: e.tensor_reduce(
                mkrow[0:1, 0:NKV], mtmp[0:1, 64:64 + NKV * 4].rearrange("o (h r) -> o h r", r=4),
                mybir.AxisListType.X, ALU.max))
            C.drain("vector")
            C.op("vector", lambda e: e.tensor_reduce(
                mtmp[0:1, 0:NKV], mtmp[0:1, 160:160 + NCORES].rearrange("o (h r) -> o h r", r=RPK),
                mybir.AxisListType.X, ALU.max))
            C.drain("vector")
            C.op("vector", lambda e: e.tensor_tensor(
                mkrow[0:1, 0:NKV], mkrow[0:1, 0:NKV], mtmp[0:1, 0:NKV], ALU.max))
            C.drain("vector")
            C.op("vector", lambda e: e.tensor_copy(
                mkrow[0:1, 32:32 + NH].rearrange("o (k gg) -> o k gg", gg=KVG),
                mkrow[0:1, 0:NKV].unsqueeze(2).broadcast_to([1, NKV, KVG])))
            C.drain("vector")
            C.op("vector", lambda e: e.tensor_copy(mkrow[0:1, 0:NH], mkrow[0:1, 32:32 + NH]), sem="vD")
            C.drain("vector")
        C.op("vector", lambda e: e.tensor_tensor(mtmp[0:1, 0:NH], mqrow[0:1, 0:NH], mkrow[0:1, 0:NH],
                                                 ALU.mult), sem="vD")
        C.wait("scalar", "vD", C.cnt["vD"])
        C.op("scalar", lambda e: e.activation(mtmp[0:1, 32:32 + NH], mtmp[0:1, 0:NH], ACT.Sqrt), sem="sD")
        C.wait("vector", "sD", C.cnt["sD"])
        C.op("vector", lambda e: e.tensor_scalar(mtmp[0:1, 0:NH], mtmp[0:1, 32:32 + NH], -SCALE, None,
                                                 ALU.mult), sem="vD")
        C.wait("tensor", "vD", C.cnt["vD"])
        C.op("tensor", lambda e: e.matmul(ps[4][:, 0:NH], onesrow[0:1, 0:128], mtmp[0:1, 0:NH],
                                          start=True, stop=True), sem="tD")
        C.wait("scalar", "tD", C.cnt["tD"])
        C.op("scalar", lambda e: e.activation(mbias[:, 0:NH], ps[4][:, 0:NH], ACT.Identity), sem="sD")

    def ph_attn(ctx, which):
        C = ctx
        if which == "a":
            groups = [[h] for h in range(NH)]
            KVLEN = cfg["S"]
        else:
            groups = [list(range(kvh * KVG, (kvh + 1) * KVG)) for kvh in range(NKV)]
            KVLEN = KV_B
        NTK = KVLEN // 128
        gmark = {}
        gdone = {-2: None, -1: None}
        plfree = {0: ("vD", 0), 1: ("vD", 0)}
        ctxfree = {0: ("vD", 0), 1: ("vD", 0)}
        laccfree = {0: ("tD", 0), 1: ("tD", 0)}
        bcast_free = ("vD", 0)
        expfree = {}
        ctxmm = {}
        vacc = {}

        for gi, grp in enumerate(groups):
            s = gi % 2
            kse, vse = "ld" + str(2 + s), "ld" + str(4 + s)
            if gi >= 2:
                C.wait("sync", "tD", gdone[gi - 2])
            if which == "a":
                h = grp[0]
                C.dma("sync", kse, kvK(s)[:, 0:KVLEN],
                      aga_k[:, h * 128:(h + 1) * 128, :].rearrange("r p t -> p r t"))
                C.dma("sync", vse, kvV(s)[:, 0:NTK * 128],
                      aga_v.rearrange("r (j p) f -> r j p f", p=128)[:, :, :, h * 128:(h + 1) * 128]
                      .rearrange("r j p f -> p r j f"))
            else:
                kvh = gi
                C.dma("sync", kse, kvK(s)[:, 0:CACHE_T],
                      agb2_k[kvh * RPK:(kvh + 1) * RPK].rearrange("r p t -> p r t") if RPK > 1
                      else agb2_k[kvh * RPK])
                C.dma("sync", kse, kvK(s)[:, CACHE_T:CACHE_T + NEW_T],
                      agb1_k[0:4, kvh * 128:(kvh + 1) * 128, :].rearrange("r p t -> p r t"))
                C.dma("sync", vse, kvV(s)[:, 0:(CACHE_T // 128) * 128],
                      agb2_v[kvh * RPK:(kvh + 1) * RPK].rearrange("r (j p) f -> p (r j) f", p=128))
                C.dma("sync", vse, kvV(s)[:, CACHE_T:CACHE_T + NEW_T],
                      agb1_v[0:4].rearrange("r (j p) f -> r j p f", p=128)[:, :, :, kvh * 128:(kvh + 1) * 128]
                      .rearrange("r j p f -> p r j f"))
            gmark[gi] = (C.cnt[kse], C.cnt[vse])
            for h in grp:
                for pan in range(NPAN):
                    qoff = h * T + pan * PANEL
                    lacc = ev[2 + pan]
                    C.wait("tensor", kse, gmark[gi][0])
                    C.wait("tensor", vse, gmark[gi][1])

                    def emit_ctx(pb, ptau):
                        C.wait("tensor", "sD", expfree[pb])
                        if ptau == 0:
                            C.wait("tensor", *ctxfree[pan])
                        C.op("tensor", lambda e, pb=pb, pan=pan, ptau=ptau, s=s: e.matmul(
                            ps[2 + pan][:, 0:PANEL], kvV(s)[:, ptau * 128:(ptau + 1) * 128],
                            pt[pb][:, 0:PANEL], start=(ptau == 0), stop=(ptau == NTK - 1)),
                            sem="tD")
                        ctxmm[pb] = C.cnt["tD"]

                    pend = None
                    for tau in range(NTK):
                        b = tau % 2
                        if expfree.get(b) is not None:
                            C.wait("tensor", "sD", expfree[b])
                        C.op("tensor", lambda e, b=b, tau=tau, qoff=qoff, s=s: e.matmul(
                            ps[b][:, 0:PANEL], kvK(s)[:, tau * 128:(tau + 1) * 128],
                            qt()[:, qoff:qoff + PANEL], start=True, stop=True), sem="tD")
                        smark = C.cnt["tD"]
                        if pend is not None:
                            emit_ctx(*pend)
                        C.wait("scalar", "tD", smark)
                        if ctxmm.get(b) is not None:
                            C.wait("scalar", "tD", ctxmm[b])
                        if vacc.get(b) is not None:
                            C.wait("scalar", "vD", vacc[b])
                        C.op("scalar", lambda e, b=b, h=h: e.activation(
                            pt[b][:, 0:PANEL], ps[b][:, 0:PANEL], ACT.Exp,
                            bias=mbias[:, h:h + 1], scale=SCALE), sem="sD")
                        expfree[b] = C.cnt["sD"]
                        C.wait("vector", "sD", expfree[b])
                        if tau == 0:
                            C.wait("vector", *laccfree[pan])
                            C.op("vector", lambda e, b=b, lacc=lacc: e.tensor_copy(
                                lacc[:, 0:PANEL], pt[b][:, 0:PANEL]), sem="vD")
                        else:
                            C.op("vector", lambda e, b=b, lacc=lacc: e.scalar_tensor_tensor(
                                lacc[:, 0:PANEL], pt[b][:, 0:PANEL], 1.0, lacc[:, 0:PANEL],
                                ALU.mult, ALU.add), sem="vD")
                        vacc[b] = C.cnt["vD"]
                        pend = (b, tau)
                    emit_ctx(*pend)
                    C.wait("tensor", "vD", vacc[pend[0]])
                    C.wait("tensor", *plfree[pan])
                    C.op("tensor", lambda e, pan=pan, lacc=lacc: e.matmul(
                        pl[pan][0:1, 0:PANEL], onesf[:], lacc[:, 0:PANEL],
                        start=True, stop=True), sem="tD")
                    laccfree[pan] = ("tD", C.cnt["tD"])
                    C.wait("vector", "tD", C.cnt["tD"])
                    C.wait("vector", *bcast_free)
                    C.op("vector", lambda e, pan=pan: e.reciprocal(linv[0:1, 0:PANEL],
                                                                   pl[pan][0:1, 0:PANEL]), sem="vD")
                    plfree[pan] = ("vD", C.cnt["vD"])
                    C.wait("tensor", "vD", C.cnt["vD"])
                    C.op("tensor", lambda e: e.matmul(ps[4][:, 0:PANEL], onesrow[0:1, 0:128],
                                                      linv[0:1, 0:PANEL], start=True, stop=True), sem="tD")
                    C.wait("scalar", "tD", C.cnt["tD"])
                    C.op("scalar", lambda e: e.activation(ev[0][:, 0:PANEL], ps[4][:, 0:PANEL],
                                                          ACT.Identity), sem="sD")
                    C.wait("vector", "sD", C.cnt["sD"])
                    C.op("vector", lambda e, pan=pan, qoff=qoff: e.tensor_tensor(
                        qt()[:, qoff:qoff + PANEL], ps[2 + pan][:, 0:PANEL], ev[0][:, 0:PANEL],
                        ALU.mult), sem="vD")
                    ctxfree[pan] = ("vD", C.cnt["vD"])
                    bcast_free = ("vD", C.cnt["vD"])
            gdone[gi] = C.cnt["tD"]

    def ph_outproj(ctx, which, res_src, res_dst):
        C = ctx
        if which == "a":
            C.wait("sync", "cco", C.cnt["cco"])
            wsrc = g["agw_ow_o"]
            bias = True
        else:
            C.wait("sync", "ccwo", C.cnt["ccwo"])
            wsrc = g["agw_woT_o"]
            bias = False
        NFC = QF // 128
        NRND = NTT // TPP
        rsfree = {-2: ("tD", 0), -1: ("tD", 0)}
        xcfree = {-2: ("vD", 0), -1: ("vD", 0)}
        evfree = {-2: ("st", 0), -1: ("st", 0)}
        bankfree = {ii: ("vD", 0) for ii in range(TPP)}
        bankdone = {}
        it_r = 0
        it_x = 0
        for dp in range(DP):
            for rnd in range(NRND):
                for fc in range(NFC):
                    sr = it_r % 2
                    if it_r >= 2:
                        C.wait("sync", *rsfree[it_r - 2])
                    C.dma("sync", "ld" + str(2 + sr), rs[sr][:, 0:512],
                          wsrc[fc * 128:(fc + 1) * 128, dp * 512:(dp + 1) * 512])
                    ldmk = C.cnt["ld" + str(2 + sr)]
                    C.wait("tensor", "ld" + str(2 + sr), ldmk)
                    for ii in range(TPP):
                        i = rnd * TPP + ii
                        if fc == 0:
                            C.wait("tensor", *bankfree[ii])
                        C.op("tensor", lambda e, sr=sr, fc=fc, ii=ii, i=i: e.matmul(
                            ps[ii][:, 0:512], qt()[:, fc * T + i * 128:fc * T + (i + 1) * 128],
                            rs[sr][:, 0:512], start=(fc == 0), stop=(fc == NFC - 1 and not bias)),
                            sem="tD")
                    rsfree[it_r] = ("tD", C.cnt["tD"])
                    it_r += 1
                if bias:
                    for ii in range(TPP):
                        C.op("tensor", lambda e, ii=ii, dp=dp: e.matmul(
                            ps[ii][:, 0:512], onesrow_b[0:1, 0:128], obrow_b[0:1, dp * 512:(dp + 1) * 512],
                            start=False, stop=True), sem="tD")
                bankdone[(dp, rnd)] = C.cnt["tD"]
                for ii in range(TPP):
                    i = rnd * TPP + ii
                    sx = it_x % 2
                    if it_x >= 2:
                        C.wait("sync", *xcfree[it_x - 2])
                    C.dma("sync", "ld" + str(4 + sx), ev[2 + sx][:, 0:512],
                          res_src[i * 128:(i + 1) * 128, dp * 512:(dp + 1) * 512])
                    ldmk = C.cnt["ld" + str(4 + sx)]
                    C.wait("vector", "ld" + str(4 + sx), ldmk)
                    C.wait("vector", "tD", bankdone[(dp, rnd)])
                    C.wait("vector", *evfree[it_x - 2])
                    C.op("vector", lambda e, ii=ii, sx=sx: e.scalar_tensor_tensor(
                        ev[sx][:, 0:512], ps[ii][:, 0:512], 1.0, ev[2 + sx][:, 0:512],
                        ALU.mult, ALU.add), sem="vD")
                    xcfree[it_x] = ("vD", C.cnt["vD"])
                    bankfree[ii] = ("vD", C.cnt["vD"])
                    C.wait("sync", "vD", C.cnt["vD"])
                    C.dma("sync", "st", res_dst[i * 128:(i + 1) * 128, dp * 512:(dp + 1) * 512],
                          ev[sx][:, 0:512])
                    evfree[it_x] = ("st", C.cnt["st"])
                    it_x += 1

    def ph_ffn(ctx, res_src, res_dst):
        C = ctx
        C.wait("sync", "ccf", C.cnt["ccf"])
        C.wait("tensor", "ccf", C.cnt["ccf"])
        RPD = max(1, 512 // DQC)
        for pan in range(NPAN):
            ln_panel(C, res_src, pan)
            C.barrier()
            gfree = ("vD", 0)
            g_sD = 0
            u_vD = 0
            gmm_prev = 0
            umm_prev = 0
            pt0free = ("tD", 0)
            for nt in range(NT_I):
                if nt >= 1:
                    C.wait("sync", "tD", gmm_prev)
                C.dma("sync", "ld0", wb[0][:, 0:KC * 128], lhsT_view(agw_g, nt))
                l0 = C.cnt["ld0"]
                if nt >= 1:
                    C.wait("sync", "tD", umm_prev)
                C.dma("sync", "ld1", wb[1][:, 0:KC * 128], lhsT_view(agw_u, nt))
                l1 = C.cnt["ld1"]
                C.wait("tensor", "ld0", l0)
                if nt >= 1:
                    C.wait("tensor", "sD", g_sD)
                for kc in range(KC):
                    C.op("tensor", lambda e, kc=kc: e.matmul(
                        ps[0][:, 0:PANEL], wb[0][:, kc * 128:(kc + 1) * 128],
                        lnT()[:, kc * PANEL:(kc + 1) * PANEL],
                        start=(kc == 0), stop=(kc == KC - 1)), sem="tD")
                gmm = C.cnt["tD"]
                C.wait("tensor", "ld1", l1)
                if nt >= 1:
                    C.wait("tensor", "vD", u_vD)
                for kc in range(KC):
                    C.op("tensor", lambda e, kc=kc: e.matmul(
                        ps[1][:, 0:PANEL], wb[1][:, kc * 128:(kc + 1) * 128],
                        lnT()[:, kc * PANEL:(kc + 1) * PANEL],
                        start=(kc == 0), stop=(kc == KC - 1)), sem="tD")
                umm = C.cnt["tD"]
                gmm_prev, umm_prev = gmm, umm
                C.wait("scalar", "tD", gmm)
                if nt >= 1:
                    C.wait("scalar", "vD", u_vD)
                C.op("scalar", lambda e: e.activation(pt[0][:, 0:PANEL], ps[0][:, 0:PANEL], ACT.Silu),
                     sem="sD")
                g_sD = C.cnt["sD"]
                C.wait("vector", "sD", g_sD)
                C.wait("vector", "tD", umm)
                C.op("vector", lambda e, nt=nt: e.tensor_tensor(
                    hT()[:, nt * PANEL:(nt + 1) * PANEL], pt[0][:, 0:PANEL], ps[1][:, 0:PANEL],
                    ALU.mult), sem="vD")
                u_vD = C.cnt["vD"]
            C.barrier()
            it_r = 0
            it_x = 0
            rsfree = {-2: ("tD", 0), -1: ("tD", 0)}
            xcfree = {-2: ("vD", 0), -1: ("vD", 0)}
            evfree = {-2: ("st", 0), -1: ("st", 0)}
            bankfree = {ii: ("vD", 0) for ii in range(TPP)}
            for dp in range(DP):
                for kc in range(NT_I):
                    sr = it_r % 2
                    if it_r >= 2:
                        C.wait("sync", *rsfree[it_r - 2])
                    r0 = dp * RPD if DQC <= 512 else dp * 512 // DQC
                    C.dma("sync", "ld" + str(2 + sr), rs[sr][:, 0:512],
                          agw_d[r0:r0 + RPD, kc * 128:(kc + 1) * 128, :].rearrange("r p cc -> p r cc"))
                    ldmk = C.cnt["ld" + str(2 + sr)]
                    C.wait("tensor", "ld" + str(2 + sr), ldmk)
                    for ii in range(TPP):
                        if kc == 0:
                            C.wait("tensor", *bankfree[ii])
                        C.op("tensor", lambda e, sr=sr, kc=kc, ii=ii: e.matmul(
                            ps[2 + ii][:, 0:512], hT()[:, kc * PANEL + ii * 128:kc * PANEL + (ii + 1) * 128],
                            rs[sr][:, 0:512], start=(kc == 0), stop=(kc == NT_I - 1)), sem="tD")
                    rsfree[it_r] = ("tD", C.cnt["tD"])
                    it_r += 1
                bankdone = C.cnt["tD"]
                for ii in range(TPP):
                    i = pan * TPP + ii
                    sx = it_x % 2
                    if it_x >= 2:
                        C.wait("sync", *xcfree[it_x - 2])
                    C.dma("sync", "ld" + str(4 + sx), ev[2 + sx][:, 0:512],
                          res_src[i * 128:(i + 1) * 128, dp * 512:(dp + 1) * 512])
                    ldmk = C.cnt["ld" + str(4 + sx)]
                    C.wait("vector", "ld" + str(4 + sx), ldmk)
                    C.wait("vector", "tD", bankdone)
                    C.wait("vector", *evfree[it_x - 2])
                    C.op("vector", lambda e, ii=ii, sx=sx: e.scalar_tensor_tensor(
                        ev[sx][:, 0:512], ps[2 + ii][:, 0:512], 1.0, ev[2 + sx][:, 0:512],
                        ALU.mult, ALU.add), sem="vD")
                    xcfree[it_x] = ("vD", C.cnt["vD"])
                    bankfree[ii] = ("vD", C.cnt["vD"])
                    C.wait("sync", "vD", C.cnt["vD"])
                    C.dma("sync", "st", res_dst[i * 128:(i + 1) * 128, dp * 512:(dp + 1) * 512],
                          ev[sx][:, 0:512])
                    evfree[it_x] = ("st", C.cnt["st"])
                    it_x += 1
            C.barrier()

    def ph_bproj(ctx, res_src):
        C = ctx
        C.op("vector", lambda e: e.memset(mqrow[:], -3e38))
        C.op("vector", lambda e: e.memset(mkrow[:], -3e38))
        C.drain("vector")
        C.wait("tensor", "ccw", C.cnt["ccw"])
        C.wait("sync", "ccw", C.cnt["ccw"])
        for pan in range(NPAN):
            ln_panel(C, res_src, pan)
            C.barrier()
            jobs = ([("k", agw_wkT, h) for h in range(NFT_KV)] + [("v", agw_wvT, h) for h in range(NFT_KV)]
                    + [("q", agw_wqT, h) for h in range(NFT_Q)])
            bank_free = {-2: ("sD", 0), -1: ("sD", 0)}
            ptfree = {-2: ("sD", 0), -1: ("sD", 0)}
            sqfree = [("sD", 0)]
            vtfree = [("sD", 0)]
            trfree = {0: ("sD", 0), 1: ("sD", 0)}
            nrmfree = [("sD", 0)]
            linvfree = [("vD", 0)]
            ropefree = [("sD", 0)]

            def epi(C2, bank, ftk, pan2, it):
                kind, _, h = jobs[it]
                C2.wait("scalar", "tD", C2.cnt["tD"])
                C2.wait("scalar", *ptfree[it - 2])
                sl = pt[it % 2]
                C2.op("scalar", lambda e, sl=sl: e.activation(sl[:, 0:PANEL], bank[:, 0:PANEL],
                                                              ACT.Identity), sem="sD")
                evk = C2.cnt["sD"]
                bank_free[it] = ("sD", evk)
                if kind in ("q", "k"):
                    C2.wait("tensor", "sD", evk)
                    C2.wait("tensor", *ropefree[0])
                    C2.op("tensor", lambda e, sl=sl: e.matmul(ps[4][:, 0:PANEL], rmat[:],
                                                              sl[:, 0:PANEL], start=True, stop=True), sem="tD")
                    rmm = C2.cnt["tD"]
                    C2.wait("scalar", "tD", rmm)
                    if kind == "q":
                        dst = qt()[:, h * T + pan2 * PANEL: h * T + pan2 * PANEL + PANEL]
                    else:
                        dst = pt3[:, 0:PANEL]
                        C2.wait("scalar", *vtfree[0])
                    C2.op("scalar", lambda e, dst=dst: e.activation(dst, ps[4][:, 0:PANEL], ACT.Identity),
                          sem="sD")
                    ropefree[0] = ("sD", C2.cnt["sD"])
                    revk = C2.cnt["sD"]
                    if kind == "k":
                        C2.wait("sync", "sD", revk)
                        C2.dma("sync", "st", g["agb1_k_i"][h * 128:(h + 1) * 128,
                                                           pan2 * PANEL:pan2 * PANEL + PANEL], pt3[:, 0:PANEL])
                        vtfree[0] = ("st", C2.cnt["st"])
                    C2.wait("vector", "sD", revk)
                    C2.wait("vector", *sqfree[0])
                    srcv = dst
                    C2.op("vector", lambda e, srcv=srcv: e.tensor_tensor(pt2[:, 0:PANEL], srcv, srcv,
                                                                         ALU.mult), sem="vD")
                    C2.wait("tensor", "vD", C2.cnt["vD"])
                    C2.wait("tensor", *nrmfree[0])
                    C2.op("tensor", lambda e: e.matmul(pl[0][0:1, 0:PANEL], onesb[:], pt2[:, 0:PANEL],
                                                       start=True, stop=True), sem="tD")
                    sqfree[0] = ("tD", C2.cnt["tD"])
                    C2.wait("scalar", "tD", C2.cnt["tD"])
                    C2.wait("scalar", *linvfree[0])
                    C2.op("scalar", lambda e: e.activation(
                        linv[0:1, 0:PANEL], pl[0][0:1, 0:PANEL], ACT.Identity), sem="sD")
                    nrmfree[0] = ("sD", C2.cnt["sD"])
                    C2.wait("vector", "sD", C2.cnt["sD"])
                    C2.op("vector", lambda e: e.tensor_reduce(
                        mtmp[0:1, 2:3], linv[0:1, 0:PANEL], mybir.AxisListType.X, ALU.max))
                    C2.drain("vector")
                    mrow_dst = mqrow if kind == "q" else mkrow
                    C2.op("vector", lambda e, h=h, mrow_dst=mrow_dst: e.tensor_tensor(
                        mrow_dst[0:1, h:h + 1], mrow_dst[0:1, h:h + 1], mtmp[0:1, 2:3], ALU.max),
                        sem="vD")
                    C2.drain("vector")
                    linvfree[0] = ("vD", C2.cnt["vD"])
                    ptfree[it] = ("tD", rmm)
                else:
                    lasttr = 0
                    for j in range(TPP):
                        C2.wait("tensor", "sD", evk)
                        C2.wait("tensor", *trfree[j % 2])
                        C2.op("tensor", lambda e, j=j, sl=sl: e.transpose(
                            psT(4 + j % 2), sl[:, j * 128:(j + 1) * 128], identb[:]), sem="tD")
                        lasttr = C2.cnt["tD"]
                        C2.wait("scalar", "tD", lasttr)
                        if j == 0:
                            C2.wait("scalar", *vtfree[0])
                        C2.op("scalar", lambda e, j=j: e.activation(
                            pt3[:, j * 128:(j + 1) * 128], psT(4 + j % 2), ACT.Identity), sem="sD")
                        trfree[j % 2] = ("sD", C2.cnt["sD"])
                    C2.wait("sync", "sD", C2.cnt["sD"])
                    C2.dma("sync", "st",
                           g["agb1_v_i"][pan2 * PANEL:(pan2 + 1) * PANEL, h * 128:(h + 1) * 128]
                           .rearrange("(j p) c -> p j c", p=128),
                           pt3[:, 0:PANEL].rearrange("p (j cc) -> p j cc", cc=128))
                    vtfree[0] = ("st", C2.cnt["st"])
                    ptfree[it] = ("tD", lasttr)
                    if pan2 == NPAN - 1 and it == 2 * NFT_KV - 1:
                        C2.wait("sync", "vD", C2.cnt["vD"])
                        C2.dma("sync", "st", g["agb1_nk_i"][0:1, 0:NKV], mkrow[0:1, 0:NKV])
                        C2.marks["st_agb1"] = C2.cnt["st"]

            gemm_panel(C, pan, [(v, h) for _, v, h in jobs], 0, epi, lambda it: bank_free[it])
            if pan < NPAN - 1:
                C.barrier()

    def ph_trigger_agb1(ctx):
        C = ctx
        C.wait("gpsimd", "st", C.marks["st_agb1"])
        C.collective("ccb1", g["agb1_k_i"], g["agb1_k_o"])
        C.collective("ccb1", g["agb1_v_i"], g["agb1_v_o"])
        C.collective("ccb1", g["agb1_nk_i"], g["agb1_nk_o"])

    def run_all(ctx):
        ph_prep(ctx)
        ctx.barrier()
        ph_wcast(ctx)
        ph_trigger1(ctx)
        ph_qzsz(ctx)
        ctx.barrier()
        ph_dequant(ctx)
        ctx.barrier()
        ph_gather(ctx)
        ctx.barrier()
        ph_trigger2(ctx)
        ph_lnqkv(ctx)
        ph_trigger_aga(ctx)
        ctx.barrier()
        mprep(ctx, "a")
        ctx.barrier()
        ph_attn(ctx, "a")
        ctx.barrier()
        ph_outproj(ctx, "a", x_in, out_ext if dbg_stop == 1 else res1)
        ctx.barrier()
        if dbg_stop == 1:
            return
        ph_ffn(ctx, res1, out_ext if dbg_stop == 2 else res2)
        if dbg_stop == 2:
            return
        ph_bproj(ctx, res2)
        ph_trigger_agb1(ctx)
        ctx.barrier()
        mprep(ctx, "b")
        ctx.barrier()
        ph_attn(ctx, "b")
        ctx.barrier()
        ph_outproj(ctx, "b", res2, out_ext if dbg_stop == 3 else res3)
        ctx.barrier()
        if dbg_stop == 3:
            return
        ph_ffn(ctx, res3, out_ext)

    with es:
        with nc.Block() as block:
            starters = dict(sync=block.sync, tensor=block.tensor, vector=block.vector,
                            scalar=block.scalar, gpsimd=block.gpsimd)
            for role in ROLES:
                def make(role):
                    def fn(eng):
                        ctx = Ctx(nc, cfg, sems, B, dbg_stop)
                        ctx.role = role
                        ctx.eng = eng
                        run_all(ctx)
                        ctx.barrier()
                    return fn
                starters[role](make(role))
    return nc


def kernel_generic(cfg, I, dbg_stop=None):
    in_maps = stage_inputs(cfg, I)
    nc = build_nc(cfg, dbg_stop=dbg_stop)
    res = run_bass_kernel_spmd(nc, in_maps, core_ids=list(range(NCORES)))
    c = derived(cfg)
    out = np.concatenate([np.asarray(r["out"]) for r in res.results], axis=0)
    return out.reshape(cfg["B"], cfg["S"], cfg["D"])



def kernel(L0_attn_a_x, L0_attn_a_attn_qkvw, L0_attn_a_attn_qkvb, L0_attn_a_attn_ow,
           L0_attn_a_attn_ob, L0_attn_b_freqs_cos, L0_attn_b_freqs_sin, L0_attn_b_wq,
           L0_attn_b_wk, L0_attn_b_wv, L0_attn_b_wo, L0_attn_b_k_cache, L0_attn_b_v_cache,
           L0_attn_b_block_idx, L0_ffn_gate_qweight, L0_ffn_gate_scales, L0_ffn_gate_qzeros,
           L0_ffn_gate_g_idx, L0_ffn_up_qweight, L0_ffn_up_scales, L0_ffn_up_qzeros,
           L0_ffn_up_g_idx, L0_ffn_down_qweight, L0_ffn_down_scales, L0_ffn_down_qzeros,
           L0_ffn_down_g_idx):
    I = dict(
        x=L0_attn_a_x, qkvw=L0_attn_a_attn_qkvw, qkvb=L0_attn_a_attn_qkvb,
        ow=L0_attn_a_attn_ow, ob=L0_attn_a_attn_ob,
        fcos=L0_attn_b_freqs_cos, fsin=L0_attn_b_freqs_sin,
        wq=L0_attn_b_wq, wk=L0_attn_b_wk, wv=L0_attn_b_wv, wo=L0_attn_b_wo,
        k_cache=L0_attn_b_k_cache, v_cache=L0_attn_b_v_cache, block_idx=L0_attn_b_block_idx,
        gq=L0_ffn_gate_qweight, gsc=L0_ffn_gate_scales, gqz=L0_ffn_gate_qzeros,
        g_gidx=L0_ffn_gate_g_idx,
        uq=L0_ffn_up_qweight, usc=L0_ffn_up_scales, uqz=L0_ffn_up_qzeros,
        u_gidx=L0_ffn_up_g_idx,
        dq=L0_ffn_down_qweight, dsc=L0_ffn_down_scales, dqz=L0_ffn_down_qzeros,
        d_gidx=L0_ffn_down_g_idx,
    )
    out = kernel_generic(full_cfg(), I)
    return out.astype(np.float32)
